# version 32
# baseline (speedup 1.0000x reference)
from contextlib import ExitStack

import numpy as np
import ml_dtypes

import concourse.bass as bass
import concourse.mybir as mybir
from concourse.bass_utils import run_bass_kernel_spmd

F32 = mybir.dt.float32
BF16 = mybir.dt.bfloat16
AF = mybir.ActivationFunctionType
ALU = mybir.AluOpType
AX = mybir.AxisListType.X

NCORES = 8
T = 2048
D = 1024
H = 4
DK = DV = 256
C = 128
NCH = T // C
G = 512
NG = T // G
CPG = G // C
FF = 3072
NFC = D // 128
NFF = FF // 128
NTT = T // 128
MAGIC = float(np.float32(3 * 2.0 ** 22))
LN16 = float(np.log(1.0 / 16.0))
EPS = 1e-5

ENGINES = ("sync", "vector", "scalar", "tensor", "gpsimd")


class Sem:
    def __init__(self, nc, stack, name):
        self.h = stack.enter_context(nc.semaphore(name))
        self.n = 0

    def inc(self, k=1):
        self.n += k
        return self.n

    @property
    def tok(self):
        return (self, self.n)


def build_program(dbg=()):
    dbg = set(dbg)
    nc = bass.Bass("TRN2", target_bir_lowering=False)
    stack = ExitStack()
    PL = {e: [] for e in ENGINES}

    def on(eng, fn, waits=(), inc=None, k=1, drain=False):
        waits = tuple(w for w in waits if w is not None)
        for (s, v) in waits:
            assert isinstance(s, Sem) and isinstance(v, int), f"bad wait {s} {v}"
        if inc is not None:
            inc.inc(k)

        def f(e, fn=fn, waits=waits, inc=inc, k=k, drain=drain):
            for (s, v) in waits:
                e.wait_ge(s.h, v)
            if drain:
                e.drain()
            last = fn(e)
            if inc is not None:
                if last is None:
                    e.nop().then_inc(inc.h, k)
                else:
                    last.then_inc(inc.h, k)
        PL[eng].append(f)
        return inc.n if inc is not None else None

    def dma(q, out_ap, in_ap, sem, waits=(), **kw):
        self_wait = sem.n
        v = sem.inc(16)
        waits = tuple(w for w in waits if w is not None) + ((sem, self_wait),)
        for (s, vv) in waits:
            assert isinstance(s, Sem) and isinstance(vv, int), f"bad dma wait {s} {vv}"

        def f(e, out_ap=out_ap, in_ap=in_ap, kw=kw, sem=sem, waits=waits):
            for (s, vv) in waits:
                e.wait_ge(s.h, vv)
            e.dma_start(out=out_ap, in_=in_ap, **kw).then_inc(sem.h, 16)
        PL[q].append(f)
        return (sem, v)

    def din(name, shape, dt=F32):
        return nc.dram_tensor(name, list(shape), dt, kind="ExternalInput")

    x_h = din("x", [T, D]).ap()
    Wq_h = din("Wq", [D, D]).ap()
    Wk_h = din("Wk", [D, D]).ap()
    Wv_h = din("Wv", [D, D]).ap()
    Wg_h = din("Wg", [D, D]).ap()
    Wgk1_h = din("Wgk1", [D, 16]).ap()
    Wgk2_h = din("Wgk2", [16, D]).ap()
    bgk2_h = din("bgk2", [D]).ap()
    ln1w_h = din("ln1w", [D]).ap()
    ln1b_h = din("ln1b", [D]).ap()
    rmsw_h = din("rmsw", [DV]).ap()
    Wo_h = din("Wo", [D, D]).ap()
    ln2w_h = din("ln2w", [D]).ap()
    ln2b_h = din("ln2b", [D]).ap()
    W1_h = din("W1", [D, FF]).ap()
    b1_h = din("b1", [FF]).ap()
    W2_h = din("W2", [FF, D]).ap()
    b2_h = din("b2", [D]).ap()
    wup_sl_h = din("wup_sl", [FF, 128]).ap()
    wgt_sl_h = din("wgt_sl", [FF, 128]).ap()
    wdn_sl_h = din("wdn_sl", [128, FF]).ap()
    cmask_h = din("cmask", [128, NCORES]).ap()

    out_h = nc.dram_tensor("out", [T, D], F32, kind="ExternalOutput").ap()

    def idram(name, shape, dt=F32):
        return nc.dram_tensor(name, list(shape), dt)

    xn_sp_t = idram("xn_sp", [NTT, 128, D])
    wq_bf_t = idram("wq_bf", [NFC, NFC, 128, 128], BF16)
    wk_bf_t = idram("wk_bf", [NFC, NFC, 128, 128], BF16)
    wv_bf_t = idram("wv_bf", [NFC, NFC, 128, 128], BF16)
    wg_bf_t = idram("wg_bf", [NFC, NFC, 128, 128], BF16)
    w1_bf_t = idram("w1_bf", [NFF, NFC, 128, 128], BF16)
    wo_bf_t = idram("wo_bf", [D, D], BF16)
    w2_bf_t = idram("w2_bf", [FF, D], BF16)
    qt_sp_t = idram("qt_sp", [NG, NFC, 128, G], BF16)
    xq_sp_t = idram("xq_sp", [NFC, 128, T], BF16)
    sp_up_t = idram("sp_up", [NFF, 128, T], BF16)
    sp_gt_t = idram("sp_gt", [NFF, 128, T], BF16)
    sp_pr_t = idram("sp_pr", [NFF, 128, T], BF16)
    sp_or_t = idram("sp_or", [NFF, 128, T], BF16)
    wag_i_t = idram("wag_i", [2, NFF, 128, 128], BF16)
    wag_o_t = idram("wag_o", [NCORES, 2, NFF, 128, 128], BF16)
    wagd_i_t = idram("wagd_i", [NFF, 128, 128], BF16)
    wagd_o_t = idram("wagd_o", [NCORES, NFF, 128, 128], BF16)
    wdn_rp_t = idram("wdn_rp", [NFF, 128, D], BF16)
    mx_i_t = idram("mx_i", [1, 128])
    mx_o_t = idram("mx_o", [1, 128])
    mx2_i_t = idram("mx2_i", [1, 256])
    mx2_o_t = idram("mx2_o", [1, 256])
    mx4_i_t = idram("mx4_i", [1, 128])
    mx4_o_t = idram("mx4_o", [1, 128])
    mx5_i_t = idram("mx5_i", [1, 128])
    mx5_o_t = idram("mx5_o", [1, 128])
    AGW = NFC * DV + NFC
    ag_i_t = idram("ag_i", [128, AGW])
    ag_o_t = idram("ag_o", [NCORES, 128, AGW])
    xn_sp = xn_sp_t.ap()
    xq_sp = xq_sp_t.ap()
    wq_bf = wq_bf_t.ap()
    wk_bf = wk_bf_t.ap()
    wv_bf = wv_bf_t.ap()
    wg_bf = wg_bf_t.ap()
    w1_bf = w1_bf_t.ap()
    wo_bf = wo_bf_t.ap()
    w2_bf = w2_bf_t.ap()
    qt_sp = qt_sp_t.ap()
    sp_up = sp_up_t.ap()
    sp_gt = sp_gt_t.ap()
    sp_pr = sp_pr_t.ap()
    sp_or = sp_or_t.ap()
    wag_i = wag_i_t.ap()
    wag_o = wag_o_t.ap()
    wagd_i = wagd_i_t.ap()
    wagd_o = wagd_o_t.ap()
    wdn_rp = wdn_rp_t.ap()
    ag_i = ag_i_t.ap()
    ag_o = ag_o_t.ap()

    DOUT = {}
    dbg_names = []

    def dbg_decl(name, shape, dt=F32):
        if name in dbg:
            DOUT[name] = nc.dram_tensor("dbg_" + name, list(shape), dt,
                                        kind="ExternalOutput").ap()
            dbg_names.append("dbg_" + name)

    dbg_decl("xn", [128, NTT * D])
    dbg_decl("z", [128, NFC * T], BF16)
    dbg_decl("cs", [NG, 128, NFC * G])
    dbg_decl("kt", [NG, 128, NFC * G], BF16)
    dbg_decl("vt", [NG, 128, NFC * G], BF16)
    dbg_decl("qt", [128, NFC * T], BF16)
    dbg_decl("oloc", [128, NTT * D])
    dbg_decl("sloc", [128, NFC * DV])
    dbg_decl("hin", [128, NFC * DV])
    dbg_decl("ofin", [128, NTT * D])
    dbg_decl("gated", [128, NFC * T], BF16)
    dbg_decl("xa1", [128, NTT * D])
    dbg_decl("y", [128, NFC * T], BF16)
    dbg_decl("xq", [128, NFC * T], BF16)
    dbg_decl("expbc", [128, NFC * NCH])
    dbg_decl("etab", [128, NFC * NCH])
    dbg_decl("scales", [1, 16])
    dbg_decl("h1", [2, 128, NFF * 1024], BF16)
    dbg_decl("xa2", [128, NTT * D])
    dbg_decl("oq", [NG, 128, NFF * G], BF16)

    SB0 = (nc.sbuf_base + 31) // 32 * 32
    cur = [SB0]

    ALOC = {}

    def alloc(name, shape, dt, at=None):
        nbytes = int(np.prod(shape[1:])) * mybir.dt.size(dt)
        if at is None:
            off = cur[0]
            cur[0] += (nbytes + 31) // 32 * 32
        else:
            off = at
        ALOC[name] = off
        assert off + nbytes <= nc.SBUF_PARTITION_SIZE_BYTES, \
            f"SBUF overflow {name}: {off}+{nbytes}>{nc.SBUF_PARTITION_SIZE_BYTES}"
        return nc.alloc_sbuf_tensor_at(name, list(shape), dt, offset=off)

    xnO = alloc("xnO", [128, NTT * D], F32)
    xqs0 = alloc("xqs0", [128, NFC * 512], BF16, at=SB0 + 12 * 4096)
    xqs1 = alloc("xqs1", [128, NFC * 512], BF16, at=SB0 + 14 * 4096)
    z = alloc("z", [128, NFC * T], BF16)
    R0 = cur[0]
    qtr = alloc("qtr", [128, 2 * NFC * G], BF16, at=R0)
    xldA0 = alloc("xldA0", [128, D], F32, at=R0)
    xldA1 = alloc("xldA1", [128, D], F32, at=R0 + 4096)
    ztA0 = alloc("ztA0", [128, D], BF16, at=R0 + 8192)
    ztA1 = alloc("ztA1", [128, D], BF16, at=R0 + 10240)
    f1s0 = alloc("f1s0", [128, 512], F32, at=R0 + 12288)
    f1s1 = alloc("f1s1", [128, 512], F32, at=R0 + 14336)
    Sjb_ = alloc("Sjb", [128, AGW], F32, at=R0)
    xldC0 = alloc("xldC0", [128, D], F32, at=R0)
    xldC1 = alloc("xldC1", [128, D], F32, at=R0 + 4096)
    cur[0] = R0 + 16384
    M0 = cur[0]
    xq = alloc("xq", [128, NFC * T], BF16, at=M0)
    cs_g = alloc("cs_g", [128, NFC * G], F32, at=M0)
    kt_g = alloc("kt_g", [128, NFC * G], BF16, at=M0 + 16384)
    vt_g = alloc("vt_g", [128, NFC * G], BF16, at=M0 + 24576)
    gated = alloc("gated", [128, NFC * T], BF16, at=M0)
    SjbB_ = alloc("SjbB", [128, AGW], F32, at=M0)
    cur[0] = M0 + 32768
    h1h = alloc("h1h", [128, NFF * 1024], BF16, at=R0)
    N0 = cur[0]
    wdq = alloc("wdq", [128, FF], F32, at=N0)
    wdqb = alloc("wdqb", [128, FF], BF16, at=N0 + 12288)
    blkm = alloc("blkm", [128, 192], F32, at=N0 + 18432)
    h_f = alloc("h_f", [128, NFC * DV], F32, at=N0)
    h_b = alloc("h_b", [128, NFC * DV], BF16, at=N0 + 8192)
    hin = alloc("hin", [128, NFC * DV], F32, at=N0 + 12288)
    hinb = alloc("hinb", [128, NFC * DV], BF16, at=N0 + 20480)
    gsc0 = alloc("gsc0", [128, 512], F32, at=N0)
    gsc1 = alloc("gsc1", [128, 512], F32, at=N0 + 2048)
    gfm_q = alloc("gfm_q", [128, NFC * 512], BF16, at=N0 + 12288)
    ztB0 = alloc("ztB0", [128, D], BF16, at=N0 + 20480)
    ztB1 = alloc("ztB1", [128, D], BF16, at=N0 + 22528)
    xldB0 = alloc("xldB0", [128, D], F32, at=N0)
    xldB1 = alloc("xldB1", [128, D], F32, at=N0 + 4096)
    fscW0 = alloc("fscW0", [128, 512], F32, at=N0)
    fscW1 = alloc("fscW1", [128, 512], F32, at=N0 + 2048)
    ztC0 = alloc("ztC0", [128, D], BF16, at=N0 + 4096)
    ztC1 = alloc("ztC1", [128, D], BF16, at=N0 + 6144)
    fsc0 = alloc("fsc0", [128, 512], F32, at=N0)
    fsc1 = alloc("fsc1", [128, 512], F32, at=N0 + 2048)
    fsc2 = alloc("fsc2", [128, 512], F32, at=N0 + 4096)
    fsc3 = alloc("fsc3", [128, 512], F32, at=N0 + 6144)
    fbf0 = alloc("fbf0", [128, 512], BF16, at=N0 + 8192)
    fbf1 = alloc("fbf1", [128, 512], BF16, at=N0 + 9216)
    fbf2 = alloc("fbf2", [128, 512], BF16, at=N0 + 10240)
    fbf3 = alloc("fbf3", [128, 512], BF16, at=N0 + 11264)
    fbf4 = alloc("fbf4", [128, 512], BF16, at=N0 + 12288)
    fbf5 = alloc("fbf5", [128, 512], BF16, at=N0 + 13312)
    oq_tg = alloc("oq_tg", [128, NFF * G], BF16, at=N0)
    oq_tgB = alloc("oq_tgB", [128, NFF * G], BF16, at=M0)
    cur[0] = N0 + 24576
    ident = alloc("ident", [128, 128], BF16)
    utmask = alloc("utmask", [128, 128], F32)
    iotaf = alloc("iotaf", [128, 128], F32)
    wstage = alloc("wstage", [128, 1024], F32)
    wbf0 = alloc("wbf0", [128, 1024], BF16)
    wbf1 = alloc("wbf1", [128, 1024], BF16)
    wbf2 = alloc("wbf2", [128, 1024], BF16)
    wbf3 = alloc("wbf3", [128, 1024], BF16)
    ktT0 = alloc("ktT0", [128, D], BF16)
    vtT0 = alloc("vtT0", [128, D], BF16)
    atm0 = alloc("atm0", [128, C], BF16)
    atm1 = alloc("atm1", [128, C], BF16)
    SCA = alloc("SCA", [128, 512], F32)
    SCB = alloc("SCB", [128, 512], F32)
    rmsrep = alloc("rmsrep", [128, D], F32, at=None)
    u1b = alloc("u1b", [16, 1024], BF16)
    wgk1b = alloc("wgk1b", [128, NFC * 16], BF16)
    wgk2b = alloc("wgk2b", [16, D], BF16)
    expbc = alloc("expbc", [128, NFC * NCH], F32)
    etab = alloc("etab", [128, NFC * NCH], F32)
    stat6 = alloc("stat6", [128, 24], F32)
    stat2 = alloc("stat2", [128, 8], F32)
    mcol = alloc("mcol", [128, 8], F32)
    ccol = alloc("ccol", [128, 16], F32)
    dcol = alloc("dcol", [128, 16], F32)
    ecol = alloc("ecol", [128, 8], F32)
    cmk = alloc("cmk", [128, NCORES], F32)
    cmki = alloc("cmki", [128, NCORES], F32)
    ln1w_c = alloc("ln1w_c", [128, NFC], F32)
    ln1b_c = alloc("ln1b_c", [128, NFC], F32)
    ln2w_c = alloc("ln2w_c", [128, NFC], F32)
    ln2b_c = alloc("ln2b_c", [128, NFC], F32)
    bgk2_c = alloc("bgk2_c", [128, NFC], F32)
    b1_c = alloc("b1_c", [128, NFF], F32)
    b1w_c = alloc("b1w_c", [128, NFF], F32)
    bq_c = alloc("bq_c", [128, NFC], F32)
    bk_c = alloc("bk_c", [128, NFC], F32)
    bv_c = alloc("bv_c", [128, NFC], F32)
    bg_c = alloc("bg_c", [128, NFC], F32)
    bgk1_c = alloc("bgk1_c", [16, 1], F32)
    rowbank = alloc("rowbank", [1, D], F32)
    onesr = alloc("onesr", [1, 128], F32)
    srow = alloc("srow", [1, 16], F32)
    dmax = alloc("dmax", [1, 256], F32)
    rcols = alloc("rcols", [128, 8], F32)
    rst16 = alloc("rst16", [128, 64], F32)
    ln1b_cb = alloc("ln1b_cb", [128, NFC], BF16)
    ln2b_cb = alloc("ln2b_cb", [128, NFC], BF16)
    f3mx = alloc("f3mx", [128, 192], F32)
    dnr = [alloc("dnr0", [128, 1024], BF16, at=ALOC["rmsrep"]),
           alloc("dnr1", [128, 1024], BF16, at=ALOC["rmsrep"] + 2048),
           alloc("dnr2", [128, 1024], BF16, at=ALOC["ktT0"]),
           alloc("dnr3", [128, 1024], BF16, at=ALOC["vtT0"]),
           alloc("dnr4", [128, 1024], BF16, at=ALOC["SCA"]),
           alloc("dnr5", [128, 1024], BF16, at=ALOC["SCB"])]

    assert cur[0] <= nc.SBUF_PARTITION_SIZE_BYTES, f"SBUF total {cur[0] - SB0}"

    b2_row = rowbank[0:1, 0:D]

    PB = [stack.enter_context(nc.psum_tensor(f"pb{i}", [128, 512], F32))
          for i in range(7)]
    PBF = stack.enter_context(nc.psum_tensor("pbf", [128, 1024], BF16))

    sems = {}

    def S_(name):
        if name not in sems:
            sems[name] = Sem(nc, stack, name)
        return sems[name]

    dO = S_("dO")
    cc = S_("cc")
    sV = S_("sV")
    sS = S_("sS")
    sT = S_("sT")
    sG = S_("sG")
    dSm = S_("dSm")
    dG = S_("dG")
    dDq = S_("dDq")
    dWg0 = S_("dWg0"); dWg1 = S_("dWg1")
    dRp = S_("dRp")
    dXl0 = S_("dXl0"); dXl1 = S_("dXl1")
    dMxg = S_("dMxg")
    dMxr = S_("dMxr")
    dSp0 = S_("dSp0"); dSp1 = S_("dSp1")
    dXq0 = S_("dXq0"); dXq1 = S_("dXq1")
    dXqs0 = S_("dXqs0"); dXqs1 = S_("dXqs1")
    dF1l0 = S_("dF1l0"); dF1l1 = S_("dF1l1")
    dF1s0 = S_("dF1s0"); dF1s1 = S_("dF1s1")
    dWc0 = S_("dWc0"); dWc1 = S_("dWc1")
    dQs0 = S_("dQs0"); dQs1 = S_("dQs1")
    dQr0 = S_("dQr0"); dQr1 = S_("dQr1")
    dAg = S_("dAg")
    dSj = S_("dSj")
    dSj2 = S_("dSj2")
    dWo0 = S_("dWo0"); dWo1 = S_("dWo1")
    dXn0 = S_("dXn0"); dXn1 = S_("dXn1")
    dF3l0 = S_("dF3l0"); dF3l1 = S_("dF3l1")
    dF3l2 = S_("dF3l2"); dF3l3 = S_("dF3l3")
    dF3s0 = S_("dF3s0"); dF3s1 = S_("dF3s1")
    dOq0 = S_("dOq0"); dOq1 = S_("dOq1")
    dDn0 = S_("dDn0"); dDn1 = S_("dDn1")
    dXc0 = S_("dXc0"); dXc1 = S_("dXc1")

    DT = {}

    def dump(name, src_ap, waits=()):
        if name in dbg:
            DT[name] = dma("sync", DOUT[name], src_ap, dO, waits=waits)

    RG = {
        "wstg": [None, None], "wbf": [None, None], "xld": [None, None],
        "gks": [None, None], "pbf": None, "zt": [None, None],
    }

    on("gpsimd", lambda e: e.iota(iotaf[:], pattern=[[1, 128]], base=0,
                                  channel_multiplier=-1,
                                  allow_small_or_imprecise_dtypes=True), inc=sG)
    t_iota = sG.tok
    on("vector", lambda e: e.tensor_scalar(ident[:], iotaf[:], 0.0, None,
                                           op0=ALU.is_equal),
       waits=[t_iota], inc=sV)
    on("vector", lambda e: e.tensor_scalar(utmask[:], iotaf[:], 0.0, None,
                                           op0=ALU.is_ge), inc=sV)
    on("vector", lambda e: e.memset(onesr[:], 1.0), inc=sV)
    on("vector", lambda e: e.memset(ecol[:, 0:1], EPS), inc=sV)
    on("vector", lambda e: e.memset(ecol[:, 1:2], LN16), inc=sV)
    on("vector", lambda e: e.memset(ecol[:, 2:3], 0.0), inc=sV)
    on("vector", lambda e: e.memset(mcol[:], 0.0), inc=sV)
    on("vector", lambda e: e.memset(srow[:], 0.0), inc=sV)
    on("vector", lambda e: e.memset(ecol[:, 3:4], 1.0), inc=sV)
    on("vector", lambda e: e.memset(ecol[:, 4:5], -MAGIC), inc=sV)
    t_consts = sV.tok

    for (dst, src) in [(ln1w_c, ln1w_h), (ln1b_c, ln1b_h), (ln2w_c, ln2w_h),
                       (ln2b_c, ln2b_h), (bgk2_c, bgk2_h)]:
        dma("gpsimd", dst[:], src.rearrange("(a b) -> b a", b=128), dG,
            allow_slow_non_contiguous=True)
    dma("gpsimd", b1_c[:], b1_h.rearrange("(a b) -> b a", b=128), dG,
        allow_slow_non_contiguous=True)
    t_gcols = dG.tok
    WPC = {}
    for nm, (w_h, w_bf, nmc) in {
            "q": (Wq_h, wq_bf, NFC), "k": (Wk_h, wk_bf, NFC),
            "v": (Wv_h, wv_bf, NFC), "g": (Wg_h, wg_bf, NFC),
            "w1": (W1_h, w1_bf, NFF)}.items():
        srcr = w_h.rearrange("(kc p) (mc c) -> mc kc p c", p=128, c=128)
        sem_a = S_("dPC" + nm)
        sem_b = S_("dPD" + nm)
        for mc_ in range(nmc):
            WPC[nm] = dma("gpsimd", w_bf[mc_], srcr[mc_],
                          sem_a if mc_ % 2 == 0 else sem_b)
        WPC[nm + "2"] = (sem_a if (nmc - 2) % 2 == 0 else sem_b,
                         WPC[nm][1] if False else None)
        WPC[nm] = ((sem_a, sem_a.n), (sem_b, sem_b.n))
    WPC["wo"] = dma("gpsimd", wo_bf, Wo_h, S_("dPCwo"))
    WPC["w2"] = dma("gpsimd", w2_bf, W2_h, S_("dPCw2"))

    dma("sync", cmk[:], cmask_h, dSm)
    dma("sync", wstage[0:16, 0:D], Wgk2_h, dSm)
    t_small = dSm.tok

    on("vector", lambda e: e.tensor_scalar(cmki[:], cmk[:], -1.0, 1.0,
                                           op0=ALU.mult, op1=ALU.add),
       waits=[t_small], inc=sV)
    on("vector", lambda e: e.tensor_copy(ln1b_cb[:], ln1b_c[:]),
       waits=[t_gcols], inc=sV)
    on("vector", lambda e: e.tensor_copy(ln2b_cb[:], ln2b_c[:]),
       waits=[t_gcols], inc=sV)
    on("vector", lambda e: e.tensor_scalar(bgk2_c[:], bgk2_c[:], -1.0, None,
                                           op0=ALU.mult),
       waits=[t_gcols], inc=sV, drain=True)
    t_gcolsb = sV.tok
    on("vector", lambda e: e.tensor_copy(wgk2b[:], wstage[0:16, 0:D]),
       waits=[t_small], inc=sV)

    vgk1 = dma("sync", SCA[:, 0:128].rearrange("p (a l) -> p a l", a=NFC),
               Wgk1_h.rearrange("(a p) l -> p a l", p=128), dSm)

    def _wgk1(e):
        return e.tensor_copy(wgk1b[:], SCA[:, 0:128])
    on("vector", _wgk1, waits=[vgk1, t_gcols], inc=sV)
    t_wgk = sV.tok
    t_masks = sV.tok


    def plan_dequant(src_ap, t_prev):
        dst = wdq[:, 0:FF]
        if len(src_ap.shape) == 3:
            dst = dst.rearrange("p (n h) -> p n h", n=NFF)
        t_ld = dma("sync", dst, src_ap, dDq, waits=list(t_prev))
        on("vector", lambda e: e.tensor_reduce(
            blkm[:, 0:96], wdq[:, 0:FF].rearrange("p (b k) -> p b k", k=32),
            axis=AX, op=ALU.max, apply_absolute_value=True),
           waits=[t_ld], inc=sV)
        on("vector", lambda e: e.tensor_scalar(
            blkm[:, 0:96], blkm[:, 0:96], 1.0 / 7.0, 1e-12,
            op0=ALU.mult, op1=ALU.max), inc=sV, drain=True)
        on("vector", lambda e: e.reciprocal(blkm[:, 96:192], blkm[:, 0:96]),
           inc=sV, drain=True)

        def _q1(e):
            last = None
            for b in range(96):
                sl = wdq[:, b * 32:(b + 1) * 32]
                last = e.tensor_scalar(sl, sl, blkm[:, 96 + b:97 + b], MAGIC,
                                       op0=ALU.mult, op1=ALU.add)
            return last
        on("vector", _q1, inc=sV, drain=True)

        def _q2(e):
            last = None
            for b in range(96):
                last = e.tensor_scalar(wdqb[:, b * 32:(b + 1) * 32],
                                       wdq[:, b * 32:(b + 1) * 32],
                                       MAGIC, blkm[:, b:b + 1],
                                       op0=ALU.subtract, op1=ALU.mult)
            return last
        on("vector", _q2, inc=sV, drain=True)
        return sV.tok

    def plan_tr_to_wag(wag_idx, t_ready):
        dmav = [None, None]
        evacv = [None, None]
        for i in range(NFF):
            hf = i % 2
            buf = wbf0 if hf == 0 else wbf1
            dsem = dWg0 if hf == 0 else dWg1
            w = [t_ready, t_masks]
            if i < 2 and RG["pbf"]:
                w.append((sV, RG["pbf"]))
            for s_ in (0, 1):
                if evacv[s_] is not None:
                    w.append((sV, evacv[s_]))
            on("tensor", lambda e, i=i, hf=hf: e.transpose(
                PBF[:, hf * 512:hf * 512 + 128],
                wdqb[:, i * 128:(i + 1) * 128], ident[:]),
               waits=w, inc=sT)
            tt_ = sT.tok
            w2 = [tt_]
            if dmav[hf] is not None:
                w2.append(dmav[hf])
            on("vector", lambda e, hf=hf, buf=buf: e.tensor_copy(
                buf[:, 0:128], PBF[:, hf * 512:hf * 512 + 128]),
               waits=w2, inc=sV)
            evacv[hf] = sV.n
            dst_ = wagd_i[i] if wag_idx == 2 else wag_i[wag_idx, i]
            dmav[hf] = dma("sync", dst_, buf[:, 0:128], dsem,
                           waits=[sV.tok])
        RG["pbf"] = sV.n
        return (dWg0.tok, dWg1.tok)

    up_src = wup_sl_h.rearrange("(n p) h -> p n h", p=128)
    gt_src = wgt_sl_h.rearrange("(n p) h -> p n h", p=128)
    t_dq = plan_dequant(up_src, [t_consts])
    t_w = plan_tr_to_wag(0, t_dq)
    t_dq = plan_dequant(gt_src, list(t_w))
    t_w_gt = plan_tr_to_wag(1, t_dq)

    on("gpsimd", lambda e: e.collective_compute(
        "AllGather", ALU.bypass, replica_groups=[list(range(NCORES))],
        ins=[wag_i_t.ap().opt()], outs=[wag_o_t.ap().opt()]),
       waits=[t_w_gt[0], t_w_gt[1]], inc=cc)
    t_wag = cc.tok

    def plan_down_chain():
        t_dq = plan_dequant(wdn_sl_h, list(t_w_gt))
        t_w = plan_tr_to_wag(2, t_dq)
        on("gpsimd", lambda e: e.collective_compute(
            "AllGather", ALU.bypass, replica_groups=[list(range(NCORES))],
            ins=[wagd_i_t.ap().opt()], outs=[wagd_o_t.ap().opt()]),
           waits=[t_w[0], t_w[1]], inc=cc)
        return cc.tok

    def plan_repack(t_wagd):
        for ic in range(NFF):
            dma("gpsimd", wdn_rp[ic].rearrange("p (r c) -> p r c", r=NCORES),
                wagd_o[:, ic, :, :].rearrange("r p c -> p r c"),
                dRp, waits=[t_wagd])
        return dRp.tok

    for tt in range(NTT):
        buf = xldA0 if tt % 2 == 0 else xldA1
        dx = dXl0 if tt % 2 == 0 else dXl1
        w = [(sS, RG["xld"][tt % 2])] if RG["xld"][tt % 2] else []
        t_x = dma("sync" if tt % 2 == 0 else "scalar",
                  buf[:], x_h[tt * 128:(tt + 1) * 128, :], dx, waits=w)
        xsl = xnO[:, tt * D:(tt + 1) * D]

        on("vector", lambda e, buf=buf: e.bn_stats(stat6[:, 0:6], buf[:, 0:512]),
           waits=[t_x, (sS, sS.n)], inc=sV, drain=True)
        on("vector", lambda e, buf=buf: e.bn_stats(stat6[:, 6:12],
                                                   buf[:, 512:1024]), inc=sV)
        on("vector", lambda e: e.bn_aggr(stat2[:, 0:2], stat6[:, 0:12]),
           inc=sV, drain=True)
        on("scalar", lambda e: e.activation(dcol[:, 0:1], stat2[:, 1:2], AF.Ln,
                                            bias=ecol[:, 0:1], scale=1.0),
           waits=[sV.tok], inc=sS)
        on("scalar", lambda e: e.activation(dcol[:, 1:2], dcol[:, 0:1], AF.Exp,
                                            bias=ecol[:, 2:3], scale=-0.5),
           inc=sS, drain=True)
        on("vector", lambda e: e.scalar_tensor_tensor(
            dcol[:, 2:3], stat2[:, 0:1], -1.0, dcol[:, 1:2],
            op0=ALU.mult, op1=ALU.mult), waits=[sS.tok], inc=sV, drain=True)
        on("scalar", lambda e, buf=buf, xsl=xsl: e.activation(
            xsl, buf[:], AF.Identity, bias=dcol[:, 2:3], scale=dcol[:, 1:2]),
           waits=[sV.tok], inc=sS)
        RG["xld"][tt % 2] = sS.n
        t_xn = sS.tok
        on("vector", lambda e, xsl=xsl: e.tensor_reduce(
            dcol[:, 3:4], xsl, axis=AX, op=ALU.max, apply_absolute_value=True),
           waits=[t_xn], inc=sV, drain=True)
        on("vector", lambda e: e.tensor_tensor(mcol[:, 0:1], mcol[:, 0:1],
                                               dcol[:, 3:4], op=ALU.max),
           inc=sV, drain=True)
        on("vector", lambda e, xsl=xsl: e.bn_stats(stat6[:, 12:18],
                                                   xsl[:, 0:512]),
           waits=[t_xn], inc=sV, drain=True)
        on("vector", lambda e, xsl=xsl: e.bn_stats(stat6[:, 18:24],
                                                   xsl[:, 512:1024]), inc=sV)
        on("vector", lambda e: e.bn_aggr(stat2[:, 2:4], stat6[:, 12:24]),
           inc=sV, drain=True)
        on("scalar", lambda e: e.activation(dcol[:, 4:5], stat2[:, 3:4], AF.Ln,
                                            bias=ecol[:, 0:1], scale=1.0),
           waits=[sV.tok], inc=sS)
        on("scalar", lambda e: e.activation(dcol[:, 5:6], dcol[:, 4:5], AF.Exp,
                                            bias=ecol[:, 2:3], scale=-0.5),
           inc=sS, drain=True)
        on("vector", lambda e: e.scalar_tensor_tensor(
            dcol[:, 6:7], stat2[:, 2:3], -1.0, dcol[:, 5:6],
            op0=ALU.mult, op1=ALU.mult), waits=[sS.tok], inc=sV, drain=True)
        zb = ztA0 if tt % 2 == 0 else ztA1
        w = [sV.tok]
        if RG["zt"][tt % 2]:
            w.append((sV, RG["zt"][tt % 2]))
        on("scalar", lambda e, zb=zb, xsl=xsl: e.activation(
            zb[:], xsl, AF.Identity, bias=dcol[:, 6:7], scale=dcol[:, 5:6]),
           waits=w, inc=sS)
        t_z = sS.tok
        w = [t_z, t_masks]
        if RG["pbf"]:
            w.append((sV, RG["pbf"]))

        def _ztr(e, zb=zb):
            last = None
            for fc in range(NFC):
                last = e.transpose(PBF[:, fc * 128:(fc + 1) * 128],
                                   zb[:, fc * 128:(fc + 1) * 128], ident[:])
            return last
        on("tensor", _ztr, waits=w, inc=sT)
        zdst = z[:].rearrange("p (f t) -> p f t", f=NFC)[:, :, tt * 128:(tt + 1) * 128]
        psrc = PBF[:].rearrange("p (f c) -> p f c", f=NFC)
        on("vector", lambda e, zdst=zdst, psrc=psrc: e.tensor_copy(zdst, psrc),
           waits=[sT.tok], inc=sV)
        RG["pbf"] = sV.n
        RG["zt"][tt % 2] = sV.n
    def _zscale(e):
        last = None
        for fc in range(NFC):
            last = e.tensor_scalar(z[:, fc * T:(fc + 1) * T],
                                   z[:, fc * T:(fc + 1) * T],
                                   ln1w_c[:, fc:fc + 1], None, op0=ALU.mult)
        return last
    on("vector", _zscale, waits=[t_gcols], inc=sV, drain=True)
    t_p1 = sV.tok
    dump("xn", xnO[:], [(sS, sS.n)])
    dump("z", z[:], [t_p1])

    tmx = dma("gpsimd", mx_i_t.ap().rearrange("a b -> b a"), mcol[:, 0:1],
              dMxg, waits=[t_p1])
    on("gpsimd", lambda e: e.collective_compute(
        "AllReduce", ALU.max, replica_groups=[list(range(NCORES))],
        ins=[mx_i_t.ap().opt()], outs=[mx_o_t.ap().opt()]),
       waits=[tmx], inc=cc)
    vv = dma("sync", dmax[0:1, 0:128], mx_o_t.ap(), dMxr, waits=[cc.tok])
    on("vector", lambda e: e.tensor_reduce(srow[0:1, 0:1], dmax[0:1, 0:128],
                                           axis=AX, op=ALU.max),
       waits=[vv], inc=sV, drain=True)
    on("vector", lambda e: e.tensor_scalar(srow[0:1, 1:2], srow[0:1, 0:1],
                                           1.0 / 32767.0, 1e-12,
                                           op0=ALU.mult, op1=ALU.max),
       inc=sV, drain=True)
    on("vector", lambda e: e.reciprocal(srow[0:1, 2:3], srow[0:1, 1:2]),
       inc=sV, drain=True)
    on("vector", lambda e: e.tensor_scalar(srow[0:1, 12:13], srow[0:1, 1:2],
                                           -MAGIC, None, op0=ALU.mult),
       inc=sV, drain=True)
    t_s1 = sV.tok

    PB6G = {"v": None}
    PB5G = {"v": None}

    def bcast_scales(t_ready):
        w6 = [t_ready]
        if PB6G["v"] is not None:
            w6.append((sV, PB6G["v"]))
        on("tensor", lambda e: e.matmul(PB[6][:, 0:16], onesr[:],
                                        srow[0:1, 0:16], start=True, stop=True),
           waits=w6, inc=sT)
        on("vector", lambda e: e.tensor_copy(ccol[:], PB[6][:, 0:16]),
           waits=[sT.tok], inc=sV, drain=True)
        PB6G["v"] = sV.n
        return sV.tok

    t_cc1 = bcast_scales(t_s1)
    dump("scales", srow[0:1, 0:16], [t_s1])

    RG["zt"] = [None, None]
    for tt in range(NTT):
        xsl = xnO[:, tt * D:(tt + 1) * D]
        buf = xldA0 if tt % 2 == 0 else xldA1
        zb = ztA0 if tt % 2 == 0 else ztA1
        w = [t_cc1]
        if RG["xld"][tt % 2]:
            w.append((sS, RG["xld"][tt % 2]))
        on("scalar", lambda e, buf=buf, xsl=xsl: e.activation(
            buf[:], xsl, AF.Copy, bias=MAGIC, scale=ccol[:, 2:3]),
           waits=w, inc=sS)
        w = [sS.tok]
        if RG["zt"][tt % 2]:
            w.append((sT, RG["zt"][tt % 2]))
        on("scalar", lambda e, buf=buf, zb=zb: e.activation(
            zb[:], buf[:], AF.Identity, bias=ccol[:, 12:13],
            scale=ccol[:, 1:2]), waits=w, inc=sS, drain=True)
        RG["xld"][tt % 2] = sS.n
        t_xq = sS.tok
        w = [t_xq]
        if RG["pbf"]:
            w.append((sV, RG["pbf"]))

        def _xtr(e, zb=zb):
            last = None
            for fc in range(NFC):
                last = e.transpose(PBF[:, fc * 128:(fc + 1) * 128],
                                   zb[:, fc * 128:(fc + 1) * 128], ident[:])
            return last
        on("tensor", _xtr, waits=w, inc=sT)
        RG["zt"][tt % 2] = sT.n
        xdst = xq[:].rearrange("p (f t) -> p f t", f=NFC)[:, :, tt * 128:(tt + 1) * 128]
        psrc = PBF[:].rearrange("p (f c) -> p f c", f=NFC)
        on("vector", lambda e, xdst=xdst, psrc=psrc: e.tensor_copy(xdst, psrc),
           waits=[sT.tok], inc=sV)
        RG["pbf"] = sV.n
        dma("sync", xn_sp[tt], xsl, dSp0 if tt % 2 == 0 else dSp1,
            waits=[sV.tok])
        dma("sync",
            xq_sp[:, :, tt * 128:(tt + 1) * 128].rearrange("f p t -> p f t"),
            xq[:].rearrange("p (f t) -> p f t",
                            f=NFC)[:, :, tt * 128:(tt + 1) * 128],
            dXq0 if tt % 2 == 0 else dXq1, waits=[sV.tok])
    t_xq_all = sV.tok
    t_xnspill = (dSp0.tok, dSp1.tok)
    dump("xq", xq[:], [t_xq_all])

    RG["xld"] = [None, None]

    t_wagd = plan_down_chain()


    t_p2s = (sS, sS.n)
    t_p2t = (sT, sT.n)
    t_xqsp0 = (dXq0, dXq0.n)
    t_xqsp1 = (dXq1, dXq1.n)
    xqs = [xqs0, xqs1]
    f1s = [wstage[:, 0:512], wstage[:, 512:1024]]
    f1_stg = [wbf2, wbf3]
    F1 = {"w0": [], "wld": [None, None], "stg": [None, None],
          "fs_sp": [None, None], "fs_red": [None, None], "pb3": None,
          "xqs": [None, None], "pend": [], "done_mm": None,
          "pb3_tok": None}

    def f1_arm():
        F1["w0"] = [(sV, sV.n), (sS, sS.n), (sT, sT.n),
                    (dSp0, dSp0.n), (dSp1, dSp1.n), t_xqsp0, t_xqsp1]

    def _f1_flush_pend(keep):
        while len(F1["pend"]) > keep:
            fh, col, t_ev = F1["pend"].pop(0)
            on("vector", lambda e, fh=fh, col=col: e.tensor_reduce(
                f3mx[:, col:col + 1], f1s[fh][:], axis=AX, op=ALU.max,
                apply_absolute_value=True), waits=[t_ev], inc=sV, drain=True)
            F1["fs_red"][fh] = sV.n

    def plan_f1_wld(widx, ic):
        hf = (widx * NFF + ic) % 2
        w = [t_wag] + F1["w0"]
        if F1["stg"][hf] is not None:
            w.append((sT, F1["stg"][hf]))
        src_ = wag_o[:, widx, ic, :, :].rearrange("r p c -> p r c")
        F1["wld"][hf] = dma(
            "sync",
            f1_stg[hf][:, 0:1024].rearrange("p (r c) -> p r c", r=NCORES),
            src_, dF1l0 if hf == 0 else dF1l1, waits=w)

    def plan_f1_unit(widx, ic, nt):
        u = (widx * NFF + ic) * 4 + nt
        fh = u % 2
        xb = u % 2
        stg_hf = (widx * NFF + ic) % 2
        wq = list(F1["w0"])
        if F1["xqs"][xb] is not None:
            wq.append((sT, F1["xqs"][xb]))
        t_xsl = dma("sync", xqs[xb][:].rearrange("p (f c) -> p f c", f=NFC),
                    xq_sp[:, :, nt * 512:(nt + 1) * 512]
                    .rearrange("f p t -> p f t"),
                    dXqs0 if xb == 0 else dXqs1, waits=wq)

        def _mm(e, stg_hf=stg_hf, xb=xb):
            last = None
            for kc in range(NFC):
                last = e.matmul(PB[3][:, 0:512],
                                f1_stg[stg_hf][:, kc * 128:(kc + 1) * 128],
                                xqs[xb][:, kc * 512:(kc + 1) * 512],
                                start=(kc == 0), stop=(kc == NFC - 1))
            return last
        w2 = [F1["wld"][stg_hf], t_xsl]
        if F1["pb3"] is not None:
            w2.append((sS, F1["pb3"]))
        on("tensor", _mm, waits=w2, inc=sT)
        F1["xqs"][xb] = sT.n
        if nt == 3:
            F1["stg"][stg_hf] = sT.n
        t_mmu = sT.tok
        _f1_flush_pend(1)
        w3 = [t_mmu]
        if F1["fs_sp"][fh] is not None:
            w3.append(F1["fs_sp"][fh])
        if F1["fs_red"][fh] is not None:
            w3.append((sV, F1["fs_red"][fh]))
        on("scalar", lambda e, fh=fh: e.copy(f1s[fh][:], PB[3][:, 0:512]),
           waits=w3, inc=sS)
        F1["pb3"] = sS.n
        t_ev = sS.tok
        spill = sp_up if widx == 0 else sp_gt
        F1["fs_sp"][fh] = dma("gpsimd",
                              spill[ic, :, nt * 512:(nt + 1) * 512],
                              f1s[fh][:], dF1s0 if fh == 0 else dF1s1,
                              waits=[t_ev])
        F1["pend"].append((fh, widx * 96 + ic * 4 + nt, t_ev))

    f1_items = []
    for widx in range(2):
        for ic in range(NFF):
            f1_items.append(lambda widx=widx, ic=ic: plan_f1_wld(widx, ic))
            for nt in range(4):
                f1_items.append(
                    lambda widx=widx, ic=ic, nt=nt: plan_f1_unit(widx, ic, nt))
    f1_pos = [0]

    def f1_pump(n=1):
        while n > 0 and f1_pos[0] < len(f1_items):
            f1_items[f1_pos[0]]()
            f1_pos[0] += 1
            n -= 1

    def f1_finish():
        f1_pump(len(f1_items))
        _f1_flush_pend(0)
        F1["done_mm"] = (sT, sT.n)
        F1["pb3_tok"] = (sS, F1["pb3"])
        on("vector", lambda e: e.tensor_reduce(
            mcol[:, 1:2], f3mx[:, 0:96], axis=AX, op=ALU.max),
           inc=sV, drain=True)
        on("vector", lambda e: e.tensor_reduce(
            mcol[:, 2:3], f3mx[:, 96:192], axis=AX, op=ALU.max),
           inc=sV, drain=True)
        tmx1 = dma("gpsimd", mx2_i_t.ap()[0:1, 0:128].rearrange("a b -> b a"),
                   mcol[:, 1:2], dMxg, waits=[sV.tok])
        tmx2 = dma("gpsimd", mx2_i_t.ap()[0:1, 128:256].rearrange("a b -> b a"),
                   mcol[:, 2:3], dMxg)
        on("gpsimd", lambda e: e.collective_compute(
            "AllReduce", ALU.max, replica_groups=[list(range(NCORES))],
            ins=[mx2_i_t.ap().opt()], outs=[mx2_o_t.ap().opt()]),
           waits=[tmx2], inc=cc)
        vv = dma("sync", dmax[0:1, 0:256], mx2_o_t.ap(), dMxr, waits=[cc.tok])
        on("vector", lambda e: e.tensor_reduce(srow[0:1, 3:4],
                                               dmax[0:1, 0:128],
                                               axis=AX, op=ALU.max),
           waits=[vv], inc=sV, drain=True)
        on("vector", lambda e: e.tensor_reduce(srow[0:1, 5:6],
                                               dmax[0:1, 128:256],
                                               axis=AX, op=ALU.max),
           inc=sV, drain=True)
        on("vector", lambda e: e.tensor_scalar(srow[0:1, 3:4], srow[0:1, 3:4],
                                               1.0 / 32767.0, 1e-12,
                                               op0=ALU.mult, op1=ALU.max),
           inc=sV, drain=True)
        on("vector", lambda e: e.reciprocal(srow[0:1, 4:5], srow[0:1, 3:4]),
           inc=sV, drain=True)
        on("vector", lambda e: e.tensor_scalar(srow[0:1, 5:6], srow[0:1, 5:6],
                                               1.0 / 32767.0, 1e-12,
                                               op0=ALU.mult, op1=ALU.max),
           inc=sV, drain=True)
        on("vector", lambda e: e.reciprocal(srow[0:1, 6:7], srow[0:1, 5:6]),
           inc=sV, drain=True)
        on("vector", lambda e: e.tensor_scalar(srow[0:1, 13:14],
                                               srow[0:1, 5:6],
                                               -MAGIC, None, op0=ALU.mult),
           inc=sV, drain=True)
        return bcast_scales(sV.tok)

    t_s23 = None

    F3 = {"t_s4": None, "t_s5": None,
          "gb": [None, None], "f0": [None, None], "ob": [None, None],
          "bb": [None, None], "done": None, "w0": []}
    f3_gb = [wbf2[:, 0:512], wbf2[:, 512:1024]]
    f3_ob = [wbf3[:, 0:512], wbf3[:, 512:1024]]
    f3_pb = f3_gb
    f3_ub = [ktT0[:, 0:512], ktT0[:, 512:1024]]
    f3_f0 = [SCA[:, 0:512], SCB[:, 0:512]]
    f3_f2 = [wstage[:, 0:512], wstage[:, 512:1024]]
    F3N = 4 * NFF

    def f3_arm():
        F3["w0"] = [(sV, sV.n), (sT, sT.n),
                    (dF1s0, dF1s0.n), (dF1s1, dF1s1.n)]

    def plan_f3a_unit(i):
        ic, nt = divmod(i, 4)
        hf = i % 2
        sl = slice(nt * 512, (nt + 1) * 512)
        w = [t_s23]
        if F3["gb"][hf] is not None:
            w.append((sS, F3["gb"][hf]))
        vld = dma("scalar", f3_gb[hf], sp_gt[ic, :, sl],
                  dF3l0 if hf == 0 else dF3l1, waits=w)
        w1_ = [vld] + F3["w0"]
        if F3["f0"][hf] is not None:
            w1_.append((sV, F3["f0"][hf]))
        on("scalar", lambda e, hf=hf: e.activation(
            f3_f0[hf], f3_gb[hf], AF.Copy, bias=MAGIC, scale=ccol[:, 6:7]),
           waits=w1_, inc=sS)
        F3["gb"][hf] = sS.n
        on("scalar", lambda e, hf=hf: e.activation(
            f3_f0[hf], f3_f0[hf], AF.Identity, bias=ccol[:, 13:14],
            scale=ccol[:, 5:6]), inc=sS, drain=True)
        on("scalar", lambda e, hf=hf: e.activation(
            f3_f2[hf], f3_f0[hf], AF.Sigmoid, bias=ecol[:, 2:3], scale=1.0),
           inc=sS, drain=True)
        on("scalar", lambda e, hf=hf: e.activation(
            f3_f2[hf], f3_f2[hf], AF.Copy, bias=MAGIC, scale=65536.0),
           inc=sS, drain=True)
        on("vector", lambda e, hf=hf: e.tensor_scalar(
            f3_f2[hf], f3_f2[hf], MAGIC, 65535.0, op0=ALU.subtract,
            op1=ALU.min), waits=[sS.tok], inc=sV, drain=True)
        w2_ = []
        if F3["ob"][hf] is not None:
            w2_.append(F3["ob"][hf])
        on("vector", lambda e, hf=hf: e.scalar_tensor_tensor(
            f3_ob[hf], f3_f0[hf], 1.0 / 65536.0, f3_f2[hf], op0=ALU.mult,
            op1=ALU.mult), waits=w2_, inc=sV, drain=True)
        F3["f0"][hf] = sV.n
        on("vector", lambda e, hf=hf, i=i: e.tensor_reduce(
            f3mx[:, i:i + 1], f3_ob[hf], axis=AX, op=ALU.max,
            apply_absolute_value=True), inc=sV, drain=True)
        F3["done"] = sV.n
        F3["ob"][hf] = dma("gpsimd", sp_pr[ic, :, sl], f3_ob[hf],
                           dF3s0 if hf == 0 else dF3s1, waits=[(sV, sV.n)])

    def plan_f3a_finish():
        on("vector", lambda e: e.tensor_reduce(
            mcol[:, 3:4], f3mx[:, 0:F3N], axis=AX, op=ALU.max),
           waits=[(sV, F3["done"])], inc=sV, drain=True)
        tmx4 = dma("gpsimd", mx4_i_t.ap().rearrange("a b -> b a"),
                   mcol[:, 3:4], dMxg, waits=[sV.tok])
        on("gpsimd", lambda e: e.collective_compute(
            "AllReduce", ALU.max, replica_groups=[list(range(NCORES))],
            ins=[mx4_i_t.ap().opt()], outs=[mx4_o_t.ap().opt()]),
           waits=[tmx4], inc=cc)
        vv = dma("sync", dmax[0:1, 0:128], mx4_o_t.ap(), dMxr, waits=[cc.tok])
        on("vector", lambda e: e.tensor_reduce(srow[0:1, 7:8],
                                               dmax[0:1, 0:128], axis=AX,
                                               op=ALU.max),
           waits=[vv], inc=sV, drain=True)
        on("vector", lambda e: e.tensor_scalar(srow[0:1, 7:8], srow[0:1, 7:8],
                                               1.0 / 32767.0, 1e-12,
                                               op0=ALU.mult, op1=ALU.max),
           inc=sV, drain=True)
        on("vector", lambda e: e.reciprocal(srow[0:1, 8:9], srow[0:1, 7:8]),
           inc=sV, drain=True)
        on("vector", lambda e: e.tensor_tensor(srow[0:1, 9:10], srow[0:1, 7:8],
                                               srow[0:1, 3:4], op=ALU.mult),
           inc=sV, drain=True)
        on("vector", lambda e: e.tensor_scalar(srow[0:1, 14:15],
                                               srow[0:1, 9:10], -MAGIC, None,
                                               op0=ALU.mult),
           inc=sV, drain=True)
        F3["t_s4"] = bcast_scales(sV.tok)
        F3["gb"] = [None, None]
        F3["f0"] = [None, None]

    def plan_f3b_unit(i):
        ic, nt = divmod(i, 4)
        hf = i % 2
        sl = slice(nt * 512, (nt + 1) * 512)
        w = [F3["t_s4"]]
        if F3["bb"][hf] is not None:
            w.append((sS, F3["bb"][hf]))
        else:
            w += [F3["ob"][0], F3["ob"][1]] + F3["w0"]
        v1_ = dma("scalar", f3_pb[hf], sp_pr[ic, :, sl],
                  dF3l0 if hf == 0 else dF3l1, waits=w)
        v2_ = dma("scalar", f3_ub[hf], sp_up[ic, :, sl],
                  dF3l2 if hf == 0 else dF3l3, waits=list(w))
        w1_ = [v1_, v2_]
        if F3["f0"][hf] is not None:
            w1_.append((sV, F3["f0"][hf]))
        on("scalar", lambda e, hf=hf: e.activation(
            f3_f0[hf], f3_pb[hf], AF.Copy, bias=MAGIC, scale=ccol[:, 8:9]),
           waits=w1_, inc=sS)
        on("scalar", lambda e, hf=hf: e.activation(
            f3_f2[hf], f3_ub[hf], AF.Copy, bias=MAGIC, scale=ccol[:, 4:5]),
           inc=sS)
        F3["bb"][hf] = sS.n
        on("scalar", lambda e, hf=hf: e.activation(
            f3_f0[hf], f3_f0[hf], AF.Identity, bias=ecol[:, 4:5], scale=1.0),
           inc=sS, drain=True)
        on("scalar", lambda e, hf=hf: e.activation(
            f3_f2[hf], f3_f2[hf], AF.Identity, bias=ccol[:, 14:15],
            scale=ccol[:, 9:10]), inc=sS, drain=True)
        w2_ = [sS.tok]
        if F3["ob"][hf] is not None:
            w2_.append(F3["ob"][hf])
        on("vector", lambda e, hf=hf: e.tensor_tensor(
            f3_ob[hf], f3_f0[hf], f3_f2[hf], op=ALU.mult),
           waits=w2_, inc=sV, drain=True)
        F3["f0"][hf] = sV.n
        on("vector", lambda e, hf=hf, i=i: e.tensor_reduce(
            f3mx[:, F3N + i:F3N + i + 1], f3_ob[hf], axis=AX, op=ALU.max,
            apply_absolute_value=True), inc=sV, drain=True)
        F3["done"] = sV.n
        F3["ob"][hf] = dma("gpsimd", sp_or[ic, :, sl], f3_ob[hf],
                           dF3s0 if hf == 0 else dF3s1, waits=[(sV, sV.n)])

    def plan_f3b_finish():
        on("vector", lambda e: e.tensor_reduce(
            mcol[:, 4:5], f3mx[:, F3N:2 * F3N], axis=AX, op=ALU.max),
           waits=[(sV, F3["done"])], inc=sV, drain=True)
        tmx5 = dma("gpsimd", mx5_i_t.ap().rearrange("a b -> b a"),
                   mcol[:, 4:5], dMxg, waits=[sV.tok])
        on("gpsimd", lambda e: e.collective_compute(
            "AllReduce", ALU.max, replica_groups=[list(range(NCORES))],
            ins=[mx5_i_t.ap().opt()], outs=[mx5_o_t.ap().opt()]),
           waits=[tmx5], inc=cc)
        vv = dma("sync", dmax[0:1, 0:128], mx5_o_t.ap(), dMxr, waits=[cc.tok])
        on("vector", lambda e: e.tensor_reduce(srow[0:1, 10:11],
                                               dmax[0:1, 0:128], axis=AX,
                                               op=ALU.max),
           waits=[vv], inc=sV, drain=True)
        on("vector", lambda e: e.tensor_scalar(srow[0:1, 10:11],
                                               srow[0:1, 10:11],
                                               1.0 / 32767.0, 1e-12,
                                               op0=ALU.mult, op1=ALU.max),
           inc=sV, drain=True)
        on("vector", lambda e: e.reciprocal(srow[0:1, 11:12],
                                            srow[0:1, 10:11]),
           inc=sV, drain=True)
        on("vector", lambda e: e.tensor_scalar(srow[0:1, 15:16],
                                               srow[0:1, 10:11], -MAGIC, None,
                                               op0=ALU.mult),
           inc=sV, drain=True)
        F3["t_s5"] = bcast_scales(sV.tok)

    f3_items = ([lambda i=i: plan_f3a_unit(i) for i in range(F3N)] +
                [plan_f3a_finish] +
                [lambda i=i: plan_f3b_unit(i) for i in range(F3N)] +
                [plan_f3b_finish])
    f3_pos = [0]

    def f3_pump(n=1):
        while n > 0 and f3_pos[0] < len(f3_items):
            f3_items[f3_pos[0]]()
            f3_pos[0] += 1
            n -= 1

    def f3_drain():
        f3_pump(len(f3_items))

    RG["gks"] = [None, None]
    qt_spill_v = [None, None]
    proj_pb_free = {"k": None, "q": None, "v": None}
    tr_guard = [None, None]
    pb4_free = None
    pb5_free = None
    h_b_ready = None
    t_group_done = None
    t_bgk1 = None
    bias_done = {"k": None, "q": None, "v": None}

    def plan_wcast(W_tiled_bf, mc, pc_tok, waits_extra=()):
        hf = mc % 2
        w = list(waits_extra) + [pc_tok[0], pc_tok[1]]
        if RG["wbf"][hf] is not None:
            w.append((sT, RG["wbf"][hf]))
        wb = wbf0 if hf == 0 else wbf1
        vld = dma("sync",
                  wb[:, 0:1024].rearrange("p (a m) -> p a m", a=NFC),
                  W_tiled_bf[mc].rearrange("kc p c -> p kc c"),
                  dWc0 if hf == 0 else dWc1, waits=w)
        return wb, vld, hf

    f1_arm()
    GK = {"pb_s": None}
    for g in range(NG):
        if g == 3:
            t_s23 = f1_finish()
        w = [t_wgk, t_p1]
        if t_group_done is not None:
            w.append(t_group_done)
        t_csfree = w[-1] if t_group_done is not None else None

        def _u1(e, g=g):
            last = None
            for fc in range(NFC):
                last = e.matmul(PB[0][0:16, 0:512],
                                wgk1b[:, fc * 16:(fc + 1) * 16],
                                z[:, fc * T + g * G:fc * T + (g + 1) * G],
                                start=(fc == 0), stop=(fc == NFC - 1))
            return last
        if proj_pb_free["k"] is not None:
            w.append((sV, proj_pb_free["k"]))
        on("tensor", _u1, waits=w, inc=sT)
        t_u1 = sT.tok
        if g == 0:
            def _u1b(e):
                last = None
                for fc in range(NFC):
                    last = e.matmul(PB[1][0:16, 0:1],
                                    wgk1b[:, fc * 16:(fc + 1) * 16],
                                    ln1b_cb[:, fc:fc + 1],
                                    start=(fc == 0), stop=(fc == NFC - 1))
                return last
            on("tensor", _u1b, waits=[t_gcolsb, t_wgk], inc=sT)
            on("vector", lambda e: e.tensor_copy(bgk1_c[:], PB[1][0:16, 0:1]),
               waits=[sT.tok], inc=sV)
            t_bgk1 = sV.tok
        u1sl = u1b[0:16, (g % 2) * G:(g % 2 + 1) * G]
        on("vector", lambda e, u1sl=u1sl: e.tensor_scalar(
            u1sl, PB[0][0:16, 0:512], bgk1_c[:], None,
            op0=ALU.add), waits=[t_u1, t_bgk1], inc=sV)
        t_u1r = sV.tok
        for mc in range(NFC):
            pb = PB[2]
            w = [t_u1r]
            if RG["gks"][mc % 2] is not None:
                w.append((sV, RG["gks"][mc % 2]))
            if GK["pb_s"] is not None:
                w.append((sS, GK["pb_s"]))
            if proj_pb_free["v"] is not None:
                w.append((sV, proj_pb_free["v"]))
            on("tensor", lambda e, mc=mc, g=g, pb=pb, u1sl=u1sl: e.matmul(
                pb[:, 0:512], wgk2b[0:16, mc * 128:(mc + 1) * 128],
                u1sl, start=True, stop=True),
               waits=w, inc=sT)
            gsb = SCA if mc % 2 == 0 else SCB
            on("scalar", lambda e, gsb=gsb, pb=pb, mc=mc: e.activation(
                gsb[:], pb[:, 0:512], AF.Exp, bias=bgk2_c[:, mc:mc + 1],
                scale=-1.0), waits=[sT.tok], inc=sS)
            GK["pb_s"] = sS.n
            on("scalar", lambda e, gsb=gsb: e.activation(
                gsb[:], gsb[:], AF.Ln, bias=ecol[:, 3:4], scale=1.0),
               inc=sS, drain=True)
            t_lg = sS.tok

            def _scan(e, mc=mc, gsb=gsb):
                last = None
                for ch in range(CPG):
                    last = e.tensor_tensor_scan(
                        cs_g[:, mc * G + ch * C:mc * G + (ch + 1) * C],
                        gsb[:, ch * C:(ch + 1) * C],
                        gsb[:, ch * C:(ch + 1) * C], 0.0,
                        op0=ALU.add, op1=ALU.bypass)
                return last
            wsc = [t_lg]
            if g == 0:
                wsc += [t_xqsp0, t_xqsp1, DT.get("xq")]
            else:
                wsc.append(DT.get("cs"))
            on("vector", _scan, waits=wsc, inc=sV)
            RG["gks"][mc % 2] = sV.n
            f1_pump(1)
            csend = cs_g[:].rearrange("p (f ch c) -> p (f ch) c",
                                      f=NFC, ch=CPG)[:, mc * CPG:(mc + 1) * CPG,
                                                     C - 1:C]
            on("scalar", lambda e, mc=mc, g=g, csend=csend: e.activation(
                expbc[:, mc * NCH + g * CPG:mc * NCH + (g + 1) * CPG],
                csend, AF.Exp, bias=ecol[:, 2:3], scale=-1.0 / 16.0),
               waits=[sV.tok], inc=sS)
        t_cs = sS.tok

        for gate, W_tl, pc_tok, bcolbuf in (
                ("k", wk_bf, WPC["k"], bk_c), ("q", wq_bf, WPC["q"], bq_c)):
            for mc in range(NFC):
                wb, t_wb, hf = plan_wcast(W_tl, mc, pc_tok)
                pb = PB[0] if gate == "k" else PB[1]

                def _mm(e, wb=wb, g=g, pb=pb):
                    last = None
                    for kc in range(NFC):
                        last = e.matmul(pb[:, 0:512],
                                        wb[:, kc * 128:(kc + 1) * 128],
                                        z[:, kc * T + g * G:kc * T + (g + 1) * G],
                                        start=(kc == 0), stop=(kc == NFC - 1))
                    return last
                w3 = [t_wb, t_p1]
                if proj_pb_free[gate] is not None:
                    w3.append((sV, proj_pb_free[gate]))
                on("tensor", _mm, waits=w3, inc=sT)
                RG["wbf"][hf] = sT.n
                t_mm = sT.tok
                if g == 0:
                    def _bmm(e, wb=wb):
                        last = None
                        for kc in range(NFC):
                            last = e.matmul(PB[5][:, 0:1],
                                            wb[:, kc * 128:(kc + 1) * 128],
                                            ln1b_cb[:, kc:kc + 1],
                                            start=(kc == 0), stop=(kc == NFC - 1))
                        return last
                    wb5_ = [t_gcolsb, t_wb]
                    if PB5G["v"] is not None:
                        wb5_.append((sV, PB5G["v"]))
                    on("tensor", _bmm, waits=wb5_, inc=sT)
                    RG["wbf"][hf] = sT.n
                    on("vector", lambda e, mc=mc, bcolbuf=bcolbuf: e.tensor_copy(
                        bcolbuf[:, mc:mc + 1], PB[5][:, 0:1]),
                       waits=[sT.tok], inc=sV)
                    bias_done[gate] = sV.n
                    PB5G["v"] = sV.n
                t_bias = (sV, bias_done[gate]) if bias_done[gate] else None
                eb = SCA if gate == "k" else SCB
                we = [t_cs]
                if proj_pb_free[gate] is not None:
                    we.append((sV, proj_pb_free[gate]))
                if gate == "k":
                    on("scalar", lambda e, mc=mc, eb=eb: e.activation(
                        eb[:], cs_g[:, mc * G:(mc + 1) * G], AF.Exp,
                        bias=ecol[:, 2:3], scale=1.0 / 16.0),
                       waits=we, inc=sS)
                else:
                    on("scalar", lambda e, mc=mc, eb=eb: e.activation(
                        eb[:], cs_g[:, mc * G:(mc + 1) * G], AF.Exp,
                        bias=ecol[:, 1:2], scale=-1.0 / 16.0),
                       waits=we, inc=sS)
                t_exp = sS.tok
                if gate == "k":
                    dsl = kt_g[:, mc * G:(mc + 1) * G]
                else:
                    dsl = qtr[:, (g % 2) * NFC * G + mc * G:
                              (g % 2) * NFC * G + (mc + 1) * G]
                wev = [t_mm, t_exp, t_bias]
                if gate == "q" and qt_spill_v[g % 2] is not None:
                    wev.append(qt_spill_v[g % 2])
                elif gate == "q":
                    wev += [t_p2s, t_p2t]
                if gate == "k" and g > 0:
                    wev.append(DT.get("kt"))
                if gate == "k" and g == 0:
                    wev += [t_xqsp0, t_xqsp1]
                on("vector", lambda e, dsl=dsl, eb=eb, pb=pb, mc=mc,
                   bcolbuf=bcolbuf: e.scalar_tensor_tensor(
                       dsl, pb[:, 0:512], bcolbuf[:, mc:mc + 1], eb[:],
                       op0=ALU.add, op1=ALU.mult),
                   waits=wev, inc=sV)
                proj_pb_free[gate] = sV.n
                f1_pump(2)

        qsl = qtr[:, (g % 2) * NFC * G:(g % 2 + 1) * NFC * G]
        qt_spill_v[g % 2] = dma(
            "gpsimd", qt_sp[g].rearrange("f p c -> p f c"),
            qsl.rearrange("p (f c) -> p f c", f=NFC),
            dQs0 if g % 2 == 0 else dQs1,
            waits=[(sV, proj_pb_free["q"])])

        for mc in range(NFC):
            wb, t_wb, hf = plan_wcast(wv_bf, mc, WPC["v"])

            def _mm(e, wb=wb, g=g):
                last = None
                for kc in range(NFC):
                    last = e.matmul(PB[2][:, 0:512],
                                    wb[:, kc * 128:(kc + 1) * 128],
                                    z[:, kc * T + g * G:kc * T + (g + 1) * G],
                                    start=(kc == 0), stop=(kc == NFC - 1))
                return last
            w3 = [t_wb, t_p1]
            if proj_pb_free["v"] is not None:
                w3.append((sV, proj_pb_free["v"]))
            on("tensor", _mm, waits=w3, inc=sT)
            RG["wbf"][hf] = sT.n
            t_mm = sT.tok
            if g == 0:
                def _bmm(e, wb=wb):
                    last = None
                    for kc in range(NFC):
                        last = e.matmul(PB[5][:, 0:1],
                                        wb[:, kc * 128:(kc + 1) * 128],
                                        ln1b_cb[:, kc:kc + 1],
                                        start=(kc == 0), stop=(kc == NFC - 1))
                    return last
                wb5_ = [t_gcolsb, t_wb]
                if PB5G["v"] is not None:
                    wb5_.append((sV, PB5G["v"]))
                on("tensor", _bmm, waits=wb5_, inc=sT)
                RG["wbf"][hf] = sT.n
                on("vector", lambda e, mc=mc: e.tensor_copy(
                    bv_c[:, mc:mc + 1], PB[5][:, 0:1]), waits=[sT.tok], inc=sV)
                bias_done["v"] = sV.n
                PB5G["v"] = sV.n
            t_bias = (sV, bias_done["v"]) if bias_done["v"] else None
            wvv = [t_mm, t_bias]
            if g > 0:
                wvv.append(DT.get("vt"))
            else:
                wvv += [t_xqsp0, t_xqsp1]
            on("vector", lambda e, mc=mc: e.tensor_scalar(
                vt_g[:, mc * G:(mc + 1) * G], PB[2][:, 0:512],
                bv_c[:, mc:mc + 1], None, op0=ALU.add),
               waits=wvv, inc=sV)
            proj_pb_free["v"] = sV.n
            f1_pump(2)
        t_proj = sV.tok

        if "cs" in dbg:
            DT["cs"] = dma("sync", DOUT["cs"][g], cs_g[:], dO,
                           waits=[t_proj, t_cs])
        if "kt" in dbg:
            DT["kt"] = dma("sync", DOUT["kt"][g], kt_g[:], dO, waits=[t_proj])
        if "vt" in dbg:
            DT["vt"] = dma("sync", DOUT["vt"][g], vt_g[:], dO, waits=[t_proj])

        for ch in range(CPG):
            c = g * CPG + ch
            hf = c % 2
            ktT = ktT0
            vtT = vtT0
            w = [t_proj]
            if tr_guard[hf] is not None:
                w.append((sV, tr_guard[hf]))
            if RG["pbf"]:
                w.append((sV, RG["pbf"]))

            def _ktr(e, ch=ch):
                last = None
                for fc in range(NFC):
                    last = e.transpose(
                        PBF[:, fc * 128:(fc + 1) * 128],
                        kt_g[:, fc * G + ch * C:fc * G + (ch + 1) * C],
                        ident[:])
                return last
            on("tensor", _ktr, waits=w, inc=sT)
            on("vector", lambda e, ktT=ktT: e.tensor_copy(ktT[:], PBF[:]),
               waits=[sT.tok], inc=sV)
            RG["pbf"] = sV.n
            t_ktT = sV.tok

            def _vtr(e, ch=ch):
                last = None
                for fc in range(NFC):
                    last = e.transpose(
                        PBF[:, fc * 128:(fc + 1) * 128],
                        vt_g[:, fc * G + ch * C:fc * G + (ch + 1) * C],
                        ident[:])
                return last
            on("tensor", _vtr, waits=[(sV, RG["pbf"])], inc=sT)
            on("vector", lambda e, vtT=vtT: e.tensor_copy(vtT[:], PBF[:]),
               waits=[sT.tok], inc=sV)
            RG["pbf"] = sV.n
            t_vtT = sV.tok

            for hh in range(H):
                ab = atm0 if hh % 2 == 0 else atm1

                def _at(e, hh=hh, ch=ch, c=c, g=g):
                    last = None
                    for kc in range(2):
                        fc = hh * 2 + kc
                        last = e.matmul(
                            PB[6][:, 0:128],
                            kt_g[:, fc * G + ch * C:fc * G + (ch + 1) * C],
                            qtr[:, (g % 2) * NFC * G + fc * G + ch * C:
                                (g % 2) * NFC * G + fc * G + (ch + 1) * C],
                            start=(kc == 0), stop=(kc == 1))
                    return last
                w = [t_proj]
                if PB6G["v"] is not None:
                    w.append((sV, PB6G["v"]))
                on("tensor", _at, waits=w, inc=sT)
                on("vector", lambda e, ab=ab: e.tensor_tensor(
                    ab[:], PB[6][:, 0:128], utmask[:], op=ALU.mult),
                   waits=[sT.tok], inc=sV)
                PB6G["v"] = sV.n
                t_atm = sV.tok

                def _o(e, hh=hh, c=c, ab=ab, vtT=vtT, g=g, ch=ch):
                    last = e.matmul(PB[5][:, 0:256], ab[:],
                                    vtT[:, hh * DV:(hh + 1) * DV],
                                    start=True, stop=(c == 0))
                    if c != 0:
                        for kc in range(2):
                            fc = hh * 2 + kc
                            last = e.matmul(
                                PB[5][:, 0:256],
                                qtr[:, (g % 2) * NFC * G + fc * G + ch * C:
                                    (g % 2) * NFC * G + fc * G + (ch + 1) * C],
                                h_b[:, fc * DV:(fc + 1) * DV],
                                start=False, stop=(kc == 1))
                    return last
                w = [t_atm, t_vtT]
                if c != 0:
                    w.append((sV, h_b_ready))
                if pb5_free is not None:
                    w.append((sV, pb5_free))
                elif PB5G["v"] is not None:
                    w.append((sV, PB5G["v"]))
                on("tensor", _o, waits=w, inc=sT)
                woe = [sT.tok]
                if c == 0 and hh == 0:
                    woe += [DT.get("xn"), t_xnspill[0], t_xnspill[1]]
                if c >= 12 and hh == 0:
                    woe.append(F1["done_mm"])
                on("vector", lambda e, c=c, hh=hh: e.tensor_copy(
                    xnO[:, c * D + hh * DV:c * D + (hh + 1) * DV],
                    PB[5][:, 0:256]), waits=woe, inc=sV)
                pb5_free = sV.n
                PB5G["v"] = sV.n

                def _st(e, hh=hh, ktT=ktT, vtT=vtT):
                    last = None
                    for kc in range(2):
                        fc = hh * 2 + kc
                        last = e.matmul(PB[4][:, kc * 256:(kc + 1) * 256],
                                        ktT[:, fc * 128:(fc + 1) * 128],
                                        vtT[:, hh * DV:(hh + 1) * DV],
                                        start=True, stop=True)
                    return last
                w = [t_ktT, t_vtT]
                if pb4_free is not None:
                    w.append((sV, pb4_free))
                on("tensor", _st, waits=w, inc=sT)
                t_stm = sT.tok

                def _hup(e, hh=hh, c=c):
                    last = None
                    for kc in range(2):
                        fc = hh * 2 + kc
                        hsl = h_f[:, fc * DV:(fc + 1) * DV]
                        ecc = expbc[:, fc * NCH + c:fc * NCH + c + 1]
                        if c == 0:
                            last = e.tensor_scalar(
                                hsl, PB[4][:, kc * 256:(kc + 1) * 256], ecc,
                                None, op0=ALU.mult)
                        else:
                            e.tensor_tensor(hsl, hsl,
                                            PB[4][:, kc * 256:(kc + 1) * 256],
                                            op=ALU.add)
                            e.drain()
                            last = e.tensor_scalar(hsl, hsl, ecc, None,
                                                   op0=ALU.mult)
                    return last
                on("vector", _hup, waits=[t_stm], inc=sV, drain=True)
                pb4_free = sV.n
                t_h = sV.tok
                if c < NCH - 1:
                    def _hb(e, hh=hh):
                        last = None
                        for kc in range(2):
                            fc = hh * 2 + kc
                            last = e.tensor_copy(h_b[:, fc * DV:(fc + 1) * DV],
                                                 h_f[:, fc * DV:(fc + 1) * DV])
                        return last
                    on("vector", _hb, waits=[t_h], inc=sV, drain=True)
                    h_b_ready = sV.n
            tr_guard[hf] = sV.n
            f1_pump(2)
        t_group_done = (sV, sV.n)

    t_gla = (sV, sV.n)
    dump("oloc", xnO[:], [t_gla])
    dump("sloc", h_f[:], [t_gla])
    dump("expbc", expbc[:], [(sS, sS.n)])

    def _escan(e):
        last = None
        for fc in range(NFC):
            last = e.tensor_tensor_scan(
                etab[:, fc * NCH:(fc + 1) * NCH],
                expbc[:, fc * NCH:(fc + 1) * NCH],
                expbc[:, fc * NCH:(fc + 1) * NCH], 1.0,
                op0=ALU.mult, op1=ALU.bypass)
        return last
    on("vector", _escan, waits=[t_gla, (sS, sS.n)], inc=sV, drain=True)
    t_et = sV.tok
    dump("etab", etab[:], [t_et])

    tag1 = dma("gpsimd", ag_i[:, 0:NFC * DV], h_f[:], dAg, waits=[t_gla])
    dsl = etab[:].rearrange("p (f c) -> p f c", f=NFC)[:, :, NCH - 1:NCH]
    tag2 = dma("gpsimd",
               ag_i[:, NFC * DV:AGW].rearrange("p (f c) -> p f c", f=NFC),
               dsl, dAg, waits=[t_et], allow_slow_non_contiguous=True)
    on("gpsimd", lambda e: e.collective_compute(
        "AllGather", ALU.bypass, replica_groups=[list(range(NCORES))],
        ins=[ag_i_t.ap().opt()], outs=[ag_o_t.ap().opt()]),
       waits=[tag2], inc=cc)
    t_ag = cc.tok
    f3_arm()
    f3_pump(6)

    on("vector", lambda e: e.memset(hin[:], 0.0), waits=[t_gla], inc=sV)
    Sj_v = [None, None]
    t_p4all = [(sS, sS.n), (sV, sV.n), (sT, sT.n)]
    for j in range(NCORES - 1):
        Sjb = Sjb_ if j % 2 == 0 else SjbB_
        w = [t_ag]
        if Sj_v[j % 2] is not None:
            w.append((sV, Sj_v[j % 2]))
        elif j == 0:
            w.extend([t for t in qt_spill_v if t is not None])
            w.append(t_gla)
        else:
            w.extend(t_p4all)
        vld = dma("sync", Sjb[:], ag_o[j],
                  dSj if j % 2 == 0 else dSj2, waits=w)

        def _comb(e, j=j, Sjb=Sjb):
            last = None
            for fc in range(NFC):
                dd = dcol[:, 9:10]
                e.scalar_tensor_tensor(dd, Sjb[:, NFC * DV + fc:NFC * DV + fc + 1],
                                       cmk[:, j:j + 1], cmki[:, j:j + 1],
                                       op0=ALU.mult, op1=ALU.add)
                e.drain()
                hsl = hin[:, fc * DV:(fc + 1) * DV]
                e.tensor_scalar(hsl, hsl, dd, None, op0=ALU.mult)
                e.drain()
                last = e.scalar_tensor_tensor(
                    hsl, Sjb[:, fc * DV:(fc + 1) * DV], cmk[:, j:j + 1], hsl,
                    op0=ALU.mult, op1=ALU.add)
                e.drain()
            return last
        w2 = [vld]
        if Sj_v[1 - j % 2] is not None:
            w2.append((sV, Sj_v[1 - j % 2]))
        on("vector", _comb, waits=w2, inc=sV, drain=True)
        Sj_v[j % 2] = sV.n
        f3_pump(4)
    t_hin = sV.tok
    dump("hin", hin[:], [t_hin])

    corr_pb = None
    qrl_v = [None, None]
    for c in range(NCH):
        if c % CPG == 0:
            g = c // CPG
            wrl = [t_gla, t_hin] + [t for t in qt_spill_v if t is not None]
            if corr_pb is not None:
                wrl.append((sT, corr_mm_last))
            qrl_v[g % 2] = dma(
                "sync", qtr[:, (g % 2) * NFC * G:(g % 2 + 1) * NFC * G]
                .rearrange("p (f c) -> p f c", f=NFC),
                qt_sp[g].rearrange("f p c -> p f c"),
                dQr0 if g % 2 == 0 else dQr1, waits=wrl)
        def _hd(e, c=c):
            last = None
            for fc in range(NFC):
                if c == 0:
                    last = e.tensor_copy(hinb[:, fc * DV:(fc + 1) * DV],
                                         hin[:, fc * DV:(fc + 1) * DV])
                else:
                    last = e.tensor_scalar(
                        hinb[:, fc * DV:(fc + 1) * DV],
                        hin[:, fc * DV:(fc + 1) * DV],
                        etab[:, fc * NCH + c - 1:fc * NCH + c], None,
                        op0=ALU.mult)
            return last
        w = [t_hin, t_et]
        if corr_pb is not None:
            w.append((sT, corr_mm_last))
        else:
            w += [DT.get("oloc"), DT.get("hin"), DT.get("sloc"),
                  DT.get("etab"), DT.get("expbc")]
        on("vector", _hd, waits=w, inc=sV, drain=True)
        t_hd = sV.tok
        for hh in range(H):
            g = c // CPG
            ch = c % CPG

            def _cmm(e, hh=hh, g=g, ch=ch):
                last = None
                for kc in range(2):
                    fc = hh * 2 + kc
                    last = e.matmul(PB[5][:, 0:256],
                                    qtr[:, (g % 2) * NFC * G + fc * G + ch * C:
                                        (g % 2) * NFC * G + fc * G + (ch + 1) * C],
                                    hinb[:, fc * DV:(fc + 1) * DV],
                                    start=(kc == 0), stop=(kc == 1))
                return last
            w = [t_hd, qrl_v[g % 2]]
            if corr_pb is not None:
                w.append((sV, corr_pb))
            elif PB5G["v"] is not None:
                w.append((sV, PB5G["v"]))
            on("tensor", _cmm, waits=w, inc=sT)
            corr_mm_last = sT.n
            on("vector", lambda e, c=c, hh=hh: e.tensor_tensor(
                xnO[:, c * D + hh * DV:c * D + (hh + 1) * DV],
                xnO[:, c * D + hh * DV:c * D + (hh + 1) * DV],
                PB[5][:, 0:256], op=ALU.add), waits=[sT.tok], inc=sV,
               drain=True)
            corr_pb = sV.n
            PB5G["v"] = sV.n
        f3_pump(4)
    t_ofin = (sV, sV.n)
    dump("ofin", xnO[:], [t_ofin])

    v_rw = dma("sync", rowbank[0:1, 0:DV], rmsw_h[None, :], dSm, waits=[t_ofin])
    w6 = [v_rw, t_consts]
    if PB6G["v"] is not None:
        w6.append((sV, PB6G["v"]))
    on("tensor", lambda e: e.matmul(PB[6][:, 0:DV], onesr[:],
                                    rowbank[0:1, 0:DV], start=True, stop=True),
       waits=w6, inc=sT)

    def _rmsrep(e):
        last = None
        for hh in range(H):
            last = e.tensor_copy(rmsrep[:, hh * DV:(hh + 1) * DV],
                                 PB[6][:, 0:DV])
        return last
    on("vector", _rmsrep, waits=[sT.tok, t_gla], inc=sV, drain=True)
    PB6G["v"] = sV.n
    t_rmsrep = sV.tok

    for tt in range(NTT):
        xsl = xnO[:, tt * D:(tt + 1) * D]

        def _rs(e, xsl=xsl):
            last = None
            for hh in range(H):
                last = e.bn_stats(stat6[:, hh * 6:(hh + 1) * 6],
                                  xsl[:, hh * DV:(hh + 1) * DV])
            return last
        on("vector", _rs, waits=[t_ofin, (sS, sS.n)], inc=sV, drain=True)

        def _ra(e):
            last = None
            for hh in range(H):
                last = e.bn_aggr(stat2[:, hh * 2:hh * 2 + 2],
                                 stat6[:, hh * 6:(hh + 1) * 6])
            return last
        on("vector", _ra, inc=sV, drain=True)

        def _ms(e, tt=tt):
            last = None
            for hh in range(H):
                last = e.scalar_tensor_tensor(
                    rst16[:, tt * 4 + hh:tt * 4 + hh + 1],
                    stat2[:, hh * 2:hh * 2 + 1],
                    stat2[:, hh * 2:hh * 2 + 1],
                    stat2[:, hh * 2 + 1:hh * 2 + 2],
                    op0=ALU.mult, op1=ALU.add)
            return last
        on("vector", _ms, inc=sV, drain=True)
    t_ms16 = sV.tok
    on("scalar", lambda e: e.activation(rst16[:], rst16[:], AF.Ln,
                                        bias=ecol[:, 0:1], scale=1.0),
       waits=[t_ms16], inc=sS)
    on("scalar", lambda e: e.activation(rst16[:], rst16[:], AF.Exp,
                                        bias=ecol[:, 2:3], scale=-0.5),
       inc=sS, drain=True)
    t_rst16 = sS.tok

    gproj_pb = None
    bg_done = None
    for q4 in range(NG):
        for mc in range(NFC):
            wb, t_wb, hf = plan_wcast(wg_bf, mc, WPC["g"],
                                      waits_extra=[t_ofin, t_hin])

            def _mm(e, wb=wb, q4=q4):
                last = None
                for kc in range(NFC):
                    last = e.matmul(PB[3][:, 0:512],
                                    wb[:, kc * 128:(kc + 1) * 128],
                                    z[:, kc * T + q4 * G:kc * T + (q4 + 1) * G],
                                    start=(kc == 0), stop=(kc == NFC - 1))
                return last
            w3 = [t_wb]
            if gproj_pb is not None:
                w3.append((sV, gproj_pb))
            elif q4 == 0 and mc == 0:
                w3.append(F1["pb3_tok"])
            on("tensor", _mm, waits=w3, inc=sT)
            RG["wbf"][hf] = sT.n
            t_mm = sT.tok
            if q4 == 0:
                def _bmm2(e, wb=wb):
                    last = None
                    for kc in range(NFC):
                        last = e.matmul(PB[5][:, 0:1],
                                        wb[:, kc * 128:(kc + 1) * 128],
                                        ln1b_cb[:, kc:kc + 1],
                                        start=(kc == 0), stop=(kc == NFC - 1))
                    return last
                wb5 = [t_gcolsb]
                if PB5G["v"] is not None:
                    wb5.append((sV, PB5G["v"]))
                on("tensor", _bmm2, waits=wb5, inc=sT)
                RG["wbf"][hf] = sT.n
                on("vector", lambda e, mc=mc: e.tensor_copy(
                    bg_c[:, mc:mc + 1], PB[5][:, 0:1]), waits=[sT.tok], inc=sV)
                bg_done = sV.n
                PB5G["v"] = sV.n
            t_bg = (sV, bg_done) if bg_done else None
            wgq = [t_mm, t_bg]
            if q4 == 0 and mc == 0:
                wgq += [DT.get("hin"), DT.get("ofin")]
            on("vector", lambda e, mc=mc: e.tensor_scalar(
                gfm_q[:, mc * G:(mc + 1) * G], PB[3][:, 0:512],
                bg_c[:, mc:mc + 1], None, op0=ALU.add),
               waits=wgq, inc=sV)
            gproj_pb = sV.n
            f3_pump(2)
        t_gq = sV.tok

        for t4 in range(4):
            tt = q4 * 4 + t4
            xsl = xnO[:, tt * D:(tt + 1) * D]
            t_rr = t_rst16
            w = [t_gq]
            if RG["pbf"]:
                w.append((sV, RG["pbf"]))

            def _gtr(e, t4=t4):
                last = None
                for fc in range(NFC):
                    last = e.transpose(
                        PBF[:, fc * 128:(fc + 1) * 128],
                        gfm_q[:, fc * G + t4 * 128:fc * G + (t4 + 1) * 128],
                        ident[:])
                return last
            on("tensor", _gtr, waits=w, inc=sT)
            zb = ztB0 if tt % 2 == 0 else ztB1
            on("vector", lambda e, zb=zb: e.tensor_copy(
                zb[:].rearrange("p (f c) -> p f c", f=NFC),
                PBF[:].rearrange("p (f c) -> p f c", f=NFC)),
               waits=[sT.tok], inc=sV)
            RG["pbf"] = sV.n
            t_gtm = sV.tok
            for half in range(2):
                gsl_b = zb[:, half * 512:(half + 1) * 512]
                fs = gsc0 if half == 0 else gsc1
                wsg = [t_gtm]
                if tt == 0:
                    wsg.append(DT.get("sloc"))
                on("scalar", lambda e, gsl_b=gsl_b, fs=fs: e.activation(
                    fs[:], gsl_b, AF.Sigmoid, bias=ecol[:, 2:3], scale=1.0),
                   waits=wsg, inc=sS)
                on("vector", lambda e, fs=fs, gsl_b=gsl_b: e.tensor_tensor(
                    fs[:], fs[:], gsl_b, op=ALU.mult),
                   waits=[sS.tok], inc=sV, drain=True)
                on("vector", lambda e, fs=fs, half=half: e.tensor_tensor(
                    fs[:], fs[:], rmsrep[:, half * 512:(half + 1) * 512],
                    op=ALU.mult), waits=[t_rmsrep], inc=sV, drain=True)
                t_silu = sV.tok

                def _gate(e, fs=fs, half=half, xsl=xsl, zb=zb, tt=tt):
                    last = None
                    for hs in range(2):
                        hh = half * 2 + hs
                        last = e.scalar_tensor_tensor(
                            zb[:, hh * DV:(hh + 1) * DV],
                            xsl[:, hh * DV:(hh + 1) * DV],
                            rst16[:, tt * 4 + hh:tt * 4 + hh + 1],
                            fs[:, hs * DV:(hs + 1) * DV],
                            op0=ALU.mult, op1=ALU.mult)
                    return last
                on("vector", _gate, waits=[t_silu, t_rr], inc=sV, drain=True)
            t_gt = sV.tok
            w = [t_gt]
            if RG["pbf"]:
                w.append((sV, RG["pbf"]))

            def _gttr(e, zb=zb):
                last = None
                for fc in range(NFC):
                    last = e.transpose(PBF[:, fc * 128:(fc + 1) * 128],
                                       zb[:, fc * 128:(fc + 1) * 128],
                                       ident[:])
                return last
            on("tensor", _gttr, waits=w, inc=sT)
            gdst = gated[:].rearrange("p (f t) -> p f t",
                                      f=NFC)[:, :, tt * 128:(tt + 1) * 128]
            wgd = [sT.tok]
            if tt == 0:
                wgd += [DT.get("cs"), DT.get("kt"), DT.get("vt")]
            on("vector", lambda e, gdst=gdst: e.tensor_copy(
                gdst, PBF[:].rearrange("p (f c) -> p f c", f=NFC)),
               waits=wgd, inc=sV)
            RG["pbf"] = sV.n
            f3_pump(4)
    t_gated = sV.tok
    dump("gated", gated[:], [t_gated])
    t_rpk = plan_repack(t_wagd)

    wo_pb = [None] * 4
    for qr in range(4):
        for nt in range(2):
            for kc in range(NFC):
                hf = kc % 2
                w = [t_gated]
                wb = wbf0 if hf == 0 else wbf1
                w.append(WPC["wo"])
                if RG["wbf"][hf] is not None:
                    w.append((sT, RG["wbf"][hf]))
                t_wb = dma("sync", wb[:, 0:512],
                           wo_bf[kc * 128:(kc + 1) * 128, nt * 512:(nt + 1) * 512],
                           dWo0 if hf == 0 else dWo1, waits=w)
                for tc4 in range(4):
                    tc = qr * 4 + tc4

                    def _mm(e, kc=kc, tc=tc, tc4=tc4, wb=wb):
                        return e.matmul(
                            PB[tc4][:, 0:512],
                            gated[:, kc * T + tc * 128:kc * T + (tc + 1) * 128],
                            wb[:, 0:512], start=(kc == 0), stop=(kc == NFC - 1))
                    w3 = [t_wb]
                    if kc == 0 and wo_pb[tc4] is not None:
                        w3.append((sV, wo_pb[tc4]))
                    on("tensor", _mm, waits=w3, inc=sT)
                    RG["wbf"][hf] = sT.n
                    if kc == NFC - 1:
                        t_mm = sT.tok
                        xb = xldB0 if tc % 2 == 0 else xldB1
                        w4 = [t_gated, t_xnspill[0], t_xnspill[1]]
                        if RG["xld"][tc % 2] is not None:
                            w4.append((sV, RG["xld"][tc % 2]))
                        vx = dma("sync", xb[:], xn_sp[tc],
                                 dXn0 if tc % 2 == 0 else dXn1, waits=w4)
                        on("vector", lambda e, tc=tc, nt=nt, xb=xb, tc4=tc4:
                           e.tensor_tensor(
                               xnO[:, tc * D + nt * 512:tc * D + (nt + 1) * 512],
                               xb[:, nt * 512:(nt + 1) * 512],
                               PB[tc4][:, 0:512], op=ALU.add),
                           waits=[t_mm, vx, DT.get("ofin")], inc=sV,
                           drain=True)
                        wo_pb[tc4] = sV.n
                        RG["xld"][tc % 2] = sV.n
    t_xa1 = sV.tok
    dump("xa1", xnO[:], [t_xa1])

    y = z
    for tt in range(NTT):
        xsl = xnO[:, tt * D:(tt + 1) * D]
        on("vector", lambda e, xsl=xsl: e.bn_stats(stat6[:, 0:6], xsl[:, 0:512]),
           waits=[t_xa1, (sS, sS.n)], inc=sV, drain=True)
        on("vector", lambda e, xsl=xsl: e.bn_stats(stat6[:, 6:12],
                                                   xsl[:, 512:1024]), inc=sV)
        on("vector", lambda e: e.bn_aggr(stat2[:, 0:2], stat6[:, 0:12]),
           inc=sV, drain=True)
        on("scalar", lambda e: e.activation(dcol[:, 0:1], stat2[:, 1:2], AF.Ln,
                                            bias=ecol[:, 0:1], scale=1.0),
           waits=[sV.tok], inc=sS)
        on("scalar", lambda e: e.activation(dcol[:, 1:2], dcol[:, 0:1], AF.Exp,
                                            bias=ecol[:, 2:3], scale=-0.5),
           inc=sS, drain=True)
        on("vector", lambda e: e.scalar_tensor_tensor(
            dcol[:, 2:3], stat2[:, 0:1], -1.0, dcol[:, 1:2],
            op0=ALU.mult, op1=ALU.mult), waits=[sS.tok], inc=sV, drain=True)
        zb = ztC0 if tt % 2 == 0 else ztC1
        on("scalar", lambda e, zb=zb, xsl=xsl: e.activation(
            zb[:], xsl, AF.Identity, bias=dcol[:, 2:3], scale=dcol[:, 1:2]),
           waits=[sV.tok], inc=sS)
        w = [sS.tok]
        if RG["pbf"]:
            w.append((sV, RG["pbf"]))

        def _ytr(e, zb=zb):
            last = None
            for fc in range(NFC):
                last = e.transpose(PBF[:, fc * 128:(fc + 1) * 128],
                                   zb[:, fc * 128:(fc + 1) * 128], ident[:])
            return last
        on("tensor", _ytr, waits=w, inc=sT)
        ydst = y[:].rearrange("p (f t) -> p f t", f=NFC)[:, :, tt * 128:(tt + 1) * 128]
        wye = [sT.tok]
        if tt == 0:
            wye.append(DT.get("z"))
        on("vector", lambda e, ydst=ydst: e.tensor_copy(
            ydst, PBF[:].rearrange("p (f c) -> p f c", f=NFC)),
           waits=wye, inc=sV)
        RG["pbf"] = sV.n
    def _yscale(e):
        last = None
        for fc in range(NFC):
            last = e.tensor_scalar(y[:, fc * T:(fc + 1) * T],
                                   y[:, fc * T:(fc + 1) * T],
                                   ln2w_c[:, fc:fc + 1], None, op0=ALU.mult)
        return last
    on("vector", _yscale, waits=[t_gcols], inc=sV, drain=True)
    t_y = sV.tok
    dump("y", y[:], [t_y])
    t_b2row = dma("sync", b2_row, b2_h[None, :], dSm, waits=[t_rmsrep])

    b1w_done = None
    w2_pb = [None] * 4
    for half in range(2):
        h1_pb = [None, None]
        for mc in range(NFF):
            wb, t_wb, hf = plan_wcast(w1_bf, mc, WPC["w1"], waits_extra=[t_y])
            if half == 0:
                def _bmm(e, wb=wb):
                    last = None
                    for kc in range(NFC):
                        last = e.matmul(PB[5][:, 0:1],
                                        wb[:, kc * 128:(kc + 1) * 128],
                                        ln2b_cb[:, kc:kc + 1],
                                        start=(kc == 0), stop=(kc == NFC - 1))
                    return last
                w5 = [t_gcolsb, t_wb]
                if PB5G["v"] is not None:
                    w5.append((sV, PB5G["v"]))
                on("tensor", _bmm, waits=w5, inc=sT)
                RG["wbf"][hf] = sT.n
                on("vector", lambda e, mc=mc: e.scalar_tensor_tensor(
                    b1w_c[:, mc:mc + 1], PB[5][:, 0:1], 1.0,
                    b1_c[:, mc:mc + 1], op0=ALU.mult, op1=ALU.add),
                   waits=[sT.tok], inc=sV)
                b1w_done = sV.n
                PB5G["v"] = sV.n
            t_b1 = (sV, b1w_done) if b1w_done else None
            for nt in range(2):
                pbx = PB[nt]

                def _mm(e, wb=wb, half=half, nt=nt, pbx=pbx):
                    last = None
                    for kc in range(NFC):
                        last = e.matmul(
                            pbx[:, 0:512], wb[:, kc * 128:(kc + 1) * 128],
                            y[:, kc * T + half * 1024 + nt * 512:
                              kc * T + half * 1024 + (nt + 1) * 512],
                            start=(kc == 0), stop=(kc == NFC - 1))
                    return last
                w3 = [t_wb]
                if h1_pb[nt] is not None:
                    w3.append((sS, h1_pb[nt]))
                elif w2_pb[nt] is not None:
                    w3.append((sV, w2_pb[nt]))
                elif wo_pb[nt] is not None:
                    w3.append((sV, wo_pb[nt]))
                on("tensor", _mm, waits=w3, inc=sT)
                if nt == 1:
                    RG["wbf"][hf] = sT.n
                t_mm = sT.tok
                fs = fscW0 if nt == 0 else fscW1
                on("scalar", lambda e, fs=fs, pbx=pbx, mc=mc: e.activation(
                    fs[:], pbx[:, 0:512], AF.Identity, bias=b1w_c[:, mc:mc + 1],
                    scale=1.0), waits=[t_mm, t_b1], inc=sS)
                h1_pb[nt] = sS.n
                t_z1 = sS.tok
                on("vector", lambda e, fs=fs: e.scalar_tensor_tensor(
                    fs[:], fs[:], 0.01, fs[:], op0=ALU.mult, op1=ALU.max),
                   waits=[t_z1], inc=sV, drain=True)
                t_lr = sV.tok
                wh1 = [t_lr]
                if mc == 0 and nt == 0:
                    wh1.append(DT.get("gated"))
                    if half == 1:
                        wh1.append(DT.get("h1"))
                on("scalar", lambda e, fs=fs, mc=mc, nt=nt: e.activation(
                    h1h[:, mc * 1024 + nt * 512:mc * 1024 + (nt + 1) * 512],
                    fs[:], AF.Square, bias=ecol[:, 2:3], scale=1.0),
                   waits=wh1, inc=sS, drain=True)
        t_h1 = sS.tok
        if "h1" in dbg:
            DT["h1"] = dma("sync", DOUT["h1"][half], h1h[:], dO, waits=[t_h1])

        for qr2 in range(2):
            for nt in range(2):
                for kc in range(NFF):
                    hf = kc % 2
                    w = [t_h1]
                    wb = wbf0 if hf == 0 else wbf1
                    w.append(WPC["w2"])
                    if RG["wbf"][hf] is not None:
                        w.append((sT, RG["wbf"][hf]))
                    t_wb = dma("sync", wb[:, 0:512],
                               w2_bf[kc * 128:(kc + 1) * 128, nt * 512:(nt + 1) * 512],
                               dWo0 if hf == 0 else dWo1, waits=w)
                    for tc4 in range(4):
                        t8 = qr2 * 4 + tc4

                        def _mm(e, kc=kc, t8=t8, tc4=tc4, wb=wb):
                            return e.matmul(
                                PB[tc4][:, 0:512],
                                h1h[:, kc * 1024 + t8 * 128:kc * 1024 + (t8 + 1) * 128],
                                wb[:, 0:512], start=(kc == 0), stop=False)
                        w3 = [t_wb]
                        if kc == 0 and w2_pb[tc4] is not None:
                            w3.append((sV, w2_pb[tc4]))
                        if kc == 0 and w2_pb[tc4] is None and wo_pb[tc4] is not None:
                            w3.append((sV, wo_pb[tc4]))
                        on("tensor", _mm, waits=w3, inc=sT)
                        RG["wbf"][hf] = sT.n
                        if kc == NFF - 1:
                            def _bm(e, tc4=tc4, nt=nt):
                                return e.matmul(PB[tc4][:, 0:512], onesr[:],
                                                b2_row[0:1, nt * 512:(nt + 1) * 512],
                                                start=False, stop=True)
                            on("tensor", _bm, waits=[t_b2row], inc=sT)
                            t_mm = sT.tok
                            tc = half * 8 + qr2 * 4 + tc4
                            on("vector", lambda e, tc=tc, nt=nt, tc4=tc4:
                               e.tensor_tensor(
                                   xnO[:, tc * D + nt * 512:tc * D + (nt + 1) * 512],
                                   xnO[:, tc * D + nt * 512:tc * D + (nt + 1) * 512],
                                   PB[tc4][:, 0:512], op=ALU.add),
                               waits=[t_mm, DT.get("xa1")], inc=sV, drain=True)
                            w2_pb[tc4] = sV.n
    t_xa2 = sV.tok
    dump("xa2", xnO[:], [t_xa2])

    f3_drain()
    t_s5 = F3["t_s5"]
    t_dn_prev = [None, None]
    dnres_t = []
    for kc in range(16):
        dnres_t.append(dma("sync", z[:, kc * 1024:(kc + 1) * 1024],
                           wdn_rp[kc], dDn0 if kc % 2 == 0 else dDn1,
                           waits=[t_xa2, t_rpk, DT.get("y")]))
    for kc in range(16, 22):
        dnres_t.append(dma("sync", dnr[kc - 16][:],
                           wdn_rp[kc], dDn0 if kc % 2 == 0 else dDn1,
                           waits=[t_xa2, t_rpk, t_s5]))
    dn_pb = [None] * 4
    for tg in range(NG):
        oq_buf = oq_tg if tg % 2 == 0 else oq_tgB
        oq_v = [None, None]
        for ic in range(NFF):
            hf = ic % 2
            ob = wbf0[:, 0:512] if hf == 0 else wbf1[:, 0:512]
            w = [t_s5, t_xa2 if tg == 0 else None, t_ofin]
            if tg == 0 and ic < 2:
                w += [DT.get("sloc"), DT.get("hin"),
                      (dF3s0, dF3s0.n), (dF3s1, dF3s1.n)]
            if oq_v[hf] is not None:
                w.append((sS, oq_v[hf]))
            else:
                w.append(DT.get("oq"))
            vld = dma("scalar", ob, sp_or[ic, :, tg * 512:(tg + 1) * 512],
                      dOq0 if hf == 0 else dOq1, waits=w)
            f0 = wstage[:, 0:512] if hf == 0 else wstage[:, 512:1024]
            on("scalar", lambda e, ob=ob, f0=f0: e.activation(
                f0, ob, AF.Copy, bias=MAGIC, scale=ccol[:, 11:12]),
               waits=[vld], inc=sS)
            wq_ = []
            if ic == 0:
                if t_dn_prev[tg % 2] is not None:
                    wq_.append(t_dn_prev[tg % 2])
                elif tg == 1:
                    wq_.append(t_xa2)
            on("scalar", lambda e, f0=f0, ic=ic, oq_buf=oq_buf: e.activation(
                oq_buf[:, ic * G:(ic + 1) * G], f0, AF.Identity,
                bias=ccol[:, 15:16], scale=ccol[:, 10:11]),
               waits=wq_, inc=sS, drain=True)
            oq_v[hf] = sS.n
        t_oq = sS.tok
        if "oq" in dbg:
            DT["oq"] = dma("sync", DOUT["oq"][tg], oq_buf[:], dO, waits=[t_oq])

        for nt in range(2):
            for kc in range(NFF):
                hf = kc % 2
                if kc < 16:
                    wb_ap = z[:, kc * 1024 + nt * 512:
                              kc * 1024 + (nt + 1) * 512]
                    t_wb = dnres_t[kc]
                elif kc < 22:
                    wb_ap = dnr[kc - 16][:, nt * 512:(nt + 1) * 512]
                    t_wb = dnres_t[kc]
                else:
                    wb = wbf2 if hf == 0 else wbf3
                    w = [t_rpk]
                    if tg == 0:
                        w.append(t_oq)
                    if RG["wbf"][hf] is not None:
                        w.append((sT, RG["wbf"][hf]))
                    t_wb = dma("sync", wb[:, 0:512],
                               wdn_rp[kc, :, nt * 512:(nt + 1) * 512],
                               dDn0 if hf == 0 else dDn1, waits=w)
                    wb_ap = wb[:, 0:512]
                for tc4 in range(4):
                    tc = tg * 4 + tc4

                    def _mm(e, kc=kc, tc4=tc4, wb_ap=wb_ap, oq_buf=oq_buf):
                        return e.matmul(
                            PB[tc4][:, 0:512],
                            oq_buf[:, kc * G + tc4 * 128:kc * G + (tc4 + 1) * 128],
                            wb_ap, start=(kc == 0), stop=(kc == NFF - 1))
                    w3 = [t_wb]
                    if kc == 0:
                        w3.append(t_oq)
                    if kc == 0 and dn_pb[tc4] is not None:
                        w3.append((sV, dn_pb[tc4]))
                    elif kc == 0:
                        w3.append(t_xa2)
                    on("tensor", _mm, waits=w3, inc=sT)
                    RG["wbf"][hf] = sT.n
                    if kc == NFF - 1:
                        t_mm = sT.tok
                        tc = tg * 4 + tc4
                        xb = xldC0 if tc % 2 == 0 else xldC1
                        w4 = [t_xa2]
                        if RG["xld"][tc % 2] is not None:
                            w4.append((sV, RG["xld"][tc % 2]))
                        vx = dma("sync", xb[:], x_h[tc * 128:(tc + 1) * 128, :],
                                 dXc0 if tc % 2 == 0 else dXc1, waits=w4)
                        on("vector", lambda e, tc=tc, nt=nt, tc4=tc4, xb=xb:
                           e.tensor_tensor(
                               xnO[:, tc * D + nt * 512:tc * D + (nt + 1) * 512],
                               xnO[:, tc * D + nt * 512:tc * D + (nt + 1) * 512],
                               PB[tc4][:, 0:512], op=ALU.add),
                           waits=[t_mm, t_xa2, DT.get("xa2")], inc=sV,
                           drain=True)
                        on("vector", lambda e, tc=tc, nt=nt, xb=xb:
                           e.tensor_tensor(
                               xnO[:, tc * D + nt * 512:tc * D + (nt + 1) * 512],
                               xnO[:, tc * D + nt * 512:tc * D + (nt + 1) * 512],
                               xb[:, nt * 512:(nt + 1) * 512], op=ALU.add),
                           waits=[vx], inc=sV, drain=True)
                        dn_pb[tc4] = sV.n
                        RG["xld"][tc % 2] = sV.n
        t_dn_prev[tg % 2] = sT.tok
        for tc4 in range(4):
            tc = tg * 4 + tc4
            dma("sync", out_h[tc * 128:(tc + 1) * 128, :],
                xnO[:, tc * D:(tc + 1) * D], dO, waits=[(sV, sV.n)])

    on("sync", lambda e: None, waits=[dO.tok], inc=None)
    on("gpsimd", lambda e: None, waits=[dO.tok], inc=None)
    on("scalar", lambda e: None, waits=[dO.tok], inc=None)

    with nc.Block() as block:
        @block.sync
        def _(e):
            for f in PL["sync"]:
                f(e)

        @block.vector
        def _(e):
            for f in PL["vector"]:
                f(e)

        @block.scalar
        def _(e):
            for f in PL["scalar"]:
                f(e)

        @block.tensor
        def _(e):
            for f in PL["tensor"]:
                f(e)

        @block.gpsimd
        def _(e):
            for f in PL["gpsimd"]:
                f(e)
    stack.close()
    return nc, dbg_names


_PROG_CACHE = {}


def get_program(dbg=()):
    key = tuple(sorted(dbg))
    if key not in _PROG_CACHE:
        _PROG_CACHE[key] = build_program(dbg)
    return _PROG_CACHE[key]


def kernel(L0_attn_x, L0_attn_ln1_w, L0_attn_ln1_b, L0_attn_Wq, L0_attn_Wk,
           L0_attn_Wv, L0_attn_Wg, L0_attn_Wgk1, L0_attn_Wgk2, L0_attn_bgk2,
           L0_attn_rms_w, L0_attn_Wo, L0_attn_ln2_w, L0_attn_ln2_b, L0_attn_W1,
           L0_attn_b1, L0_attn_W2, L0_attn_b2, L0_ffn_w_gate, L0_ffn_w_up,
           L0_ffn_w_down, _dbg=()):
    f32 = np.float32
    x = np.ascontiguousarray(np.asarray(L0_attn_x, f32).reshape(NCORES * T, D))
    full = {
        "Wq": np.ascontiguousarray(np.asarray(L0_attn_Wq, f32)),
        "Wk": np.ascontiguousarray(np.asarray(L0_attn_Wk, f32)),
        "Wv": np.ascontiguousarray(np.asarray(L0_attn_Wv, f32)),
        "Wg": np.ascontiguousarray(np.asarray(L0_attn_Wg, f32)),
        "Wgk1": np.ascontiguousarray(np.asarray(L0_attn_Wgk1, f32)),
        "Wgk2": np.ascontiguousarray(np.asarray(L0_attn_Wgk2, f32)),
        "bgk2": np.ascontiguousarray(np.asarray(L0_attn_bgk2, f32)),
        "ln1w": np.ascontiguousarray(np.asarray(L0_attn_ln1_w, f32)),
        "ln1b": np.ascontiguousarray(np.asarray(L0_attn_ln1_b, f32)),
        "rmsw": np.ascontiguousarray(np.asarray(L0_attn_rms_w, f32)),
        "Wo": np.ascontiguousarray(np.asarray(L0_attn_Wo, f32)),
        "ln2w": np.ascontiguousarray(np.asarray(L0_attn_ln2_w, f32)),
        "ln2b": np.ascontiguousarray(np.asarray(L0_attn_ln2_b, f32)),
        "W1": np.ascontiguousarray(np.asarray(L0_attn_W1, f32)),
        "b1": np.ascontiguousarray(np.asarray(L0_attn_b1, f32)),
        "W2": np.ascontiguousarray(np.asarray(L0_attn_W2, f32)),
        "b2": np.ascontiguousarray(np.asarray(L0_attn_b2, f32)),
    }
    w_up = np.asarray(L0_ffn_w_up, f32)
    w_gt = np.asarray(L0_ffn_w_gate, f32)
    w_dn = np.asarray(L0_ffn_w_down, f32)

    nc, dbg_names = get_program(tuple(_dbg))
    in_maps = []
    for c in range(NCORES):
        m = dict(full)
        m["x"] = np.ascontiguousarray(x[c * T:(c + 1) * T])
        m["wup_sl"] = np.ascontiguousarray(w_up[:, c * 128:(c + 1) * 128])
        m["wgt_sl"] = np.ascontiguousarray(w_gt[:, c * 128:(c + 1) * 128])
        m["wdn_sl"] = np.ascontiguousarray(w_dn[c * 128:(c + 1) * 128, :])
        cm = np.zeros((128, NCORES), f32)
        cm[:, :c] = 1.0
        m["cmask"] = cm
        in_maps.append(m)
    res = run_bass_kernel_spmd(nc, in_maps, core_ids=list(range(NCORES)))
    outs = [np.asarray(r["out"]) for r in res.results]
    full_out = np.concatenate(outs, axis=0).reshape(1, NCORES * T, D)
    if _dbg:
        dbg_res = [{k: np.asarray(r[k]) for k in dbg_names}
                   for r in res.results]
        return full_out, dbg_res
    return full_out


# version 33
# speedup vs baseline: 1.0187x; 1.0187x over previous
from contextlib import ExitStack

import numpy as np
import ml_dtypes

import concourse.bass as bass
import concourse.mybir as mybir
from concourse.bass_utils import run_bass_kernel_spmd

F32 = mybir.dt.float32
BF16 = mybir.dt.bfloat16
AF = mybir.ActivationFunctionType
ALU = mybir.AluOpType
AX = mybir.AxisListType.X

NCORES = 8
T = 2048
D = 1024
H = 4
DK = DV = 256
C = 128
NCH = T // C
G = 512
NG = T // G
CPG = G // C
FF = 3072
NFC = D // 128
NFF = FF // 128
NTT = T // 128
MAGIC = float(np.float32(3 * 2.0 ** 22))
LN16 = float(np.log(1.0 / 16.0))
EPS = 1e-5

ENGINES = ("sync", "vector", "scalar", "tensor", "gpsimd")


class Sem:
    def __init__(self, nc, stack, name):
        self.h = stack.enter_context(nc.semaphore(name))
        self.n = 0

    def inc(self, k=1):
        self.n += k
        return self.n

    @property
    def tok(self):
        return (self, self.n)


def build_program(dbg=()):
    dbg = set(dbg)
    nc = bass.Bass("TRN2", target_bir_lowering=False)
    stack = ExitStack()
    PL = {e: [] for e in ENGINES}

    def on(eng, fn, waits=(), inc=None, k=1, drain=False):
        waits = tuple(w for w in waits if w is not None)
        for (s, v) in waits:
            assert isinstance(s, Sem) and isinstance(v, int), f"bad wait {s} {v}"
        if inc is not None:
            inc.inc(k)

        def f(e, fn=fn, waits=waits, inc=inc, k=k, drain=drain):
            for (s, v) in waits:
                e.wait_ge(s.h, v)
            if drain:
                e.drain()
            last = fn(e)
            if inc is not None:
                if last is None:
                    e.nop().then_inc(inc.h, k)
                else:
                    last.then_inc(inc.h, k)
        PL[eng].append(f)
        return inc.n if inc is not None else None

    def dma(q, out_ap, in_ap, sem, waits=(), **kw):
        self_wait = sem.n
        v = sem.inc(16)
        waits = tuple(w for w in waits if w is not None) + ((sem, self_wait),)
        for (s, vv) in waits:
            assert isinstance(s, Sem) and isinstance(vv, int), f"bad dma wait {s} {vv}"

        def f(e, out_ap=out_ap, in_ap=in_ap, kw=kw, sem=sem, waits=waits):
            for (s, vv) in waits:
                e.wait_ge(s.h, vv)
            e.dma_start(out=out_ap, in_=in_ap, **kw).then_inc(sem.h, 16)
        PL[q].append(f)
        return (sem, v)

    def din(name, shape, dt=F32):
        return nc.dram_tensor(name, list(shape), dt, kind="ExternalInput")

    x_h = din("x", [T, D]).ap()
    Wq_h = din("Wq", [D, D]).ap()
    Wk_h = din("Wk", [D, D]).ap()
    Wv_h = din("Wv", [D, D]).ap()
    Wg_h = din("Wg", [D, D]).ap()
    Wgk1_h = din("Wgk1", [D, 16]).ap()
    Wgk2_h = din("Wgk2", [16, D]).ap()
    bgk2_h = din("bgk2", [D]).ap()
    ln1w_h = din("ln1w", [D]).ap()
    ln1b_h = din("ln1b", [D]).ap()
    rmsw_h = din("rmsw", [DV]).ap()
    Wo_h = din("Wo", [D, D]).ap()
    ln2w_h = din("ln2w", [D]).ap()
    ln2b_h = din("ln2b", [D]).ap()
    W1_h = din("W1", [D, FF]).ap()
    b1_h = din("b1", [FF]).ap()
    W2_h = din("W2", [FF, D]).ap()
    b2_h = din("b2", [D]).ap()
    wup_sl_h = din("wup_sl", [FF, 128]).ap()
    wgt_sl_h = din("wgt_sl", [FF, 128]).ap()
    wdn_sl_h = din("wdn_sl", [128, FF]).ap()
    cmask_h = din("cmask", [128, NCORES]).ap()

    out_h = nc.dram_tensor("out", [T, D], F32, kind="ExternalOutput").ap()

    def idram(name, shape, dt=F32):
        return nc.dram_tensor(name, list(shape), dt)

    xn_sp_t = idram("xn_sp", [NTT, 128, D])
    wq_bf_t = idram("wq_bf", [NFC, NFC, 128, 128], BF16)
    wk_bf_t = idram("wk_bf", [NFC, NFC, 128, 128], BF16)
    wv_bf_t = idram("wv_bf", [NFC, NFC, 128, 128], BF16)
    wg_bf_t = idram("wg_bf", [NFC, NFC, 128, 128], BF16)
    w1_bf_t = idram("w1_bf", [NFF, NFC, 128, 128], BF16)
    wo_bf_t = idram("wo_bf", [D, D], BF16)
    w2_bf_t = idram("w2_bf", [FF, D], BF16)
    qt_sp_t = idram("qt_sp", [NG, NFC, 128, G], BF16)
    xq_sp_t = idram("xq_sp", [NFC, 128, T], BF16)
    sp_up_t = idram("sp_up", [NFF, 128, T], BF16)
    sp_gt_t = idram("sp_gt", [NFF, 128, T], BF16)
    sp_pr_t = idram("sp_pr", [NFF, 128, T], BF16)
    sp_or_t = idram("sp_or", [NFF, 128, T], BF16)
    wag_i_t = idram("wag_i", [2, NFF, 128, 128], BF16)
    wag_o_t = idram("wag_o", [NCORES, 2, NFF, 128, 128], BF16)
    wagd_i_t = idram("wagd_i", [NFF, 128, 128], BF16)
    wagd_o_t = idram("wagd_o", [NCORES, NFF, 128, 128], BF16)
    wdn_rp_t = idram("wdn_rp", [NFF, 128, D], BF16)
    mx_i_t = idram("mx_i", [1, 128])
    mx_o_t = idram("mx_o", [1, 128])
    mx2_i_t = idram("mx2_i", [1, 256])
    mx2_o_t = idram("mx2_o", [1, 256])
    mx4_i_t = idram("mx4_i", [1, 128])
    mx4_o_t = idram("mx4_o", [1, 128])
    mx5_i_t = idram("mx5_i", [1, 128])
    mx5_o_t = idram("mx5_o", [1, 128])
    AGW = NFC * DV + NFC
    ag_i_t = idram("ag_i", [128, AGW])
    ag_o_t = idram("ag_o", [NCORES, 128, AGW])
    xn_sp = xn_sp_t.ap()
    xq_sp = xq_sp_t.ap()
    wq_bf = wq_bf_t.ap()
    wk_bf = wk_bf_t.ap()
    wv_bf = wv_bf_t.ap()
    wg_bf = wg_bf_t.ap()
    w1_bf = w1_bf_t.ap()
    wo_bf = wo_bf_t.ap()
    w2_bf = w2_bf_t.ap()
    qt_sp = qt_sp_t.ap()
    sp_up = sp_up_t.ap()
    sp_gt = sp_gt_t.ap()
    sp_pr = sp_pr_t.ap()
    sp_or = sp_or_t.ap()
    wag_i = wag_i_t.ap()
    wag_o = wag_o_t.ap()
    wagd_i = wagd_i_t.ap()
    wagd_o = wagd_o_t.ap()
    wdn_rp = wdn_rp_t.ap()
    ag_i = ag_i_t.ap()
    ag_o = ag_o_t.ap()

    DOUT = {}
    dbg_names = []

    def dbg_decl(name, shape, dt=F32):
        if name in dbg:
            DOUT[name] = nc.dram_tensor("dbg_" + name, list(shape), dt,
                                        kind="ExternalOutput").ap()
            dbg_names.append("dbg_" + name)

    dbg_decl("xn", [128, NTT * D])
    dbg_decl("z", [128, NFC * T], BF16)
    dbg_decl("cs", [NG, 128, NFC * G])
    dbg_decl("kt", [NG, 128, NFC * G], BF16)
    dbg_decl("vt", [NG, 128, NFC * G], BF16)
    dbg_decl("qt", [128, NFC * T], BF16)
    dbg_decl("oloc", [128, NTT * D])
    dbg_decl("sloc", [128, NFC * DV])
    dbg_decl("hin", [128, NFC * DV])
    dbg_decl("ofin", [128, NTT * D])
    dbg_decl("gated", [128, NFC * T], BF16)
    dbg_decl("xa1", [128, NTT * D])
    dbg_decl("y", [128, NFC * T], BF16)
    dbg_decl("xq", [128, NFC * T], BF16)
    dbg_decl("expbc", [128, NFC * NCH])
    dbg_decl("etab", [128, NFC * NCH])
    dbg_decl("scales", [1, 16])
    dbg_decl("h1", [2, 128, NFF * 1024], BF16)
    dbg_decl("xa2", [128, NTT * D])
    dbg_decl("oq", [NG, 128, NFF * G], BF16)

    SB0 = (nc.sbuf_base + 31) // 32 * 32
    cur = [SB0]

    ALOC = {}

    def alloc(name, shape, dt, at=None):
        nbytes = int(np.prod(shape[1:])) * mybir.dt.size(dt)
        if at is None:
            off = cur[0]
            cur[0] += (nbytes + 31) // 32 * 32
        else:
            off = at
        ALOC[name] = off
        assert off + nbytes <= nc.SBUF_PARTITION_SIZE_BYTES, \
            f"SBUF overflow {name}: {off}+{nbytes}>{nc.SBUF_PARTITION_SIZE_BYTES}"
        return nc.alloc_sbuf_tensor_at(name, list(shape), dt, offset=off)

    xnO = alloc("xnO", [128, NTT * D], F32)
    xqs0 = alloc("xqs0", [128, NFC * 512], BF16, at=SB0 + 12 * 4096)
    xqs1 = alloc("xqs1", [128, NFC * 512], BF16, at=SB0 + 14 * 4096)
    z = alloc("z", [128, NFC * T], BF16)
    R0 = cur[0]
    qtr = alloc("qtr", [128, 2 * NFC * G], BF16, at=R0)
    xldA0 = alloc("xldA0", [128, D], F32, at=R0)
    xldA1 = alloc("xldA1", [128, D], F32, at=R0 + 4096)
    ztA0 = alloc("ztA0", [128, D], BF16, at=R0 + 8192)
    ztA1 = alloc("ztA1", [128, D], BF16, at=R0 + 10240)
    f1s0 = alloc("f1s0", [128, 512], F32, at=R0 + 12288)
    f1s1 = alloc("f1s1", [128, 512], F32, at=R0 + 14336)
    Sjb_ = alloc("Sjb", [128, AGW], F32, at=R0)
    xldC0 = alloc("xldC0", [128, D], F32, at=R0)
    xldC1 = alloc("xldC1", [128, D], F32, at=R0 + 4096)
    cur[0] = R0 + 16384
    M0 = cur[0]
    xq = alloc("xq", [128, NFC * T], BF16, at=M0)
    cs_g = alloc("cs_g", [128, NFC * G], F32, at=M0)
    kt_g = alloc("kt_g", [128, NFC * G], BF16, at=M0 + 16384)
    vt_g = alloc("vt_g", [128, NFC * G], BF16, at=M0 + 24576)
    gated = alloc("gated", [128, NFC * T], BF16, at=M0)
    SjbB_ = alloc("SjbB", [128, AGW], F32, at=M0)
    cur[0] = M0 + 32768
    h1h = alloc("h1h", [128, NFF * 1024], BF16, at=R0)
    N0 = cur[0]
    wdq = alloc("wdq", [128, FF], F32, at=N0)
    wdqb = alloc("wdqb", [128, FF], BF16, at=N0 + 12288)
    blkm = alloc("blkm", [128, 192], F32, at=N0 + 18432)
    h_f = alloc("h_f", [128, NFC * DV], F32, at=N0)
    h_b = alloc("h_b", [128, NFC * DV], BF16, at=N0 + 8192)
    hin = alloc("hin", [128, NFC * DV], F32, at=N0 + 12288)
    hinb = alloc("hinb", [128, NFC * DV], BF16, at=N0 + 20480)
    gsc0 = alloc("gsc0", [128, 512], F32, at=N0)
    gsc1 = alloc("gsc1", [128, 512], F32, at=N0 + 2048)
    gfm_q = alloc("gfm_q", [128, NFC * 512], BF16, at=N0 + 12288)
    ztB0 = alloc("ztB0", [128, D], BF16, at=N0 + 20480)
    ztB1 = alloc("ztB1", [128, D], BF16, at=N0 + 22528)
    xldB0 = alloc("xldB0", [128, D], F32, at=N0)
    xldB1 = alloc("xldB1", [128, D], F32, at=N0 + 4096)
    fscW0 = alloc("fscW0", [128, 512], F32, at=N0)
    fscW1 = alloc("fscW1", [128, 512], F32, at=N0 + 2048)
    ztC0 = alloc("ztC0", [128, D], BF16, at=N0 + 4096)
    ztC1 = alloc("ztC1", [128, D], BF16, at=N0 + 6144)
    fsc0 = alloc("fsc0", [128, 512], F32, at=N0)
    fsc1 = alloc("fsc1", [128, 512], F32, at=N0 + 2048)
    fsc2 = alloc("fsc2", [128, 512], F32, at=N0 + 4096)
    fsc3 = alloc("fsc3", [128, 512], F32, at=N0 + 6144)
    fbf0 = alloc("fbf0", [128, 512], BF16, at=N0 + 8192)
    fbf1 = alloc("fbf1", [128, 512], BF16, at=N0 + 9216)
    fbf2 = alloc("fbf2", [128, 512], BF16, at=N0 + 10240)
    fbf3 = alloc("fbf3", [128, 512], BF16, at=N0 + 11264)
    fbf4 = alloc("fbf4", [128, 512], BF16, at=N0 + 12288)
    fbf5 = alloc("fbf5", [128, 512], BF16, at=N0 + 13312)
    oq_tg = alloc("oq_tg", [128, NFF * G], BF16, at=N0)
    oq_tgB = alloc("oq_tgB", [128, NFF * G], BF16, at=M0)
    cur[0] = N0 + 24576
    ident = alloc("ident", [128, 128], BF16)
    utmask = alloc("utmask", [128, 128], F32)
    iotaf = alloc("iotaf", [128, 128], F32)
    wstage = alloc("wstage", [128, 1024], F32)
    wbf0 = alloc("wbf0", [128, 1024], BF16)
    wbf1 = alloc("wbf1", [128, 1024], BF16)
    wbf2 = alloc("wbf2", [128, 1024], BF16)
    wbf3 = alloc("wbf3", [128, 1024], BF16)
    ktT0 = alloc("ktT0", [128, D], BF16)
    vtT0 = alloc("vtT0", [128, D], BF16)
    atm0 = alloc("atm0", [128, C], BF16)
    atm1 = alloc("atm1", [128, C], BF16)
    SCA = alloc("SCA", [128, 512], F32)
    SCB = alloc("SCB", [128, 512], F32)
    rmsrep = alloc("rmsrep", [128, D], F32, at=None)
    u1b = alloc("u1b", [16, 1024], BF16)
    wgk1b = alloc("wgk1b", [128, NFC * 16], BF16)
    wgk2b = alloc("wgk2b", [16, D], BF16)
    expbc = alloc("expbc", [128, NFC * NCH], F32)
    etab = alloc("etab", [128, NFC * NCH], F32)
    stat6 = alloc("stat6", [128, 24], F32)
    stat2 = alloc("stat2", [128, 8], F32)
    mcol = alloc("mcol", [128, 8], F32)
    ccol = alloc("ccol", [128, 16], F32)
    dcol = alloc("dcol", [128, 16], F32)
    ecol = alloc("ecol", [128, 8], F32)
    cmk = alloc("cmk", [128, NCORES], F32)
    cmki = alloc("cmki", [128, NCORES], F32)
    ln1w_c = alloc("ln1w_c", [128, NFC], F32)
    ln1b_c = alloc("ln1b_c", [128, NFC], F32)
    ln2w_c = alloc("ln2w_c", [128, NFC], F32)
    ln2b_c = alloc("ln2b_c", [128, NFC], F32)
    bgk2_c = alloc("bgk2_c", [128, NFC], F32)
    b1_c = alloc("b1_c", [128, NFF], F32)
    b1w_c = alloc("b1w_c", [128, NFF], F32)
    bq_c = alloc("bq_c", [128, NFC], F32)
    bk_c = alloc("bk_c", [128, NFC], F32)
    bv_c = alloc("bv_c", [128, NFC], F32)
    bg_c = alloc("bg_c", [128, NFC], F32)
    bgk1_c = alloc("bgk1_c", [16, 1], F32)
    rowbank = alloc("rowbank", [1, D], F32)
    onesr = alloc("onesr", [1, 128], F32)
    srow = alloc("srow", [1, 16], F32)
    dmax = alloc("dmax", [1, 256], F32)
    rcols = alloc("rcols", [128, 8], F32)
    rst16 = alloc("rst16", [128, 64], F32)
    ln1b_cb = alloc("ln1b_cb", [128, NFC], BF16)
    ln2b_cb = alloc("ln2b_cb", [128, NFC], BF16)
    f3mx = alloc("f3mx", [128, 192], F32)
    dnr = [alloc("dnr0", [128, 1024], BF16, at=ALOC["rmsrep"]),
           alloc("dnr1", [128, 1024], BF16, at=ALOC["rmsrep"] + 2048),
           alloc("dnr2", [128, 1024], BF16, at=ALOC["ktT0"]),
           alloc("dnr3", [128, 1024], BF16, at=ALOC["vtT0"])]

    assert cur[0] <= nc.SBUF_PARTITION_SIZE_BYTES, f"SBUF total {cur[0] - SB0}"

    b2_row = rowbank[0:1, 0:D]

    PB = [stack.enter_context(nc.psum_tensor(f"pb{i}", [128, 512], F32))
          for i in range(7)]
    PBF = stack.enter_context(nc.psum_tensor("pbf", [128, 1024], BF16))

    sems = {}

    def S_(name):
        if name not in sems:
            sems[name] = Sem(nc, stack, name)
        return sems[name]

    dO = S_("dO")
    cc = S_("cc")
    sV = S_("sV")
    sS = S_("sS")
    sT = S_("sT")
    sG = S_("sG")
    dSm = S_("dSm")
    dG = S_("dG")
    dDq = S_("dDq")
    dWg0 = S_("dWg0"); dWg1 = S_("dWg1")
    dRp = S_("dRp")
    dXl0 = S_("dXl0"); dXl1 = S_("dXl1")
    dMxg = S_("dMxg")
    dMxr = S_("dMxr")
    dSp0 = S_("dSp0"); dSp1 = S_("dSp1")
    dXq0 = S_("dXq0"); dXq1 = S_("dXq1")
    dXqs0 = S_("dXqs0"); dXqs1 = S_("dXqs1")
    dF1l0 = S_("dF1l0"); dF1l1 = S_("dF1l1")
    dF1s0 = S_("dF1s0"); dF1s1 = S_("dF1s1")
    dWc0 = S_("dWc0"); dWc1 = S_("dWc1")
    dQs0 = S_("dQs0"); dQs1 = S_("dQs1")
    dQr0 = S_("dQr0"); dQr1 = S_("dQr1")
    dAg = S_("dAg")
    dSj = S_("dSj")
    dSj2 = S_("dSj2")
    dWo0 = S_("dWo0"); dWo1 = S_("dWo1")
    dXn0 = S_("dXn0"); dXn1 = S_("dXn1")
    dF3l0 = S_("dF3l0"); dF3l1 = S_("dF3l1")
    dF3l2 = S_("dF3l2"); dF3l3 = S_("dF3l3")
    dF3s0 = S_("dF3s0"); dF3s1 = S_("dF3s1")
    dOq0 = S_("dOq0"); dOq1 = S_("dOq1")
    dDn0 = S_("dDn0"); dDn1 = S_("dDn1")
    dXc0 = S_("dXc0"); dXc1 = S_("dXc1")

    DT = {}

    def dump(name, src_ap, waits=()):
        if name in dbg:
            DT[name] = dma("sync", DOUT[name], src_ap, dO, waits=waits)

    RG = {
        "wstg": [None, None], "wbf": [None, None], "xld": [None, None],
        "gks": [None, None], "pbf": None, "zt": [None, None],
    }

    on("gpsimd", lambda e: e.iota(iotaf[:], pattern=[[1, 128]], base=0,
                                  channel_multiplier=-1,
                                  allow_small_or_imprecise_dtypes=True), inc=sG)
    t_iota = sG.tok
    on("vector", lambda e: e.tensor_scalar(ident[:], iotaf[:], 0.0, None,
                                           op0=ALU.is_equal),
       waits=[t_iota], inc=sV)
    on("vector", lambda e: e.tensor_scalar(utmask[:], iotaf[:], 0.0, None,
                                           op0=ALU.is_ge), inc=sV)
    on("vector", lambda e: e.memset(onesr[:], 1.0), inc=sV)
    on("vector", lambda e: e.memset(ecol[:, 0:1], EPS), inc=sV)
    on("vector", lambda e: e.memset(ecol[:, 1:2], LN16), inc=sV)
    on("vector", lambda e: e.memset(ecol[:, 2:3], 0.0), inc=sV)
    on("vector", lambda e: e.memset(mcol[:], 0.0), inc=sV)
    on("vector", lambda e: e.memset(srow[:], 0.0), inc=sV)
    on("vector", lambda e: e.memset(ecol[:, 3:4], 1.0), inc=sV)
    on("vector", lambda e: e.memset(ecol[:, 4:5], -MAGIC), inc=sV)
    t_consts = sV.tok

    for (dst, src) in [(ln1w_c, ln1w_h), (ln1b_c, ln1b_h), (ln2w_c, ln2w_h),
                       (ln2b_c, ln2b_h), (bgk2_c, bgk2_h)]:
        dma("gpsimd", dst[:], src.rearrange("(a b) -> b a", b=128), dG,
            allow_slow_non_contiguous=True)
    dma("gpsimd", b1_c[:], b1_h.rearrange("(a b) -> b a", b=128), dG,
        allow_slow_non_contiguous=True)
    t_gcols = dG.tok
    WPC = {}
    for nm, (w_h, w_bf, nmc) in {
            "q": (Wq_h, wq_bf, NFC), "k": (Wk_h, wk_bf, NFC),
            "v": (Wv_h, wv_bf, NFC), "g": (Wg_h, wg_bf, NFC),
            "w1": (W1_h, w1_bf, NFF)}.items():
        srcr = w_h.rearrange("(kc p) (mc c) -> mc kc p c", p=128, c=128)
        sem_a = S_("dPC" + nm)
        sem_b = S_("dPD" + nm)
        for mc_ in range(nmc):
            WPC[nm] = dma("gpsimd", w_bf[mc_], srcr[mc_],
                          sem_a if mc_ % 2 == 0 else sem_b)
        WPC[nm + "2"] = (sem_a if (nmc - 2) % 2 == 0 else sem_b,
                         WPC[nm][1] if False else None)
        WPC[nm] = ((sem_a, sem_a.n), (sem_b, sem_b.n))
    WPC["wo"] = dma("gpsimd", wo_bf, Wo_h, S_("dPCwo"))
    WPC["w2"] = dma("gpsimd", w2_bf, W2_h, S_("dPCw2"))

    dma("sync", cmk[:], cmask_h, dSm)
    dma("sync", wstage[0:16, 0:D], Wgk2_h, dSm)
    t_small = dSm.tok

    on("vector", lambda e: e.tensor_scalar(cmki[:], cmk[:], -1.0, 1.0,
                                           op0=ALU.mult, op1=ALU.add),
       waits=[t_small], inc=sV)
    on("vector", lambda e: e.tensor_copy(ln1b_cb[:], ln1b_c[:]),
       waits=[t_gcols], inc=sV)
    on("vector", lambda e: e.tensor_copy(ln2b_cb[:], ln2b_c[:]),
       waits=[t_gcols], inc=sV)
    on("vector", lambda e: e.tensor_scalar(bgk2_c[:], bgk2_c[:], -1.0, None,
                                           op0=ALU.mult),
       waits=[t_gcols], inc=sV, drain=True)
    t_gcolsb = sV.tok
    on("vector", lambda e: e.tensor_copy(wgk2b[:], wstage[0:16, 0:D]),
       waits=[t_small], inc=sV)

    vgk1 = dma("sync", SCA[:, 0:128].rearrange("p (a l) -> p a l", a=NFC),
               Wgk1_h.rearrange("(a p) l -> p a l", p=128), dSm)

    def _wgk1(e):
        return e.tensor_copy(wgk1b[:], SCA[:, 0:128])
    on("vector", _wgk1, waits=[vgk1, t_gcols], inc=sV)
    t_wgk = sV.tok
    t_masks = sV.tok


    def plan_dequant(src_ap, t_prev):
        dst = wdq[:, 0:FF]
        if len(src_ap.shape) == 3:
            dst = dst.rearrange("p (n h) -> p n h", n=NFF)
        t_ld = dma("sync", dst, src_ap, dDq, waits=list(t_prev))
        on("vector", lambda e: e.tensor_reduce(
            blkm[:, 0:96], wdq[:, 0:FF].rearrange("p (b k) -> p b k", k=32),
            axis=AX, op=ALU.max, apply_absolute_value=True),
           waits=[t_ld], inc=sV)
        on("vector", lambda e: e.tensor_scalar(
            blkm[:, 0:96], blkm[:, 0:96], 1.0 / 7.0, 1e-12,
            op0=ALU.mult, op1=ALU.max), inc=sV, drain=True)
        on("vector", lambda e: e.reciprocal(blkm[:, 96:192], blkm[:, 0:96]),
           inc=sV, drain=True)

        def _q1(e):
            last = None
            for b in range(96):
                sl = wdq[:, b * 32:(b + 1) * 32]
                last = e.tensor_scalar(sl, sl, blkm[:, 96 + b:97 + b], MAGIC,
                                       op0=ALU.mult, op1=ALU.add)
            return last
        on("vector", _q1, inc=sV, drain=True)

        def _q2(e):
            last = None
            for b in range(96):
                last = e.tensor_scalar(wdqb[:, b * 32:(b + 1) * 32],
                                       wdq[:, b * 32:(b + 1) * 32],
                                       MAGIC, blkm[:, b:b + 1],
                                       op0=ALU.subtract, op1=ALU.mult)
            return last
        on("vector", _q2, inc=sV, drain=True)
        return sV.tok

    def plan_tr_to_wag(wag_idx, t_ready):
        dmav = [None, None]
        evacv = [None, None]
        for i in range(NFF):
            hf = i % 2
            buf = wbf0 if hf == 0 else wbf1
            dsem = dWg0 if hf == 0 else dWg1
            w = [t_ready, t_masks]
            if i < 2 and RG["pbf"]:
                w.append((sV, RG["pbf"]))
            for s_ in (0, 1):
                if evacv[s_] is not None:
                    w.append((sV, evacv[s_]))
            on("tensor", lambda e, i=i, hf=hf: e.transpose(
                PBF[:, hf * 512:hf * 512 + 128],
                wdqb[:, i * 128:(i + 1) * 128], ident[:]),
               waits=w, inc=sT)
            tt_ = sT.tok
            w2 = [tt_]
            if dmav[hf] is not None:
                w2.append(dmav[hf])
            on("vector", lambda e, hf=hf, buf=buf: e.tensor_copy(
                buf[:, 0:128], PBF[:, hf * 512:hf * 512 + 128]),
               waits=w2, inc=sV)
            evacv[hf] = sV.n
            dst_ = wagd_i[i] if wag_idx == 2 else wag_i[wag_idx, i]
            dmav[hf] = dma("sync", dst_, buf[:, 0:128], dsem,
                           waits=[sV.tok])
        RG["pbf"] = sV.n
        return (dWg0.tok, dWg1.tok)

    up_src = wup_sl_h.rearrange("(n p) h -> p n h", p=128)
    gt_src = wgt_sl_h.rearrange("(n p) h -> p n h", p=128)
    t_dq = plan_dequant(up_src, [t_consts])
    t_w = plan_tr_to_wag(0, t_dq)
    t_dq = plan_dequant(gt_src, list(t_w))
    t_w_gt = plan_tr_to_wag(1, t_dq)

    on("gpsimd", lambda e: e.collective_compute(
        "AllGather", ALU.bypass, replica_groups=[list(range(NCORES))],
        ins=[wag_i_t.ap().opt()], outs=[wag_o_t.ap().opt()]),
       waits=[t_w_gt[0], t_w_gt[1]], inc=cc)
    t_wag = cc.tok

    def plan_down_chain():
        t_dq = plan_dequant(wdn_sl_h, list(t_w_gt))
        t_w = plan_tr_to_wag(2, t_dq)
        on("gpsimd", lambda e: e.collective_compute(
            "AllGather", ALU.bypass, replica_groups=[list(range(NCORES))],
            ins=[wagd_i_t.ap().opt()], outs=[wagd_o_t.ap().opt()]),
           waits=[t_w[0], t_w[1]], inc=cc)
        return cc.tok

    def plan_repack(t_wagd):
        for ic in range(NFF):
            dma("gpsimd", wdn_rp[ic].rearrange("p (r c) -> p r c", r=NCORES),
                wagd_o[:, ic, :, :].rearrange("r p c -> p r c"),
                dRp, waits=[t_wagd])
        return dRp.tok

    for tt in range(NTT):
        buf = xldA0 if tt % 2 == 0 else xldA1
        dx = dXl0 if tt % 2 == 0 else dXl1
        w = [(sS, RG["xld"][tt % 2])] if RG["xld"][tt % 2] else []
        t_x = dma("sync" if tt % 2 == 0 else "scalar",
                  buf[:], x_h[tt * 128:(tt + 1) * 128, :], dx, waits=w)
        xsl = xnO[:, tt * D:(tt + 1) * D]

        on("vector", lambda e, buf=buf: e.bn_stats(stat6[:, 0:6], buf[:, 0:512]),
           waits=[t_x, (sS, sS.n)], inc=sV, drain=True)
        on("vector", lambda e, buf=buf: e.bn_stats(stat6[:, 6:12],
                                                   buf[:, 512:1024]), inc=sV)
        on("vector", lambda e: e.bn_aggr(stat2[:, 0:2], stat6[:, 0:12]),
           inc=sV, drain=True)
        on("scalar", lambda e: e.activation(dcol[:, 0:1], stat2[:, 1:2], AF.Ln,
                                            bias=ecol[:, 0:1], scale=1.0),
           waits=[sV.tok], inc=sS)
        on("scalar", lambda e: e.activation(dcol[:, 1:2], dcol[:, 0:1], AF.Exp,
                                            bias=ecol[:, 2:3], scale=-0.5),
           inc=sS, drain=True)
        on("vector", lambda e: e.scalar_tensor_tensor(
            dcol[:, 2:3], stat2[:, 0:1], -1.0, dcol[:, 1:2],
            op0=ALU.mult, op1=ALU.mult), waits=[sS.tok], inc=sV, drain=True)
        on("scalar", lambda e, buf=buf, xsl=xsl: e.activation(
            xsl, buf[:], AF.Identity, bias=dcol[:, 2:3], scale=dcol[:, 1:2]),
           waits=[sV.tok], inc=sS)
        RG["xld"][tt % 2] = sS.n
        t_xn = sS.tok
        on("vector", lambda e, xsl=xsl: e.tensor_reduce(
            dcol[:, 3:4], xsl, axis=AX, op=ALU.max, apply_absolute_value=True),
           waits=[t_xn], inc=sV, drain=True)
        on("vector", lambda e: e.tensor_tensor(mcol[:, 0:1], mcol[:, 0:1],
                                               dcol[:, 3:4], op=ALU.max),
           inc=sV, drain=True)
        on("vector", lambda e, xsl=xsl: e.bn_stats(stat6[:, 12:18],
                                                   xsl[:, 0:512]),
           waits=[t_xn], inc=sV, drain=True)
        on("vector", lambda e, xsl=xsl: e.bn_stats(stat6[:, 18:24],
                                                   xsl[:, 512:1024]), inc=sV)
        on("vector", lambda e: e.bn_aggr(stat2[:, 2:4], stat6[:, 12:24]),
           inc=sV, drain=True)
        on("scalar", lambda e: e.activation(dcol[:, 4:5], stat2[:, 3:4], AF.Ln,
                                            bias=ecol[:, 0:1], scale=1.0),
           waits=[sV.tok], inc=sS)
        on("scalar", lambda e: e.activation(dcol[:, 5:6], dcol[:, 4:5], AF.Exp,
                                            bias=ecol[:, 2:3], scale=-0.5),
           inc=sS, drain=True)
        on("vector", lambda e: e.scalar_tensor_tensor(
            dcol[:, 6:7], stat2[:, 2:3], -1.0, dcol[:, 5:6],
            op0=ALU.mult, op1=ALU.mult), waits=[sS.tok], inc=sV, drain=True)
        zb = ztA0 if tt % 2 == 0 else ztA1
        w = [sV.tok]
        if RG["zt"][tt % 2]:
            w.append((sV, RG["zt"][tt % 2]))
        on("scalar", lambda e, zb=zb, xsl=xsl: e.activation(
            zb[:], xsl, AF.Identity, bias=dcol[:, 6:7], scale=dcol[:, 5:6]),
           waits=w, inc=sS)
        t_z = sS.tok
        w = [t_z, t_masks]
        if RG["pbf"]:
            w.append((sV, RG["pbf"]))

        def _ztr(e, zb=zb):
            last = None
            for fc in range(NFC):
                last = e.transpose(PBF[:, fc * 128:(fc + 1) * 128],
                                   zb[:, fc * 128:(fc + 1) * 128], ident[:])
            return last
        on("tensor", _ztr, waits=w, inc=sT)
        zdst = z[:].rearrange("p (f t) -> p f t", f=NFC)[:, :, tt * 128:(tt + 1) * 128]
        psrc = PBF[:].rearrange("p (f c) -> p f c", f=NFC)
        on("vector", lambda e, zdst=zdst, psrc=psrc: e.tensor_copy(zdst, psrc),
           waits=[sT.tok], inc=sV)
        RG["pbf"] = sV.n
        RG["zt"][tt % 2] = sV.n
    def _zscale(e):
        last = None
        for fc in range(NFC):
            last = e.tensor_scalar(z[:, fc * T:(fc + 1) * T],
                                   z[:, fc * T:(fc + 1) * T],
                                   ln1w_c[:, fc:fc + 1], None, op0=ALU.mult)
        return last
    on("vector", _zscale, waits=[t_gcols], inc=sV, drain=True)
    t_p1 = sV.tok
    dump("xn", xnO[:], [(sS, sS.n)])
    dump("z", z[:], [t_p1])

    tmx = dma("gpsimd", mx_i_t.ap().rearrange("a b -> b a"), mcol[:, 0:1],
              dMxg, waits=[t_p1])
    on("gpsimd", lambda e: e.collective_compute(
        "AllReduce", ALU.max, replica_groups=[list(range(NCORES))],
        ins=[mx_i_t.ap().opt()], outs=[mx_o_t.ap().opt()]),
       waits=[tmx], inc=cc)
    vv = dma("sync", dmax[0:1, 0:128], mx_o_t.ap(), dMxr, waits=[cc.tok])
    on("vector", lambda e: e.tensor_reduce(srow[0:1, 0:1], dmax[0:1, 0:128],
                                           axis=AX, op=ALU.max),
       waits=[vv], inc=sV, drain=True)
    on("vector", lambda e: e.tensor_scalar(srow[0:1, 1:2], srow[0:1, 0:1],
                                           1.0 / 32767.0, 1e-12,
                                           op0=ALU.mult, op1=ALU.max),
       inc=sV, drain=True)
    on("vector", lambda e: e.reciprocal(srow[0:1, 2:3], srow[0:1, 1:2]),
       inc=sV, drain=True)
    on("vector", lambda e: e.tensor_scalar(srow[0:1, 12:13], srow[0:1, 1:2],
                                           -MAGIC, None, op0=ALU.mult),
       inc=sV, drain=True)
    t_s1 = sV.tok

    PB6G = {"v": None}
    PB5G = {"v": None}

    def bcast_scales(t_ready):
        w6 = [t_ready]
        if PB6G["v"] is not None:
            w6.append((sV, PB6G["v"]))
        on("tensor", lambda e: e.matmul(PB[6][:, 0:16], onesr[:],
                                        srow[0:1, 0:16], start=True, stop=True),
           waits=w6, inc=sT)
        on("vector", lambda e: e.tensor_copy(ccol[:], PB[6][:, 0:16]),
           waits=[sT.tok], inc=sV, drain=True)
        PB6G["v"] = sV.n
        return sV.tok

    t_cc1 = bcast_scales(t_s1)
    dump("scales", srow[0:1, 0:16], [t_s1])

    RG["zt"] = [None, None]
    for tt in range(NTT):
        xsl = xnO[:, tt * D:(tt + 1) * D]
        buf = xldA0 if tt % 2 == 0 else xldA1
        zb = ztA0 if tt % 2 == 0 else ztA1
        w = [t_cc1]
        if RG["xld"][tt % 2]:
            w.append((sS, RG["xld"][tt % 2]))
        on("scalar", lambda e, buf=buf, xsl=xsl: e.activation(
            buf[:], xsl, AF.Copy, bias=MAGIC, scale=ccol[:, 2:3]),
           waits=w, inc=sS)
        w = [sS.tok]
        if RG["zt"][tt % 2]:
            w.append((sT, RG["zt"][tt % 2]))
        on("scalar", lambda e, buf=buf, zb=zb: e.activation(
            zb[:], buf[:], AF.Identity, bias=ccol[:, 12:13],
            scale=ccol[:, 1:2]), waits=w, inc=sS, drain=True)
        RG["xld"][tt % 2] = sS.n
        t_xq = sS.tok
        w = [t_xq]
        if RG["pbf"]:
            w.append((sV, RG["pbf"]))

        def _xtr(e, zb=zb):
            last = None
            for fc in range(NFC):
                last = e.transpose(PBF[:, fc * 128:(fc + 1) * 128],
                                   zb[:, fc * 128:(fc + 1) * 128], ident[:])
            return last
        on("tensor", _xtr, waits=w, inc=sT)
        RG["zt"][tt % 2] = sT.n
        xdst = xq[:].rearrange("p (f t) -> p f t", f=NFC)[:, :, tt * 128:(tt + 1) * 128]
        psrc = PBF[:].rearrange("p (f c) -> p f c", f=NFC)
        on("vector", lambda e, xdst=xdst, psrc=psrc: e.tensor_copy(xdst, psrc),
           waits=[sT.tok], inc=sV)
        RG["pbf"] = sV.n
        dma("sync", xn_sp[tt], xsl, dSp0 if tt % 2 == 0 else dSp1,
            waits=[sV.tok])
        dma("sync",
            xq_sp[:, :, tt * 128:(tt + 1) * 128].rearrange("f p t -> p f t"),
            xq[:].rearrange("p (f t) -> p f t",
                            f=NFC)[:, :, tt * 128:(tt + 1) * 128],
            dXq0 if tt % 2 == 0 else dXq1, waits=[sV.tok])
    t_xq_all = sV.tok
    t_xnspill = (dSp0.tok, dSp1.tok)
    dump("xq", xq[:], [t_xq_all])

    RG["xld"] = [None, None]

    t_wagd = plan_down_chain()


    t_p2s = (sS, sS.n)
    t_p2t = (sT, sT.n)
    t_xqsp0 = (dXq0, dXq0.n)
    t_xqsp1 = (dXq1, dXq1.n)
    xqs = [xqs0, xqs1]
    f1s = [wstage[:, 0:512], wstage[:, 512:1024]]
    f1_stg = [wbf2, wbf3]
    F1 = {"w0": [], "wld": [None, None], "stg": [None, None],
          "fs_sp": [None, None], "fs_red": [None, None], "pb3": None,
          "xqs": [None, None], "pend": [], "done_mm": None,
          "pb3_tok": None}

    def f1_arm():
        F1["w0"] = [(sV, sV.n), (sS, sS.n), (sT, sT.n),
                    (dSp0, dSp0.n), (dSp1, dSp1.n), t_xqsp0, t_xqsp1]

    def _f1_flush_pend(keep):
        while len(F1["pend"]) > keep:
            fh, col, t_ev = F1["pend"].pop(0)
            on("vector", lambda e, fh=fh, col=col: e.tensor_reduce(
                f3mx[:, col:col + 1], f1s[fh][:], axis=AX, op=ALU.max,
                apply_absolute_value=True), waits=[t_ev], inc=sV, drain=True)
            F1["fs_red"][fh] = sV.n

    def plan_f1_wld(widx, ic):
        hf = (widx * NFF + ic) % 2
        w = [t_wag] + F1["w0"]
        if F1["stg"][hf] is not None:
            w.append((sT, F1["stg"][hf]))
        src_ = wag_o[:, widx, ic, :, :].rearrange("r p c -> p r c")
        F1["wld"][hf] = dma(
            "sync",
            f1_stg[hf][:, 0:1024].rearrange("p (r c) -> p r c", r=NCORES),
            src_, dF1l0 if hf == 0 else dF1l1, waits=w)

    def plan_f1_unit(widx, ic, nt):
        u = (widx * NFF + ic) * 4 + nt
        fh = u % 2
        xb = u % 2
        stg_hf = (widx * NFF + ic) % 2
        wq = list(F1["w0"])
        if F1["xqs"][xb] is not None:
            wq.append((sT, F1["xqs"][xb]))
        t_xsl = dma("sync", xqs[xb][:].rearrange("p (f c) -> p f c", f=NFC),
                    xq_sp[:, :, nt * 512:(nt + 1) * 512]
                    .rearrange("f p t -> p f t"),
                    dXqs0 if xb == 0 else dXqs1, waits=wq)

        def _mm(e, stg_hf=stg_hf, xb=xb):
            last = None
            for kc in range(NFC):
                last = e.matmul(PB[3][:, 0:512],
                                f1_stg[stg_hf][:, kc * 128:(kc + 1) * 128],
                                xqs[xb][:, kc * 512:(kc + 1) * 512],
                                start=(kc == 0), stop=(kc == NFC - 1))
            return last
        w2 = [F1["wld"][stg_hf], t_xsl]
        if F1["pb3"] is not None:
            w2.append((sS, F1["pb3"]))
        on("tensor", _mm, waits=w2, inc=sT)
        F1["xqs"][xb] = sT.n
        if nt == 3:
            F1["stg"][stg_hf] = sT.n
        t_mmu = sT.tok
        _f1_flush_pend(1)
        w3 = [t_mmu]
        if F1["fs_sp"][fh] is not None:
            w3.append(F1["fs_sp"][fh])
        if F1["fs_red"][fh] is not None:
            w3.append((sV, F1["fs_red"][fh]))
        on("scalar", lambda e, fh=fh: e.copy(f1s[fh][:], PB[3][:, 0:512]),
           waits=w3, inc=sS)
        F1["pb3"] = sS.n
        t_ev = sS.tok
        spill = sp_up if widx == 0 else sp_gt
        F1["fs_sp"][fh] = dma("gpsimd",
                              spill[ic, :, nt * 512:(nt + 1) * 512],
                              f1s[fh][:], dF1s0 if fh == 0 else dF1s1,
                              waits=[t_ev])
        F1["pend"].append((fh, widx * 96 + ic * 4 + nt, t_ev))

    f1_items = []
    for widx in range(2):
        for ic in range(NFF):
            f1_items.append(lambda widx=widx, ic=ic: plan_f1_wld(widx, ic))
            for nt in range(4):
                f1_items.append(
                    lambda widx=widx, ic=ic, nt=nt: plan_f1_unit(widx, ic, nt))
    f1_pos = [0]

    def f1_pump(n=1):
        while n > 0 and f1_pos[0] < len(f1_items):
            f1_items[f1_pos[0]]()
            f1_pos[0] += 1
            n -= 1

    def f1_finish():
        f1_pump(len(f1_items))
        _f1_flush_pend(0)
        F1["done_mm"] = (sT, sT.n)
        F1["pb3_tok"] = (sS, F1["pb3"])
        on("vector", lambda e: e.tensor_reduce(
            mcol[:, 1:2], f3mx[:, 0:96], axis=AX, op=ALU.max),
           inc=sV, drain=True)
        on("vector", lambda e: e.tensor_reduce(
            mcol[:, 2:3], f3mx[:, 96:192], axis=AX, op=ALU.max),
           inc=sV, drain=True)
        tmx1 = dma("gpsimd", mx2_i_t.ap()[0:1, 0:128].rearrange("a b -> b a"),
                   mcol[:, 1:2], dMxg, waits=[sV.tok])
        tmx2 = dma("gpsimd", mx2_i_t.ap()[0:1, 128:256].rearrange("a b -> b a"),
                   mcol[:, 2:3], dMxg)
        on("gpsimd", lambda e: e.collective_compute(
            "AllReduce", ALU.max, replica_groups=[list(range(NCORES))],
            ins=[mx2_i_t.ap().opt()], outs=[mx2_o_t.ap().opt()]),
           waits=[tmx2], inc=cc)
        vv = dma("sync", dmax[0:1, 0:256], mx2_o_t.ap(), dMxr, waits=[cc.tok])
        on("vector", lambda e: e.tensor_reduce(srow[0:1, 3:4],
                                               dmax[0:1, 0:128],
                                               axis=AX, op=ALU.max),
           waits=[vv], inc=sV, drain=True)
        on("vector", lambda e: e.tensor_reduce(srow[0:1, 5:6],
                                               dmax[0:1, 128:256],
                                               axis=AX, op=ALU.max),
           inc=sV, drain=True)
        on("vector", lambda e: e.tensor_scalar(srow[0:1, 3:4], srow[0:1, 3:4],
                                               1.0 / 32767.0, 1e-12,
                                               op0=ALU.mult, op1=ALU.max),
           inc=sV, drain=True)
        on("vector", lambda e: e.reciprocal(srow[0:1, 4:5], srow[0:1, 3:4]),
           inc=sV, drain=True)
        on("vector", lambda e: e.tensor_scalar(srow[0:1, 5:6], srow[0:1, 5:6],
                                               1.0 / 32767.0, 1e-12,
                                               op0=ALU.mult, op1=ALU.max),
           inc=sV, drain=True)
        on("vector", lambda e: e.reciprocal(srow[0:1, 6:7], srow[0:1, 5:6]),
           inc=sV, drain=True)
        on("vector", lambda e: e.tensor_scalar(srow[0:1, 13:14],
                                               srow[0:1, 5:6],
                                               -MAGIC, None, op0=ALU.mult),
           inc=sV, drain=True)
        return bcast_scales(sV.tok)

    t_s23 = None

    F3 = {"t_s4": None, "t_s5": None,
          "gb": [None, None], "f0": [None, None], "ob": [None, None],
          "bb": [None, None], "done": None, "w0": []}
    f3_gb = [wbf2[:, 0:512], wbf2[:, 512:1024]]
    f3_ob = [wbf3[:, 0:512], wbf3[:, 512:1024]]
    f3_pb = f3_gb
    f3_ub = [ktT0[:, 0:512], ktT0[:, 512:1024]]
    f3_f0 = [SCA[:, 0:512], SCB[:, 0:512]]
    f3_f2 = [wstage[:, 0:512], wstage[:, 512:1024]]
    F3N = 4 * NFF

    def f3_arm():
        F3["w0"] = [(sV, sV.n), (sT, sT.n),
                    (dF1s0, dF1s0.n), (dF1s1, dF1s1.n)]

    def plan_f3a_unit(i):
        ic, nt = divmod(i, 4)
        hf = i % 2
        sl = slice(nt * 512, (nt + 1) * 512)
        w = [t_s23]
        if F3["gb"][hf] is not None:
            w.append((sS, F3["gb"][hf]))
        vld = dma("scalar", f3_gb[hf], sp_gt[ic, :, sl],
                  dF3l0 if hf == 0 else dF3l1, waits=w)
        w1_ = [vld] + F3["w0"]
        if F3["f0"][hf] is not None:
            w1_.append((sV, F3["f0"][hf]))
        on("scalar", lambda e, hf=hf: e.activation(
            f3_f0[hf], f3_gb[hf], AF.Copy, bias=MAGIC, scale=ccol[:, 6:7]),
           waits=w1_, inc=sS)
        F3["gb"][hf] = sS.n
        on("scalar", lambda e, hf=hf: e.activation(
            f3_f0[hf], f3_f0[hf], AF.Identity, bias=ccol[:, 13:14],
            scale=ccol[:, 5:6]), inc=sS, drain=True)
        on("scalar", lambda e, hf=hf: e.activation(
            f3_f2[hf], f3_f0[hf], AF.Sigmoid, bias=ecol[:, 2:3], scale=1.0),
           inc=sS, drain=True)
        on("scalar", lambda e, hf=hf: e.activation(
            f3_f2[hf], f3_f2[hf], AF.Copy, bias=MAGIC, scale=65536.0),
           inc=sS, drain=True)
        on("vector", lambda e, hf=hf: e.tensor_scalar(
            f3_f2[hf], f3_f2[hf], MAGIC, 65535.0, op0=ALU.subtract,
            op1=ALU.min), waits=[sS.tok], inc=sV, drain=True)
        w2_ = []
        if F3["ob"][hf] is not None:
            w2_.append(F3["ob"][hf])
        on("vector", lambda e, hf=hf: e.scalar_tensor_tensor(
            f3_ob[hf], f3_f0[hf], 1.0 / 65536.0, f3_f2[hf], op0=ALU.mult,
            op1=ALU.mult), waits=w2_, inc=sV, drain=True)
        F3["f0"][hf] = sV.n
        on("vector", lambda e, hf=hf, i=i: e.tensor_reduce(
            f3mx[:, i:i + 1], f3_ob[hf], axis=AX, op=ALU.max,
            apply_absolute_value=True), inc=sV, drain=True)
        F3["done"] = sV.n
        F3["ob"][hf] = dma("gpsimd", sp_pr[ic, :, sl], f3_ob[hf],
                           dF3s0 if hf == 0 else dF3s1, waits=[(sV, sV.n)])

    def plan_f3a_finish():
        on("vector", lambda e: e.tensor_reduce(
            mcol[:, 3:4], f3mx[:, 0:F3N], axis=AX, op=ALU.max),
           waits=[(sV, F3["done"])], inc=sV, drain=True)
        tmx4 = dma("gpsimd", mx4_i_t.ap().rearrange("a b -> b a"),
                   mcol[:, 3:4], dMxg, waits=[sV.tok])
        on("gpsimd", lambda e: e.collective_compute(
            "AllReduce", ALU.max, replica_groups=[list(range(NCORES))],
            ins=[mx4_i_t.ap().opt()], outs=[mx4_o_t.ap().opt()]),
           waits=[tmx4], inc=cc)
        vv = dma("sync", dmax[0:1, 0:128], mx4_o_t.ap(), dMxr, waits=[cc.tok])
        on("vector", lambda e: e.tensor_reduce(srow[0:1, 7:8],
                                               dmax[0:1, 0:128], axis=AX,
                                               op=ALU.max),
           waits=[vv], inc=sV, drain=True)
        on("vector", lambda e: e.tensor_scalar(srow[0:1, 7:8], srow[0:1, 7:8],
                                               1.0 / 32767.0, 1e-12,
                                               op0=ALU.mult, op1=ALU.max),
           inc=sV, drain=True)
        on("vector", lambda e: e.reciprocal(srow[0:1, 8:9], srow[0:1, 7:8]),
           inc=sV, drain=True)
        on("vector", lambda e: e.tensor_tensor(srow[0:1, 9:10], srow[0:1, 7:8],
                                               srow[0:1, 3:4], op=ALU.mult),
           inc=sV, drain=True)
        on("vector", lambda e: e.tensor_scalar(srow[0:1, 14:15],
                                               srow[0:1, 9:10], -MAGIC, None,
                                               op0=ALU.mult),
           inc=sV, drain=True)
        F3["t_s4"] = bcast_scales(sV.tok)
        F3["gb"] = [None, None]
        F3["f0"] = [None, None]

    def plan_f3b_unit(i):
        ic, nt = divmod(i, 4)
        hf = i % 2
        sl = slice(nt * 512, (nt + 1) * 512)
        w = [F3["t_s4"]]
        if F3["bb"][hf] is not None:
            w.append((sS, F3["bb"][hf]))
        else:
            w += [F3["ob"][0], F3["ob"][1]] + F3["w0"]
        v1_ = dma("scalar", f3_pb[hf], sp_pr[ic, :, sl],
                  dF3l0 if hf == 0 else dF3l1, waits=w)
        v2_ = dma("scalar", f3_ub[hf], sp_up[ic, :, sl],
                  dF3l2 if hf == 0 else dF3l3, waits=list(w))
        w1_ = [v1_, v2_]
        if F3["f0"][hf] is not None:
            w1_.append((sV, F3["f0"][hf]))
        on("scalar", lambda e, hf=hf: e.activation(
            f3_f0[hf], f3_pb[hf], AF.Copy, bias=MAGIC, scale=ccol[:, 8:9]),
           waits=w1_, inc=sS)
        on("scalar", lambda e, hf=hf: e.activation(
            f3_f2[hf], f3_ub[hf], AF.Copy, bias=MAGIC, scale=ccol[:, 4:5]),
           inc=sS)
        F3["bb"][hf] = sS.n
        on("scalar", lambda e, hf=hf: e.activation(
            f3_f0[hf], f3_f0[hf], AF.Identity, bias=ecol[:, 4:5], scale=1.0),
           inc=sS, drain=True)
        on("scalar", lambda e, hf=hf: e.activation(
            f3_f2[hf], f3_f2[hf], AF.Identity, bias=ccol[:, 14:15],
            scale=ccol[:, 9:10]), inc=sS, drain=True)
        w2_ = [sS.tok]
        if F3["ob"][hf] is not None:
            w2_.append(F3["ob"][hf])
        on("vector", lambda e, hf=hf: e.tensor_tensor(
            f3_ob[hf], f3_f0[hf], f3_f2[hf], op=ALU.mult),
           waits=w2_, inc=sV, drain=True)
        F3["f0"][hf] = sV.n
        on("vector", lambda e, hf=hf, i=i: e.tensor_reduce(
            f3mx[:, F3N + i:F3N + i + 1], f3_ob[hf], axis=AX, op=ALU.max,
            apply_absolute_value=True), inc=sV, drain=True)
        F3["done"] = sV.n
        F3["ob"][hf] = dma("gpsimd", sp_or[ic, :, sl], f3_ob[hf],
                           dF3s0 if hf == 0 else dF3s1, waits=[(sV, sV.n)])

    def plan_f3b_finish():
        on("vector", lambda e: e.tensor_reduce(
            mcol[:, 4:5], f3mx[:, F3N:2 * F3N], axis=AX, op=ALU.max),
           waits=[(sV, F3["done"])], inc=sV, drain=True)
        tmx5 = dma("gpsimd", mx5_i_t.ap().rearrange("a b -> b a"),
                   mcol[:, 4:5], dMxg, waits=[sV.tok])
        on("gpsimd", lambda e: e.collective_compute(
            "AllReduce", ALU.max, replica_groups=[list(range(NCORES))],
            ins=[mx5_i_t.ap().opt()], outs=[mx5_o_t.ap().opt()]),
           waits=[tmx5], inc=cc)
        vv = dma("sync", dmax[0:1, 0:128], mx5_o_t.ap(), dMxr, waits=[cc.tok])
        on("vector", lambda e: e.tensor_reduce(srow[0:1, 10:11],
                                               dmax[0:1, 0:128], axis=AX,
                                               op=ALU.max),
           waits=[vv], inc=sV, drain=True)
        on("vector", lambda e: e.tensor_scalar(srow[0:1, 10:11],
                                               srow[0:1, 10:11],
                                               1.0 / 32767.0, 1e-12,
                                               op0=ALU.mult, op1=ALU.max),
           inc=sV, drain=True)
        on("vector", lambda e: e.reciprocal(srow[0:1, 11:12],
                                            srow[0:1, 10:11]),
           inc=sV, drain=True)
        on("vector", lambda e: e.tensor_scalar(srow[0:1, 15:16],
                                               srow[0:1, 10:11], -MAGIC, None,
                                               op0=ALU.mult),
           inc=sV, drain=True)
        F3["t_s5"] = bcast_scales(sV.tok)

    f3_items = ([lambda i=i: plan_f3a_unit(i) for i in range(F3N)] +
                [plan_f3a_finish] +
                [lambda i=i: plan_f3b_unit(i) for i in range(F3N)] +
                [plan_f3b_finish])
    f3_pos = [0]

    def f3_pump(n=1):
        while n > 0 and f3_pos[0] < len(f3_items):
            f3_items[f3_pos[0]]()
            f3_pos[0] += 1
            n -= 1

    def f3_drain():
        f3_pump(len(f3_items))

    RG["gks"] = [None, None]
    qt_spill_v = [None, None]
    proj_pb_free = {"k": None, "q": None, "v": None}
    tr_guard = [None, None]
    pb4_free = None
    pb5_free = None
    h_b_ready = None
    t_group_done = None
    t_bgk1 = None
    bias_done = {"k": None, "q": None, "v": None}

    def plan_wcast(W_tiled_bf, mc, pc_tok, waits_extra=()):
        hf = mc % 2
        w = list(waits_extra) + [pc_tok[0], pc_tok[1]]
        if RG["wbf"][hf] is not None:
            w.append((sT, RG["wbf"][hf]))
        wb = wbf0 if hf == 0 else wbf1
        vld = dma("sync",
                  wb[:, 0:1024].rearrange("p (a m) -> p a m", a=NFC),
                  W_tiled_bf[mc].rearrange("kc p c -> p kc c"),
                  dWc0 if hf == 0 else dWc1, waits=w)
        return wb, vld, hf

    f1_arm()
    GK = {"pb_s": None}
    for g in range(NG):
        if g == 3:
            t_s23 = f1_finish()
        w = [t_wgk, t_p1]
        if t_group_done is not None:
            w.append(t_group_done)
        t_csfree = w[-1] if t_group_done is not None else None

        def _u1(e, g=g):
            last = None
            for fc in range(NFC):
                last = e.matmul(PB[0][0:16, 0:512],
                                wgk1b[:, fc * 16:(fc + 1) * 16],
                                z[:, fc * T + g * G:fc * T + (g + 1) * G],
                                start=(fc == 0), stop=(fc == NFC - 1))
            return last
        if proj_pb_free["k"] is not None:
            w.append((sV, proj_pb_free["k"]))
        on("tensor", _u1, waits=w, inc=sT)
        t_u1 = sT.tok
        if g == 0:
            def _u1b(e):
                last = None
                for fc in range(NFC):
                    last = e.matmul(PB[1][0:16, 0:1],
                                    wgk1b[:, fc * 16:(fc + 1) * 16],
                                    ln1b_cb[:, fc:fc + 1],
                                    start=(fc == 0), stop=(fc == NFC - 1))
                return last
            on("tensor", _u1b, waits=[t_gcolsb, t_wgk], inc=sT)
            on("vector", lambda e: e.tensor_copy(bgk1_c[:], PB[1][0:16, 0:1]),
               waits=[sT.tok], inc=sV)
            t_bgk1 = sV.tok
        u1sl = u1b[0:16, (g % 2) * G:(g % 2 + 1) * G]
        on("vector", lambda e, u1sl=u1sl: e.tensor_scalar(
            u1sl, PB[0][0:16, 0:512], bgk1_c[:], None,
            op0=ALU.add), waits=[t_u1, t_bgk1], inc=sV)
        t_u1r = sV.tok
        for mc in range(NFC):
            pb = PB[2]
            w = [t_u1r]
            if RG["gks"][mc % 2] is not None:
                w.append((sV, RG["gks"][mc % 2]))
            if GK["pb_s"] is not None:
                w.append((sS, GK["pb_s"]))
            if proj_pb_free["v"] is not None:
                w.append((sV, proj_pb_free["v"]))
            on("tensor", lambda e, mc=mc, g=g, pb=pb, u1sl=u1sl: e.matmul(
                pb[:, 0:512], wgk2b[0:16, mc * 128:(mc + 1) * 128],
                u1sl, start=True, stop=True),
               waits=w, inc=sT)
            gsb = SCA if mc % 2 == 0 else SCB
            on("scalar", lambda e, gsb=gsb, pb=pb, mc=mc: e.activation(
                gsb[:], pb[:, 0:512], AF.Exp, bias=bgk2_c[:, mc:mc + 1],
                scale=-1.0), waits=[sT.tok], inc=sS)
            GK["pb_s"] = sS.n
            on("scalar", lambda e, gsb=gsb: e.activation(
                gsb[:], gsb[:], AF.Ln, bias=ecol[:, 3:4], scale=1.0),
               inc=sS, drain=True)
            t_lg = sS.tok

            def _scan(e, mc=mc, gsb=gsb):
                last = None
                for ch in range(CPG):
                    last = e.tensor_tensor_scan(
                        cs_g[:, mc * G + ch * C:mc * G + (ch + 1) * C],
                        gsb[:, ch * C:(ch + 1) * C],
                        gsb[:, ch * C:(ch + 1) * C], 0.0,
                        op0=ALU.add, op1=ALU.bypass)
                return last
            wsc = [t_lg]
            if g == 0:
                wsc += [t_xqsp0, t_xqsp1, DT.get("xq")]
            else:
                wsc.append(DT.get("cs"))
            on("vector", _scan, waits=wsc, inc=sV)
            RG["gks"][mc % 2] = sV.n
            f1_pump(1)
            csend = cs_g[:].rearrange("p (f ch c) -> p (f ch) c",
                                      f=NFC, ch=CPG)[:, mc * CPG:(mc + 1) * CPG,
                                                     C - 1:C]
            on("scalar", lambda e, mc=mc, g=g, csend=csend: e.activation(
                expbc[:, mc * NCH + g * CPG:mc * NCH + (g + 1) * CPG],
                csend, AF.Exp, bias=ecol[:, 2:3], scale=-1.0 / 16.0),
               waits=[sV.tok], inc=sS)
        t_cs = sS.tok

        for gate, W_tl, pc_tok, bcolbuf in (
                ("k", wk_bf, WPC["k"], bk_c), ("q", wq_bf, WPC["q"], bq_c)):
            for mc in range(NFC):
                wb, t_wb, hf = plan_wcast(W_tl, mc, pc_tok)
                pb = PB[0] if gate == "k" else PB[1]

                def _mm(e, wb=wb, g=g, pb=pb):
                    last = None
                    for kc in range(NFC):
                        last = e.matmul(pb[:, 0:512],
                                        wb[:, kc * 128:(kc + 1) * 128],
                                        z[:, kc * T + g * G:kc * T + (g + 1) * G],
                                        start=(kc == 0), stop=(kc == NFC - 1))
                    return last
                w3 = [t_wb, t_p1]
                if proj_pb_free[gate] is not None:
                    w3.append((sV, proj_pb_free[gate]))
                on("tensor", _mm, waits=w3, inc=sT)
                RG["wbf"][hf] = sT.n
                t_mm = sT.tok
                if g == 0:
                    def _bmm(e, wb=wb):
                        last = None
                        for kc in range(NFC):
                            last = e.matmul(PB[5][:, 0:1],
                                            wb[:, kc * 128:(kc + 1) * 128],
                                            ln1b_cb[:, kc:kc + 1],
                                            start=(kc == 0), stop=(kc == NFC - 1))
                        return last
                    wb5_ = [t_gcolsb, t_wb]
                    if PB5G["v"] is not None:
                        wb5_.append((sV, PB5G["v"]))
                    on("tensor", _bmm, waits=wb5_, inc=sT)
                    RG["wbf"][hf] = sT.n
                    on("vector", lambda e, mc=mc, bcolbuf=bcolbuf: e.tensor_copy(
                        bcolbuf[:, mc:mc + 1], PB[5][:, 0:1]),
                       waits=[sT.tok], inc=sV)
                    bias_done[gate] = sV.n
                    PB5G["v"] = sV.n
                t_bias = (sV, bias_done[gate]) if bias_done[gate] else None
                eb = SCA if gate == "k" else SCB
                we = [t_cs]
                if proj_pb_free[gate] is not None:
                    we.append((sV, proj_pb_free[gate]))
                if gate == "k":
                    on("scalar", lambda e, mc=mc, eb=eb: e.activation(
                        eb[:], cs_g[:, mc * G:(mc + 1) * G], AF.Exp,
                        bias=ecol[:, 2:3], scale=1.0 / 16.0),
                       waits=we, inc=sS)
                else:
                    on("scalar", lambda e, mc=mc, eb=eb: e.activation(
                        eb[:], cs_g[:, mc * G:(mc + 1) * G], AF.Exp,
                        bias=ecol[:, 1:2], scale=-1.0 / 16.0),
                       waits=we, inc=sS)
                t_exp = sS.tok
                if gate == "k":
                    dsl = kt_g[:, mc * G:(mc + 1) * G]
                else:
                    dsl = qtr[:, (g % 2) * NFC * G + mc * G:
                              (g % 2) * NFC * G + (mc + 1) * G]
                wev = [t_mm, t_exp, t_bias]
                if gate == "q" and qt_spill_v[g % 2] is not None:
                    wev.append(qt_spill_v[g % 2])
                elif gate == "q":
                    wev += [t_p2s, t_p2t]
                if gate == "k" and g > 0:
                    wev.append(DT.get("kt"))
                if gate == "k" and g == 0:
                    wev += [t_xqsp0, t_xqsp1]
                on("vector", lambda e, dsl=dsl, eb=eb, pb=pb, mc=mc,
                   bcolbuf=bcolbuf: e.scalar_tensor_tensor(
                       dsl, pb[:, 0:512], bcolbuf[:, mc:mc + 1], eb[:],
                       op0=ALU.add, op1=ALU.mult),
                   waits=wev, inc=sV)
                proj_pb_free[gate] = sV.n
                f1_pump(2)

        qsl = qtr[:, (g % 2) * NFC * G:(g % 2 + 1) * NFC * G]
        qt_spill_v[g % 2] = dma(
            "gpsimd", qt_sp[g].rearrange("f p c -> p f c"),
            qsl.rearrange("p (f c) -> p f c", f=NFC),
            dQs0 if g % 2 == 0 else dQs1,
            waits=[(sV, proj_pb_free["q"])])

        for mc in range(NFC):
            wb, t_wb, hf = plan_wcast(wv_bf, mc, WPC["v"])

            def _mm(e, wb=wb, g=g):
                last = None
                for kc in range(NFC):
                    last = e.matmul(PB[2][:, 0:512],
                                    wb[:, kc * 128:(kc + 1) * 128],
                                    z[:, kc * T + g * G:kc * T + (g + 1) * G],
                                    start=(kc == 0), stop=(kc == NFC - 1))
                return last
            w3 = [t_wb, t_p1]
            if proj_pb_free["v"] is not None:
                w3.append((sV, proj_pb_free["v"]))
            on("tensor", _mm, waits=w3, inc=sT)
            RG["wbf"][hf] = sT.n
            t_mm = sT.tok
            if g == 0:
                def _bmm(e, wb=wb):
                    last = None
                    for kc in range(NFC):
                        last = e.matmul(PB[5][:, 0:1],
                                        wb[:, kc * 128:(kc + 1) * 128],
                                        ln1b_cb[:, kc:kc + 1],
                                        start=(kc == 0), stop=(kc == NFC - 1))
                    return last
                wb5_ = [t_gcolsb, t_wb]
                if PB5G["v"] is not None:
                    wb5_.append((sV, PB5G["v"]))
                on("tensor", _bmm, waits=wb5_, inc=sT)
                RG["wbf"][hf] = sT.n
                on("vector", lambda e, mc=mc: e.tensor_copy(
                    bv_c[:, mc:mc + 1], PB[5][:, 0:1]), waits=[sT.tok], inc=sV)
                bias_done["v"] = sV.n
                PB5G["v"] = sV.n
            t_bias = (sV, bias_done["v"]) if bias_done["v"] else None
            wvv = [t_mm, t_bias]
            if g > 0:
                wvv.append(DT.get("vt"))
            else:
                wvv += [t_xqsp0, t_xqsp1]
            on("vector", lambda e, mc=mc: e.tensor_scalar(
                vt_g[:, mc * G:(mc + 1) * G], PB[2][:, 0:512],
                bv_c[:, mc:mc + 1], None, op0=ALU.add),
               waits=wvv, inc=sV)
            proj_pb_free["v"] = sV.n
            f1_pump(2)
        t_proj = sV.tok

        if "cs" in dbg:
            DT["cs"] = dma("sync", DOUT["cs"][g], cs_g[:], dO,
                           waits=[t_proj, t_cs])
        if "kt" in dbg:
            DT["kt"] = dma("sync", DOUT["kt"][g], kt_g[:], dO, waits=[t_proj])
        if "vt" in dbg:
            DT["vt"] = dma("sync", DOUT["vt"][g], vt_g[:], dO, waits=[t_proj])

        for ch in range(CPG):
            c = g * CPG + ch
            hf = c % 2
            ktT = ktT0
            vtT = vtT0
            w = [t_proj]
            if tr_guard[hf] is not None:
                w.append((sV, tr_guard[hf]))
            if RG["pbf"]:
                w.append((sV, RG["pbf"]))

            def _ktr(e, ch=ch):
                last = None
                for fc in range(NFC):
                    last = e.transpose(
                        PBF[:, fc * 128:(fc + 1) * 128],
                        kt_g[:, fc * G + ch * C:fc * G + (ch + 1) * C],
                        ident[:])
                return last
            on("tensor", _ktr, waits=w, inc=sT)
            on("vector", lambda e, ktT=ktT: e.tensor_copy(ktT[:], PBF[:]),
               waits=[sT.tok], inc=sV)
            RG["pbf"] = sV.n
            t_ktT = sV.tok

            def _vtr(e, ch=ch):
                last = None
                for fc in range(NFC):
                    last = e.transpose(
                        PBF[:, fc * 128:(fc + 1) * 128],
                        vt_g[:, fc * G + ch * C:fc * G + (ch + 1) * C],
                        ident[:])
                return last
            on("tensor", _vtr, waits=[(sV, RG["pbf"])], inc=sT)
            on("vector", lambda e, vtT=vtT: e.tensor_copy(vtT[:], PBF[:]),
               waits=[sT.tok], inc=sV)
            RG["pbf"] = sV.n
            t_vtT = sV.tok

            for hh in range(H):
                ab = atm0 if hh % 2 == 0 else atm1

                def _at(e, hh=hh, ch=ch, c=c, g=g):
                    last = None
                    for kc in range(2):
                        fc = hh * 2 + kc
                        last = e.matmul(
                            PB[6][:, 0:128],
                            kt_g[:, fc * G + ch * C:fc * G + (ch + 1) * C],
                            qtr[:, (g % 2) * NFC * G + fc * G + ch * C:
                                (g % 2) * NFC * G + fc * G + (ch + 1) * C],
                            start=(kc == 0), stop=(kc == 1))
                    return last
                w = [t_proj]
                if PB6G["v"] is not None:
                    w.append((sV, PB6G["v"]))
                on("tensor", _at, waits=w, inc=sT)
                on("vector", lambda e, ab=ab: e.tensor_tensor(
                    ab[:], PB[6][:, 0:128], utmask[:], op=ALU.mult),
                   waits=[sT.tok], inc=sV)
                PB6G["v"] = sV.n
                t_atm = sV.tok

                def _o(e, hh=hh, c=c, ab=ab, vtT=vtT, g=g, ch=ch):
                    last = e.matmul(PB[5][:, 0:256], ab[:],
                                    vtT[:, hh * DV:(hh + 1) * DV],
                                    start=True, stop=(c == 0))
                    if c != 0:
                        for kc in range(2):
                            fc = hh * 2 + kc
                            last = e.matmul(
                                PB[5][:, 0:256],
                                qtr[:, (g % 2) * NFC * G + fc * G + ch * C:
                                    (g % 2) * NFC * G + fc * G + (ch + 1) * C],
                                h_b[:, fc * DV:(fc + 1) * DV],
                                start=False, stop=(kc == 1))
                    return last
                w = [t_atm, t_vtT]
                if c != 0:
                    w.append((sV, h_b_ready))
                if pb5_free is not None:
                    w.append((sV, pb5_free))
                elif PB5G["v"] is not None:
                    w.append((sV, PB5G["v"]))
                on("tensor", _o, waits=w, inc=sT)
                woe = [sT.tok]
                if c == 0 and hh == 0:
                    woe += [DT.get("xn"), t_xnspill[0], t_xnspill[1]]
                if c >= 12 and hh == 0:
                    woe.append(F1["done_mm"])
                on("vector", lambda e, c=c, hh=hh: e.tensor_copy(
                    xnO[:, c * D + hh * DV:c * D + (hh + 1) * DV],
                    PB[5][:, 0:256]), waits=woe, inc=sV)
                pb5_free = sV.n
                PB5G["v"] = sV.n

                def _st(e, hh=hh, ktT=ktT, vtT=vtT):
                    last = None
                    for kc in range(2):
                        fc = hh * 2 + kc
                        last = e.matmul(PB[4][:, kc * 256:(kc + 1) * 256],
                                        ktT[:, fc * 128:(fc + 1) * 128],
                                        vtT[:, hh * DV:(hh + 1) * DV],
                                        start=True, stop=True)
                    return last
                w = [t_ktT, t_vtT]
                if pb4_free is not None:
                    w.append((sV, pb4_free))
                on("tensor", _st, waits=w, inc=sT)
                t_stm = sT.tok

                def _hup(e, hh=hh, c=c):
                    last = None
                    for kc in range(2):
                        fc = hh * 2 + kc
                        hsl = h_f[:, fc * DV:(fc + 1) * DV]
                        ecc = expbc[:, fc * NCH + c:fc * NCH + c + 1]
                        if c == 0:
                            last = e.tensor_scalar(
                                hsl, PB[4][:, kc * 256:(kc + 1) * 256], ecc,
                                None, op0=ALU.mult)
                        else:
                            e.tensor_tensor(hsl, hsl,
                                            PB[4][:, kc * 256:(kc + 1) * 256],
                                            op=ALU.add)
                            e.drain()
                            last = e.tensor_scalar(hsl, hsl, ecc, None,
                                                   op0=ALU.mult)
                    return last
                on("vector", _hup, waits=[t_stm], inc=sV, drain=True)
                pb4_free = sV.n
                t_h = sV.tok
                if c < NCH - 1:
                    def _hb(e, hh=hh):
                        last = None
                        for kc in range(2):
                            fc = hh * 2 + kc
                            last = e.tensor_copy(h_b[:, fc * DV:(fc + 1) * DV],
                                                 h_f[:, fc * DV:(fc + 1) * DV])
                        return last
                    on("vector", _hb, waits=[t_h], inc=sV, drain=True)
                    h_b_ready = sV.n
            tr_guard[hf] = sV.n
            f1_pump(2)
        t_group_done = (sV, sV.n)

    t_gla = (sV, sV.n)
    dump("oloc", xnO[:], [t_gla])
    dump("sloc", h_f[:], [t_gla])
    dump("expbc", expbc[:], [(sS, sS.n)])

    def _escan(e):
        last = None
        for fc in range(NFC):
            last = e.tensor_tensor_scan(
                etab[:, fc * NCH:(fc + 1) * NCH],
                expbc[:, fc * NCH:(fc + 1) * NCH],
                expbc[:, fc * NCH:(fc + 1) * NCH], 1.0,
                op0=ALU.mult, op1=ALU.bypass)
        return last
    on("vector", _escan, waits=[t_gla, (sS, sS.n)], inc=sV, drain=True)
    t_et = sV.tok
    dump("etab", etab[:], [t_et])

    tag1 = dma("gpsimd", ag_i[:, 0:NFC * DV], h_f[:], dAg, waits=[t_gla])
    dsl = etab[:].rearrange("p (f c) -> p f c", f=NFC)[:, :, NCH - 1:NCH]
    tag2 = dma("gpsimd",
               ag_i[:, NFC * DV:AGW].rearrange("p (f c) -> p f c", f=NFC),
               dsl, dAg, waits=[t_et], allow_slow_non_contiguous=True)
    on("gpsimd", lambda e: e.collective_compute(
        "AllGather", ALU.bypass, replica_groups=[list(range(NCORES))],
        ins=[ag_i_t.ap().opt()], outs=[ag_o_t.ap().opt()]),
       waits=[tag2], inc=cc)
    t_ag = cc.tok
    f3_arm()
    f3_pump(6)

    on("vector", lambda e: e.memset(hin[:], 0.0), waits=[t_gla], inc=sV)
    Sj_v = [None, None]
    t_p4all = [(sS, sS.n), (sV, sV.n), (sT, sT.n)]
    for j in range(NCORES - 1):
        Sjb = Sjb_ if j % 2 == 0 else SjbB_
        w = [t_ag]
        if Sj_v[j % 2] is not None:
            w.append((sV, Sj_v[j % 2]))
        elif j == 0:
            w.extend([t for t in qt_spill_v if t is not None])
            w.append(t_gla)
        else:
            w.extend(t_p4all)
        vld = dma("sync", Sjb[:], ag_o[j],
                  dSj if j % 2 == 0 else dSj2, waits=w)

        def _comb(e, j=j, Sjb=Sjb):
            last = None
            for fc in range(NFC):
                dd = dcol[:, 9:10]
                e.scalar_tensor_tensor(dd, Sjb[:, NFC * DV + fc:NFC * DV + fc + 1],
                                       cmk[:, j:j + 1], cmki[:, j:j + 1],
                                       op0=ALU.mult, op1=ALU.add)
                e.drain()
                hsl = hin[:, fc * DV:(fc + 1) * DV]
                e.tensor_scalar(hsl, hsl, dd, None, op0=ALU.mult)
                e.drain()
                last = e.scalar_tensor_tensor(
                    hsl, Sjb[:, fc * DV:(fc + 1) * DV], cmk[:, j:j + 1], hsl,
                    op0=ALU.mult, op1=ALU.add)
                e.drain()
            return last
        w2 = [vld]
        if Sj_v[1 - j % 2] is not None:
            w2.append((sV, Sj_v[1 - j % 2]))
        on("vector", _comb, waits=w2, inc=sV, drain=True)
        Sj_v[j % 2] = sV.n
        f3_pump(4)
    t_hin = sV.tok
    dump("hin", hin[:], [t_hin])

    corr_pb = None
    qrl_v = [None, None]
    for c in range(NCH):
        if c % CPG == 0:
            g = c // CPG
            wrl = [t_gla, t_hin] + [t for t in qt_spill_v if t is not None]
            if corr_pb is not None:
                wrl.append((sT, corr_mm_last))
            qrl_v[g % 2] = dma(
                "sync", qtr[:, (g % 2) * NFC * G:(g % 2 + 1) * NFC * G]
                .rearrange("p (f c) -> p f c", f=NFC),
                qt_sp[g].rearrange("f p c -> p f c"),
                dQr0 if g % 2 == 0 else dQr1, waits=wrl)
        def _hd(e, c=c):
            last = None
            for fc in range(NFC):
                if c == 0:
                    last = e.tensor_copy(hinb[:, fc * DV:(fc + 1) * DV],
                                         hin[:, fc * DV:(fc + 1) * DV])
                else:
                    last = e.tensor_scalar(
                        hinb[:, fc * DV:(fc + 1) * DV],
                        hin[:, fc * DV:(fc + 1) * DV],
                        etab[:, fc * NCH + c - 1:fc * NCH + c], None,
                        op0=ALU.mult)
            return last
        w = [t_hin, t_et]
        if corr_pb is not None:
            w.append((sT, corr_mm_last))
        else:
            w += [DT.get("oloc"), DT.get("hin"), DT.get("sloc"),
                  DT.get("etab"), DT.get("expbc")]
        on("vector", _hd, waits=w, inc=sV, drain=True)
        t_hd = sV.tok
        for hh in range(H):
            g = c // CPG
            ch = c % CPG

            def _cmm(e, hh=hh, g=g, ch=ch):
                last = None
                for kc in range(2):
                    fc = hh * 2 + kc
                    last = e.matmul(PB[5][:, 0:256],
                                    qtr[:, (g % 2) * NFC * G + fc * G + ch * C:
                                        (g % 2) * NFC * G + fc * G + (ch + 1) * C],
                                    hinb[:, fc * DV:(fc + 1) * DV],
                                    start=(kc == 0), stop=(kc == 1))
                return last
            w = [t_hd, qrl_v[g % 2]]
            if corr_pb is not None:
                w.append((sV, corr_pb))
            elif PB5G["v"] is not None:
                w.append((sV, PB5G["v"]))
            on("tensor", _cmm, waits=w, inc=sT)
            corr_mm_last = sT.n
            on("vector", lambda e, c=c, hh=hh: e.tensor_tensor(
                xnO[:, c * D + hh * DV:c * D + (hh + 1) * DV],
                xnO[:, c * D + hh * DV:c * D + (hh + 1) * DV],
                PB[5][:, 0:256], op=ALU.add), waits=[sT.tok], inc=sV,
               drain=True)
            corr_pb = sV.n
            PB5G["v"] = sV.n
        f3_pump(4)
    t_ofin = (sV, sV.n)
    dump("ofin", xnO[:], [t_ofin])

    v_rw = dma("sync", rowbank[0:1, 0:DV], rmsw_h[None, :], dSm, waits=[t_ofin])
    w6 = [v_rw, t_consts]
    if PB6G["v"] is not None:
        w6.append((sV, PB6G["v"]))
    on("tensor", lambda e: e.matmul(PB[6][:, 0:DV], onesr[:],
                                    rowbank[0:1, 0:DV], start=True, stop=True),
       waits=w6, inc=sT)

    def _rmsrep(e):
        last = None
        for hh in range(H):
            last = e.tensor_copy(rmsrep[:, hh * DV:(hh + 1) * DV],
                                 PB[6][:, 0:DV])
        return last
    on("vector", _rmsrep, waits=[sT.tok, t_gla], inc=sV, drain=True)
    PB6G["v"] = sV.n
    t_rmsrep = sV.tok

    for tt in range(NTT):
        xsl = xnO[:, tt * D:(tt + 1) * D]

        def _rs(e, xsl=xsl):
            last = None
            for hh in range(H):
                last = e.bn_stats(stat6[:, hh * 6:(hh + 1) * 6],
                                  xsl[:, hh * DV:(hh + 1) * DV])
            return last
        on("vector", _rs, waits=[t_ofin, (sS, sS.n)], inc=sV, drain=True)

        def _ra(e):
            last = None
            for hh in range(H):
                last = e.bn_aggr(stat2[:, hh * 2:hh * 2 + 2],
                                 stat6[:, hh * 6:(hh + 1) * 6])
            return last
        on("vector", _ra, inc=sV, drain=True)

        def _ms(e, tt=tt):
            last = None
            for hh in range(H):
                last = e.scalar_tensor_tensor(
                    rst16[:, tt * 4 + hh:tt * 4 + hh + 1],
                    stat2[:, hh * 2:hh * 2 + 1],
                    stat2[:, hh * 2:hh * 2 + 1],
                    stat2[:, hh * 2 + 1:hh * 2 + 2],
                    op0=ALU.mult, op1=ALU.add)
            return last
        on("vector", _ms, inc=sV, drain=True)
    t_ms16 = sV.tok
    on("scalar", lambda e: e.activation(rst16[:], rst16[:], AF.Ln,
                                        bias=ecol[:, 0:1], scale=1.0),
       waits=[t_ms16], inc=sS)
    on("scalar", lambda e: e.activation(rst16[:], rst16[:], AF.Exp,
                                        bias=ecol[:, 2:3], scale=-0.5),
       inc=sS, drain=True)
    t_rst16 = sS.tok

    gproj_pb = None
    bg_done = None
    for q4 in range(NG):
        for mc in range(NFC):
            wb, t_wb, hf = plan_wcast(wg_bf, mc, WPC["g"],
                                      waits_extra=[t_ofin, t_hin])

            def _mm(e, wb=wb, q4=q4):
                last = None
                for kc in range(NFC):
                    last = e.matmul(PB[3][:, 0:512],
                                    wb[:, kc * 128:(kc + 1) * 128],
                                    z[:, kc * T + q4 * G:kc * T + (q4 + 1) * G],
                                    start=(kc == 0), stop=(kc == NFC - 1))
                return last
            w3 = [t_wb]
            if gproj_pb is not None:
                w3.append((sV, gproj_pb))
            elif q4 == 0 and mc == 0:
                w3.append(F1["pb3_tok"])
            on("tensor", _mm, waits=w3, inc=sT)
            RG["wbf"][hf] = sT.n
            t_mm = sT.tok
            if q4 == 0:
                def _bmm2(e, wb=wb):
                    last = None
                    for kc in range(NFC):
                        last = e.matmul(PB[5][:, 0:1],
                                        wb[:, kc * 128:(kc + 1) * 128],
                                        ln1b_cb[:, kc:kc + 1],
                                        start=(kc == 0), stop=(kc == NFC - 1))
                    return last
                wb5 = [t_gcolsb]
                if PB5G["v"] is not None:
                    wb5.append((sV, PB5G["v"]))
                on("tensor", _bmm2, waits=wb5, inc=sT)
                RG["wbf"][hf] = sT.n
                on("vector", lambda e, mc=mc: e.tensor_copy(
                    bg_c[:, mc:mc + 1], PB[5][:, 0:1]), waits=[sT.tok], inc=sV)
                bg_done = sV.n
                PB5G["v"] = sV.n
            t_bg = (sV, bg_done) if bg_done else None
            wgq = [t_mm, t_bg]
            if q4 == 0 and mc == 0:
                wgq += [DT.get("hin"), DT.get("ofin")]
            on("vector", lambda e, mc=mc: e.tensor_scalar(
                gfm_q[:, mc * G:(mc + 1) * G], PB[3][:, 0:512],
                bg_c[:, mc:mc + 1], None, op0=ALU.add),
               waits=wgq, inc=sV)
            gproj_pb = sV.n
            f3_pump(2)
        t_gq = sV.tok

        for t4 in range(4):
            tt = q4 * 4 + t4
            xsl = xnO[:, tt * D:(tt + 1) * D]
            t_rr = t_rst16
            w = [t_gq]
            if RG["pbf"]:
                w.append((sV, RG["pbf"]))

            def _gtr(e, t4=t4):
                last = None
                for fc in range(NFC):
                    last = e.transpose(
                        PBF[:, fc * 128:(fc + 1) * 128],
                        gfm_q[:, fc * G + t4 * 128:fc * G + (t4 + 1) * 128],
                        ident[:])
                return last
            on("tensor", _gtr, waits=w, inc=sT)
            zb = ztB0 if tt % 2 == 0 else ztB1
            on("vector", lambda e, zb=zb: e.tensor_copy(
                zb[:].rearrange("p (f c) -> p f c", f=NFC),
                PBF[:].rearrange("p (f c) -> p f c", f=NFC)),
               waits=[sT.tok], inc=sV)
            RG["pbf"] = sV.n
            t_gtm = sV.tok
            for half in range(2):
                gsl_b = zb[:, half * 512:(half + 1) * 512]
                fs = gsc0 if half == 0 else gsc1
                wsg = [t_gtm]
                if tt == 0:
                    wsg.append(DT.get("sloc"))
                on("scalar", lambda e, gsl_b=gsl_b, fs=fs: e.activation(
                    fs[:], gsl_b, AF.Sigmoid, bias=ecol[:, 2:3], scale=1.0),
                   waits=wsg, inc=sS)
                on("vector", lambda e, fs=fs, gsl_b=gsl_b: e.tensor_tensor(
                    fs[:], fs[:], gsl_b, op=ALU.mult),
                   waits=[sS.tok], inc=sV, drain=True)
                on("vector", lambda e, fs=fs, half=half: e.tensor_tensor(
                    fs[:], fs[:], rmsrep[:, half * 512:(half + 1) * 512],
                    op=ALU.mult), waits=[t_rmsrep], inc=sV, drain=True)
                t_silu = sV.tok

                def _gate(e, fs=fs, half=half, xsl=xsl, zb=zb, tt=tt):
                    last = None
                    for hs in range(2):
                        hh = half * 2 + hs
                        last = e.scalar_tensor_tensor(
                            zb[:, hh * DV:(hh + 1) * DV],
                            xsl[:, hh * DV:(hh + 1) * DV],
                            rst16[:, tt * 4 + hh:tt * 4 + hh + 1],
                            fs[:, hs * DV:(hs + 1) * DV],
                            op0=ALU.mult, op1=ALU.mult)
                    return last
                on("vector", _gate, waits=[t_silu, t_rr], inc=sV, drain=True)
            t_gt = sV.tok
            w = [t_gt]
            if RG["pbf"]:
                w.append((sV, RG["pbf"]))

            def _gttr(e, zb=zb):
                last = None
                for fc in range(NFC):
                    last = e.transpose(PBF[:, fc * 128:(fc + 1) * 128],
                                       zb[:, fc * 128:(fc + 1) * 128],
                                       ident[:])
                return last
            on("tensor", _gttr, waits=w, inc=sT)
            gdst = gated[:].rearrange("p (f t) -> p f t",
                                      f=NFC)[:, :, tt * 128:(tt + 1) * 128]
            wgd = [sT.tok]
            if tt == 0:
                wgd += [DT.get("cs"), DT.get("kt"), DT.get("vt")]
            on("vector", lambda e, gdst=gdst: e.tensor_copy(
                gdst, PBF[:].rearrange("p (f c) -> p f c", f=NFC)),
               waits=wgd, inc=sV)
            RG["pbf"] = sV.n
            f3_pump(4)
    t_gated = sV.tok
    dump("gated", gated[:], [t_gated])
    t_rpk = plan_repack(t_wagd)

    wo_pb = [None] * 4
    for qr in range(4):
        for nt in range(2):
            for kc in range(NFC):
                hf = kc % 2
                w = [t_gated]
                wb = wbf0 if hf == 0 else wbf1
                w.append(WPC["wo"])
                if RG["wbf"][hf] is not None:
                    w.append((sT, RG["wbf"][hf]))
                t_wb = dma("sync", wb[:, 0:512],
                           wo_bf[kc * 128:(kc + 1) * 128, nt * 512:(nt + 1) * 512],
                           dWo0 if hf == 0 else dWo1, waits=w)
                for tc4 in range(4):
                    tc = qr * 4 + tc4

                    def _mm(e, kc=kc, tc=tc, tc4=tc4, wb=wb):
                        return e.matmul(
                            PB[tc4][:, 0:512],
                            gated[:, kc * T + tc * 128:kc * T + (tc + 1) * 128],
                            wb[:, 0:512], start=(kc == 0), stop=(kc == NFC - 1))
                    w3 = [t_wb]
                    if kc == 0 and wo_pb[tc4] is not None:
                        w3.append((sV, wo_pb[tc4]))
                    on("tensor", _mm, waits=w3, inc=sT)
                    RG["wbf"][hf] = sT.n
                    if kc == NFC - 1:
                        t_mm = sT.tok
                        xb = xldB0 if tc % 2 == 0 else xldB1
                        w4 = [t_gated, t_xnspill[0], t_xnspill[1]]
                        if RG["xld"][tc % 2] is not None:
                            w4.append((sV, RG["xld"][tc % 2]))
                        vx = dma("sync", xb[:], xn_sp[tc],
                                 dXn0 if tc % 2 == 0 else dXn1, waits=w4)
                        on("vector", lambda e, tc=tc, nt=nt, xb=xb, tc4=tc4:
                           e.tensor_tensor(
                               xnO[:, tc * D + nt * 512:tc * D + (nt + 1) * 512],
                               xb[:, nt * 512:(nt + 1) * 512],
                               PB[tc4][:, 0:512], op=ALU.add),
                           waits=[t_mm, vx, DT.get("ofin")], inc=sV,
                           drain=True)
                        wo_pb[tc4] = sV.n
                        RG["xld"][tc % 2] = sV.n
    t_xa1 = sV.tok
    dump("xa1", xnO[:], [t_xa1])

    y = z
    for tt in range(NTT):
        xsl = xnO[:, tt * D:(tt + 1) * D]
        on("vector", lambda e, xsl=xsl: e.bn_stats(stat6[:, 0:6], xsl[:, 0:512]),
           waits=[t_xa1, (sS, sS.n)], inc=sV, drain=True)
        on("vector", lambda e, xsl=xsl: e.bn_stats(stat6[:, 6:12],
                                                   xsl[:, 512:1024]), inc=sV)
        on("vector", lambda e: e.bn_aggr(stat2[:, 0:2], stat6[:, 0:12]),
           inc=sV, drain=True)
        on("scalar", lambda e: e.activation(dcol[:, 0:1], stat2[:, 1:2], AF.Ln,
                                            bias=ecol[:, 0:1], scale=1.0),
           waits=[sV.tok], inc=sS)
        on("scalar", lambda e: e.activation(dcol[:, 1:2], dcol[:, 0:1], AF.Exp,
                                            bias=ecol[:, 2:3], scale=-0.5),
           inc=sS, drain=True)
        on("vector", lambda e: e.scalar_tensor_tensor(
            dcol[:, 2:3], stat2[:, 0:1], -1.0, dcol[:, 1:2],
            op0=ALU.mult, op1=ALU.mult), waits=[sS.tok], inc=sV, drain=True)
        zb = ztC0 if tt % 2 == 0 else ztC1
        on("scalar", lambda e, zb=zb, xsl=xsl: e.activation(
            zb[:], xsl, AF.Identity, bias=dcol[:, 2:3], scale=dcol[:, 1:2]),
           waits=[sV.tok], inc=sS)
        w = [sS.tok]
        if RG["pbf"]:
            w.append((sV, RG["pbf"]))

        def _ytr(e, zb=zb):
            last = None
            for fc in range(NFC):
                last = e.transpose(PBF[:, fc * 128:(fc + 1) * 128],
                                   zb[:, fc * 128:(fc + 1) * 128], ident[:])
            return last
        on("tensor", _ytr, waits=w, inc=sT)
        ydst = y[:].rearrange("p (f t) -> p f t", f=NFC)[:, :, tt * 128:(tt + 1) * 128]
        wye = [sT.tok]
        if tt == 0:
            wye.append(DT.get("z"))
        on("vector", lambda e, ydst=ydst: e.tensor_copy(
            ydst, PBF[:].rearrange("p (f c) -> p f c", f=NFC)),
           waits=wye, inc=sV)
        RG["pbf"] = sV.n
    def _yscale(e):
        last = None
        for fc in range(NFC):
            last = e.tensor_scalar(y[:, fc * T:(fc + 1) * T],
                                   y[:, fc * T:(fc + 1) * T],
                                   ln2w_c[:, fc:fc + 1], None, op0=ALU.mult)
        return last
    on("vector", _yscale, waits=[t_gcols], inc=sV, drain=True)
    t_y = sV.tok
    dump("y", y[:], [t_y])
    t_b2row = dma("sync", b2_row, b2_h[None, :], dSm, waits=[t_rmsrep])

    b1w_done = None
    w2_pb = [None] * 4
    for half in range(2):
        h1_pb = [None, None]
        for mc in range(NFF):
            wb, t_wb, hf = plan_wcast(w1_bf, mc, WPC["w1"], waits_extra=[t_y])
            if half == 0:
                def _bmm(e, wb=wb):
                    last = None
                    for kc in range(NFC):
                        last = e.matmul(PB[5][:, 0:1],
                                        wb[:, kc * 128:(kc + 1) * 128],
                                        ln2b_cb[:, kc:kc + 1],
                                        start=(kc == 0), stop=(kc == NFC - 1))
                    return last
                w5 = [t_gcolsb, t_wb]
                if PB5G["v"] is not None:
                    w5.append((sV, PB5G["v"]))
                on("tensor", _bmm, waits=w5, inc=sT)
                RG["wbf"][hf] = sT.n
                on("vector", lambda e, mc=mc: e.scalar_tensor_tensor(
                    b1w_c[:, mc:mc + 1], PB[5][:, 0:1], 1.0,
                    b1_c[:, mc:mc + 1], op0=ALU.mult, op1=ALU.add),
                   waits=[sT.tok], inc=sV)
                b1w_done = sV.n
                PB5G["v"] = sV.n
            t_b1 = (sV, b1w_done) if b1w_done else None
            for nt in range(2):
                pbx = PB[nt]

                def _mm(e, wb=wb, half=half, nt=nt, pbx=pbx):
                    last = None
                    for kc in range(NFC):
                        last = e.matmul(
                            pbx[:, 0:512], wb[:, kc * 128:(kc + 1) * 128],
                            y[:, kc * T + half * 1024 + nt * 512:
                              kc * T + half * 1024 + (nt + 1) * 512],
                            start=(kc == 0), stop=(kc == NFC - 1))
                    return last
                w3 = [t_wb]
                if h1_pb[nt] is not None:
                    w3.append((sS, h1_pb[nt]))
                elif w2_pb[nt] is not None:
                    w3.append((sV, w2_pb[nt]))
                elif wo_pb[nt] is not None:
                    w3.append((sV, wo_pb[nt]))
                on("tensor", _mm, waits=w3, inc=sT)
                if nt == 1:
                    RG["wbf"][hf] = sT.n
                t_mm = sT.tok
                fs = fscW0 if nt == 0 else fscW1
                on("scalar", lambda e, fs=fs, pbx=pbx, mc=mc: e.activation(
                    fs[:], pbx[:, 0:512], AF.Identity, bias=b1w_c[:, mc:mc + 1],
                    scale=1.0), waits=[t_mm, t_b1], inc=sS)
                h1_pb[nt] = sS.n
                t_z1 = sS.tok
                on("vector", lambda e, fs=fs: e.scalar_tensor_tensor(
                    fs[:], fs[:], 0.01, fs[:], op0=ALU.mult, op1=ALU.max),
                   waits=[t_z1], inc=sV, drain=True)
                t_lr = sV.tok
                wh1 = [t_lr]
                if mc == 0 and nt == 0:
                    wh1.append(DT.get("gated"))
                    if half == 1:
                        wh1.append(DT.get("h1"))
                on("scalar", lambda e, fs=fs, mc=mc, nt=nt: e.activation(
                    h1h[:, mc * 1024 + nt * 512:mc * 1024 + (nt + 1) * 512],
                    fs[:], AF.Square, bias=ecol[:, 2:3], scale=1.0),
                   waits=wh1, inc=sS, drain=True)
        t_h1 = sS.tok
        if half == 1:
            t_w1done = (sT, sT.n)
        if "h1" in dbg:
            DT["h1"] = dma("sync", DOUT["h1"][half], h1h[:], dO, waits=[t_h1])

        for qr2 in range(2):
            for nt in range(2):
                for kc in range(NFF):
                    hf = kc % 2
                    w = [t_h1]
                    wb = wbf0 if hf == 0 else wbf1
                    w.append(WPC["w2"])
                    if RG["wbf"][hf] is not None:
                        w.append((sT, RG["wbf"][hf]))
                    t_wb = dma("sync", wb[:, 0:512],
                               w2_bf[kc * 128:(kc + 1) * 128, nt * 512:(nt + 1) * 512],
                               dWo0 if hf == 0 else dWo1, waits=w)
                    for tc4 in range(4):
                        t8 = qr2 * 4 + tc4

                        def _mm(e, kc=kc, t8=t8, tc4=tc4, wb=wb):
                            return e.matmul(
                                PB[tc4][:, 0:512],
                                h1h[:, kc * 1024 + t8 * 128:kc * 1024 + (t8 + 1) * 128],
                                wb[:, 0:512], start=(kc == 0), stop=False)
                        w3 = [t_wb]
                        if kc == 0 and w2_pb[tc4] is not None:
                            w3.append((sV, w2_pb[tc4]))
                        if kc == 0 and w2_pb[tc4] is None and wo_pb[tc4] is not None:
                            w3.append((sV, wo_pb[tc4]))
                        on("tensor", _mm, waits=w3, inc=sT)
                        RG["wbf"][hf] = sT.n
                        if kc == NFF - 1:
                            def _bm(e, tc4=tc4, nt=nt):
                                return e.matmul(PB[tc4][:, 0:512], onesr[:],
                                                b2_row[0:1, nt * 512:(nt + 1) * 512],
                                                start=False, stop=True)
                            on("tensor", _bm, waits=[t_b2row], inc=sT)
                            t_mm = sT.tok
                            tc = half * 8 + qr2 * 4 + tc4
                            on("vector", lambda e, tc=tc, nt=nt, tc4=tc4:
                               e.tensor_tensor(
                                   xnO[:, tc * D + nt * 512:tc * D + (nt + 1) * 512],
                                   xnO[:, tc * D + nt * 512:tc * D + (nt + 1) * 512],
                                   PB[tc4][:, 0:512], op=ALU.add),
                               waits=[t_mm, DT.get("xa1")], inc=sV, drain=True)
                            w2_pb[tc4] = sV.n
    t_xa2 = sV.tok
    dump("xa2", xnO[:], [t_xa2])

    f3_drain()
    t_s5 = F3["t_s5"]
    t_dn_prev = [None, None]
    dnres_t = []
    for kc in range(16):
        dnres_t.append(dma("sync", z[:, kc * 1024:(kc + 1) * 1024],
                           wdn_rp[kc], dDn0 if kc % 2 == 0 else dDn1,
                           waits=[t_w1done, t_rpk, DT.get("y")]))
    for kc in range(16, 20):
        dnres_t.append(dma("sync", dnr[kc - 16][:],
                           wdn_rp[kc], dDn0 if kc % 2 == 0 else dDn1,
                           waits=[t_w1done, t_rpk, t_s5]))
    dn_pb = [None] * 4
    for tg in range(NG):
        oq_buf = oq_tg if tg % 2 == 0 else oq_tgB
        oq_v = [None, None]
        for ic in range(NFF):
            hf = ic % 2
            ob = wbf0[:, 0:512] if hf == 0 else wbf1[:, 0:512]
            w = [t_s5, t_xa2 if tg == 0 else None, t_ofin]
            if tg == 0 and ic < 2:
                w += [DT.get("sloc"), DT.get("hin"),
                      (dF3s0, dF3s0.n), (dF3s1, dF3s1.n)]
            if oq_v[hf] is not None:
                w.append((sS, oq_v[hf]))
            else:
                w.append(DT.get("oq"))
            vld = dma("scalar", ob, sp_or[ic, :, tg * 512:(tg + 1) * 512],
                      dOq0 if hf == 0 else dOq1, waits=w)
            f0 = wstage[:, 0:512] if hf == 0 else wstage[:, 512:1024]
            on("scalar", lambda e, ob=ob, f0=f0: e.activation(
                f0, ob, AF.Copy, bias=MAGIC, scale=ccol[:, 11:12]),
               waits=[vld], inc=sS)
            wq_ = []
            if ic == 0:
                if t_dn_prev[tg % 2] is not None:
                    wq_.append(t_dn_prev[tg % 2])
                elif tg == 1:
                    wq_.append(t_xa2)
            on("scalar", lambda e, f0=f0, ic=ic, oq_buf=oq_buf: e.activation(
                oq_buf[:, ic * G:(ic + 1) * G], f0, AF.Identity,
                bias=ccol[:, 15:16], scale=ccol[:, 10:11]),
               waits=wq_, inc=sS, drain=True)
            oq_v[hf] = sS.n
        t_oq = sS.tok
        if "oq" in dbg:
            DT["oq"] = dma("sync", DOUT["oq"][tg], oq_buf[:], dO, waits=[t_oq])

        for nt in range(2):
            for kc in range(NFF):
                hf = kc % 2
                if kc < 16:
                    wb_ap = z[:, kc * 1024 + nt * 512:
                              kc * 1024 + (nt + 1) * 512]
                    t_wb = dnres_t[kc]
                elif kc < 20:
                    wb_ap = dnr[kc - 16][:, nt * 512:(nt + 1) * 512]
                    t_wb = dnres_t[kc]
                else:
                    wb = wbf2 if hf == 0 else wbf3
                    w = [t_rpk]
                    if tg == 0:
                        w.append(t_oq)
                    if RG["wbf"][hf] is not None:
                        w.append((sT, RG["wbf"][hf]))
                    t_wb = dma("sync", wb[:, 0:512],
                               wdn_rp[kc, :, nt * 512:(nt + 1) * 512],
                               dDn0 if hf == 0 else dDn1, waits=w)
                    wb_ap = wb[:, 0:512]
                for tc4 in range(4):
                    tc = tg * 4 + tc4

                    def _mm(e, kc=kc, tc4=tc4, wb_ap=wb_ap, oq_buf=oq_buf):
                        return e.matmul(
                            PB[tc4][:, 0:512],
                            oq_buf[:, kc * G + tc4 * 128:kc * G + (tc4 + 1) * 128],
                            wb_ap, start=(kc == 0), stop=(kc == NFF - 1))
                    w3 = [t_wb]
                    if kc == 0:
                        w3.append(t_oq)
                    if kc == 0 and dn_pb[tc4] is not None:
                        w3.append((sV, dn_pb[tc4]))
                    elif kc == 0:
                        w3.append(t_xa2)
                    on("tensor", _mm, waits=w3, inc=sT)
                    RG["wbf"][hf] = sT.n
                    if kc == NFF - 1:
                        t_mm = sT.tok
                        tc = tg * 4 + tc4
                        xb = xldC0 if tc % 2 == 0 else xldC1
                        w4 = [t_xa2]
                        if RG["xld"][tc % 2] is not None:
                            w4.append((sV, RG["xld"][tc % 2]))
                        vx = dma("sync", xb[:], x_h[tc * 128:(tc + 1) * 128, :],
                                 dXc0 if tc % 2 == 0 else dXc1, waits=w4)
                        on("vector", lambda e, tc=tc, nt=nt, tc4=tc4, xb=xb:
                           e.tensor_tensor(
                               xnO[:, tc * D + nt * 512:tc * D + (nt + 1) * 512],
                               xnO[:, tc * D + nt * 512:tc * D + (nt + 1) * 512],
                               PB[tc4][:, 0:512], op=ALU.add),
                           waits=[t_mm, t_xa2, DT.get("xa2")], inc=sV,
                           drain=True)
                        on("vector", lambda e, tc=tc, nt=nt, xb=xb:
                           e.tensor_tensor(
                               xnO[:, tc * D + nt * 512:tc * D + (nt + 1) * 512],
                               xnO[:, tc * D + nt * 512:tc * D + (nt + 1) * 512],
                               xb[:, nt * 512:(nt + 1) * 512], op=ALU.add),
                           waits=[vx], inc=sV, drain=True)
                        dn_pb[tc4] = sV.n
                        RG["xld"][tc % 2] = sV.n
        t_dn_prev[tg % 2] = sT.tok
        for tc4 in range(4):
            tc = tg * 4 + tc4
            dma("sync", out_h[tc * 128:(tc + 1) * 128, :],
                xnO[:, tc * D:(tc + 1) * D], dO, waits=[(sV, sV.n)])

    on("sync", lambda e: None, waits=[dO.tok], inc=None)
    on("gpsimd", lambda e: None, waits=[dO.tok], inc=None)
    on("scalar", lambda e: None, waits=[dO.tok], inc=None)

    with nc.Block() as block:
        @block.sync
        def _(e):
            for f in PL["sync"]:
                f(e)

        @block.vector
        def _(e):
            for f in PL["vector"]:
                f(e)

        @block.scalar
        def _(e):
            for f in PL["scalar"]:
                f(e)

        @block.tensor
        def _(e):
            for f in PL["tensor"]:
                f(e)

        @block.gpsimd
        def _(e):
            for f in PL["gpsimd"]:
                f(e)
    stack.close()
    return nc, dbg_names


_PROG_CACHE = {}


def get_program(dbg=()):
    key = tuple(sorted(dbg))
    if key not in _PROG_CACHE:
        _PROG_CACHE[key] = build_program(dbg)
    return _PROG_CACHE[key]


def kernel(L0_attn_x, L0_attn_ln1_w, L0_attn_ln1_b, L0_attn_Wq, L0_attn_Wk,
           L0_attn_Wv, L0_attn_Wg, L0_attn_Wgk1, L0_attn_Wgk2, L0_attn_bgk2,
           L0_attn_rms_w, L0_attn_Wo, L0_attn_ln2_w, L0_attn_ln2_b, L0_attn_W1,
           L0_attn_b1, L0_attn_W2, L0_attn_b2, L0_ffn_w_gate, L0_ffn_w_up,
           L0_ffn_w_down, _dbg=()):
    f32 = np.float32
    x = np.ascontiguousarray(np.asarray(L0_attn_x, f32).reshape(NCORES * T, D))
    full = {
        "Wq": np.ascontiguousarray(np.asarray(L0_attn_Wq, f32)),
        "Wk": np.ascontiguousarray(np.asarray(L0_attn_Wk, f32)),
        "Wv": np.ascontiguousarray(np.asarray(L0_attn_Wv, f32)),
        "Wg": np.ascontiguousarray(np.asarray(L0_attn_Wg, f32)),
        "Wgk1": np.ascontiguousarray(np.asarray(L0_attn_Wgk1, f32)),
        "Wgk2": np.ascontiguousarray(np.asarray(L0_attn_Wgk2, f32)),
        "bgk2": np.ascontiguousarray(np.asarray(L0_attn_bgk2, f32)),
        "ln1w": np.ascontiguousarray(np.asarray(L0_attn_ln1_w, f32)),
        "ln1b": np.ascontiguousarray(np.asarray(L0_attn_ln1_b, f32)),
        "rmsw": np.ascontiguousarray(np.asarray(L0_attn_rms_w, f32)),
        "Wo": np.ascontiguousarray(np.asarray(L0_attn_Wo, f32)),
        "ln2w": np.ascontiguousarray(np.asarray(L0_attn_ln2_w, f32)),
        "ln2b": np.ascontiguousarray(np.asarray(L0_attn_ln2_b, f32)),
        "W1": np.ascontiguousarray(np.asarray(L0_attn_W1, f32)),
        "b1": np.ascontiguousarray(np.asarray(L0_attn_b1, f32)),
        "W2": np.ascontiguousarray(np.asarray(L0_attn_W2, f32)),
        "b2": np.ascontiguousarray(np.asarray(L0_attn_b2, f32)),
    }
    w_up = np.asarray(L0_ffn_w_up, f32)
    w_gt = np.asarray(L0_ffn_w_gate, f32)
    w_dn = np.asarray(L0_ffn_w_down, f32)

    nc, dbg_names = get_program(tuple(_dbg))
    in_maps = []
    for c in range(NCORES):
        m = dict(full)
        m["x"] = np.ascontiguousarray(x[c * T:(c + 1) * T])
        m["wup_sl"] = np.ascontiguousarray(w_up[:, c * 128:(c + 1) * 128])
        m["wgt_sl"] = np.ascontiguousarray(w_gt[:, c * 128:(c + 1) * 128])
        m["wdn_sl"] = np.ascontiguousarray(w_dn[c * 128:(c + 1) * 128, :])
        cm = np.zeros((128, NCORES), f32)
        cm[:, :c] = 1.0
        m["cmask"] = cm
        in_maps.append(m)
    res = run_bass_kernel_spmd(nc, in_maps, core_ids=list(range(NCORES)))
    outs = [np.asarray(r["out"]) for r in res.results]
    full_out = np.concatenate(outs, axis=0).reshape(1, NCORES * T, D)
    if _dbg:
        dbg_res = [{k: np.asarray(r[k]) for k in dbg_names}
                   for r in res.results]
        return full_out, dbg_res
    return full_out
